# version 23 (final)
# speedup vs baseline: 1.0334x; 1.0035x over previous
import math
import numpy as np
import ml_dtypes
from contextlib import ExitStack

import concourse.bass as bass
import concourse.mybir as mybir
from concourse.bass_utils import run_bass_kernel_spmd

F32 = mybir.dt.float32
BF16 = mybir.dt.bfloat16
I32 = mybir.dt.int32
AF = mybir.ActivationFunctionType
ALU = mybir.AluOpType
AX = mybir.AxisListType
BF = ml_dtypes.bfloat16

D = 2048
DI = 4096
DXB = 512
DST = 16
DCONV = 4
DTR = 128
NB = 4
L = 2048
ZXROWS = 2 * DI + 2 * DXB + DTR
FI0 = 5632
FI1 = 8192
NH = 16
DH = 128
WIN = 512
SCALE = 1.0 / math.sqrt(DH)
NCORE = 8
TC = 1024
CSL = 512
EPS = 1e-5

OFF_Z = 0
OFF_X = DI
OFF_B = DI + DXB
OFF_C = DI + 2 * DXB
OFF_DT = 2 * DI + 2 * DXB

NT = TC // 128
NDC = D // 128
NJ_IN = ZXROWS // 128
NJ_DT = DI // 128
NJ_F = FI0 // 128


class Sem:
    def __init__(self, es, nc, name):
        self.h = es.enter_context(nc.semaphore(name))
        self.n = 0


def bf16(a):
    return np.ascontiguousarray(np.asarray(a).astype(BF))


def f32(a):
    return np.ascontiguousarray(np.asarray(a), dtype=np.float32)


def col_major_pc(vec, nchunks):
    v = np.ascontiguousarray(np.asarray(vec, dtype=np.float32)).reshape(nchunks, 128)
    return np.ascontiguousarray(v.T)


def tile_wT(wT, nj, nin=NDC, block=128):
    a = np.asarray(wT, dtype=np.float32)
    ni = a.shape[0] // 128
    a = a.reshape(ni, 128, nj, block).transpose(2, 1, 0, 3).reshape(nj, 128, ni * block)
    return np.ascontiguousarray(a)



def build_launch1():
    nc = bass.Bass("TRN2", target_bir_lowering=False)
    x_in = nc.dram_tensor("x", [TC, D], F32, kind="ExternalInput").ap()
    w_in_t = nc.dram_tensor("w_in_t", [NJ_IN, 128, NDC * 128], F32, kind="ExternalInput").ap()
    w_dtT = nc.dram_tensor("w_dtT", [DTR, DI], F32, kind="ExternalInput").ap()
    bdt_pc = nc.dram_tensor("bdt_pc", [128, NJ_DT], F32, kind="ExternalInput").ap()
    gw_t = nc.dram_tensor("gw_t", [NJ_F, 128, NDC * 128], F32, kind="ExternalInput").ap()
    uw_t = nc.dram_tensor("uw_t", [NJ_F, 128, NDC * 128], F32, kind="ExternalInput").ap()
    dw_t = nc.dram_tensor("dw_t", [2, 128, NJ_F * 1024], F32, kind="ExternalInput").ap()
    gs_pc = nc.dram_tensor("gs_pc", [128, NJ_F], F32, kind="ExternalInput").ap()
    us_pc = nc.dram_tensor("us_pc", [128, NJ_F], F32, kind="ExternalInput").ap()
    ds_row = nc.dram_tensor("ds_row", [1, D], F32, kind="ExternalInput").ap()
    ident_in = nc.dram_tensor("ident", [128, 128], BF16, kind="ExternalInput").ap()

    zx_out = nc.dram_tensor("zx", [ZXROWS, TC], BF16, kind="ExternalOutput").ap()
    delta_out = nc.dram_tensor("delta", [DI, TC], F32, kind="ExternalOutput").ap()
    ffn0_out = nc.dram_tensor("ffn0", [TC, D], F32, kind="ExternalOutput").ap()
    h_scr = nc.dram_tensor("h_scr", [128, NJ_F, TC], BF16, kind="Internal").ap()

    es = ExitStack()
    ident = es.enter_context(nc.sbuf_tensor("identt", [128, 128], BF16))
    xnT = [es.enter_context(nc.sbuf_tensor(f"xnT{j}", [128, TC], BF16)) for j in range(NDC)]
    wbuf = [es.enter_context(nc.sbuf_tensor(f"wbuf{k}", [128, NDC * 128], BF16)) for k in range(2)]
    stg = [es.enter_context(nc.sbuf_tensor(f"stg{k}", [128, TC], F32)) for k in range(2)]
    stgb = [es.enter_context(nc.sbuf_tensor(f"stgb{k}", [128, TC], BF16)) for k in range(2)]
    sgst = [es.enter_context(nc.sbuf_tensor(f"sgst{k}", [128, TC], F32)) for k in range(2)]
    dtl_sb = es.enter_context(nc.sbuf_tensor("dtl_sb", [128, TC], BF16))
    bdt1 = es.enter_context(nc.sbuf_tensor("bdt1", [128, NJ_DT], F32))
    bdt2 = es.enter_context(nc.sbuf_tensor("bdt2", [128, NJ_DT], F32))
    gscol = es.enter_context(nc.sbuf_tensor("gscol", [128, NJ_F], F32))
    uscol = es.enter_context(nc.sbuf_tensor("uscol", [128, NJ_F], F32))
    dsb = es.enter_context(nc.sbuf_tensor("dsb", [128, D], F32))
    dsrow_sb = es.enter_context(nc.sbuf_tensor("dsrow_sb", [1, D], BF16))
    ones1 = es.enter_context(nc.sbuf_tensor("ones1", [1, 128], BF16))
    mcol = es.enter_context(nc.sbuf_tensor("mcol", [128, NT], F32))
    rcol = es.enter_context(nc.sbuf_tensor("rcol", [128, NT], F32))
    scr1 = es.enter_context(nc.sbuf_tensor("scr1", [128, NT], F32))
    scr2 = es.enter_context(nc.sbuf_tensor("scr2", [128, NT], F32))
    konst = es.enter_context(nc.sbuf_tensor("konst", [128, 2], F32))


    s_c = Sem(es, nc, "s_c")
    s_cg = Sem(es, nc, "s_cg")
    s_x = [Sem(es, nc, f"s_x{k}") for k in range(2)]
    s_wq = [Sem(es, nc, f"s_wq{k}") for k in range(2)]
    s_hb = [Sem(es, nc, f"s_hb{k}") for k in range(2)]
    s_sb = [Sem(es, nc, f"s_sb{k}") for k in range(2)]
    s_sg = [Sem(es, nc, f"s_sg{k}") for k in range(2)]
    s_v = Sem(es, nc, "s_v")
    s_sc = Sem(es, nc, "s_sc")
    s_mm = Sem(es, nc, "s_mm")
    s_ms = Sem(es, nc, "s_ms")

    wq_marks = {}
    x_marks = {}
    hb_marks = {}

    with nc.sbuf_tensor("xt0", [128, D], F32) as xt0, \
         nc.sbuf_tensor("xt1", [128, D], F32) as xt1, \
         nc.sbuf_tensor("sq0", [128, D], F32) as sq0, \
         nc.sbuf_tensor("sq1", [128, D], F32) as sq1, \
         nc.sbuf_tensor("xnb0", [128, D], BF16) as xnb0, \
         nc.sbuf_tensor("xnb1", [128, D], BF16) as xnb1:
        xt = [xt0, xt1]
        sq = [sq0, sq1]
        xnb = [xnb0, xnb1]

        with nc.Block() as b:
            @b.sync
            def _(e):
                e.dma_start(ident[:], ident_in).then_inc(s_c.h, 16)
                e.dma_start(bdt1[:], bdt_pc).then_inc(s_c.h, 16)
                e.dma_start(gscol[:], gs_pc).then_inc(s_c.h, 16)
                e.dma_start(uscol[:], us_pc).then_inc(s_c.h, 16)
                s_c.n += 64
                for i in range(NT):
                    p = i % 2
                    if i >= 2:
                        e.wait_ge(s_v.h, 22 * (i - 1))
                        e.wait_ge(s_sc.h, 3 * (i - 1))
                    e.dma_start(xt[p][:], x_in[i * 128:(i + 1) * 128, :]).then_inc(s_x[p].h, 16)
                    s_x[p].n += 16
                    x_marks[i] = s_x[p].n

            @b.gpsimd
            def _(e):
                e.dma_start(dsrow_sb[:], ds_row).then_inc(s_cg.h, 16)
                e.memset(ones1[:], 1.0)
                e.memset(konst[:, 0:1], EPS)
                e.memset(konst[:, 1:2], -0.5).then_inc(s_ms.h, 1)
                s_ms.n += 1
                s_cg.n += 16
                e.dma_start(wbuf[0][:], w_in_t[0]).then_inc(s_wq[0].h, 16)
                s_wq[0].n += 16
                wq_marks[("in", 0)] = (0, s_wq[0].n)

            @b.scalar
            def _(e):
                e.wait_ge(s_ms.h, 1)
                for i in range(NT):
                    p = i % 2
                    e.wait_ge(s_x[p].h, x_marks[i])
                    e.square(sq[p][:], xt[p][:]).then_inc(s_sc.h, 1)
                    e.wait_ge(s_v.h, 22 * i + 5 + 1)
                    e.activation(scr1[:, i:i + 1], scr2[:, i:i + 1], AF.Ln, bias=konst[:, 0:1]).then_inc(s_sc.h, 1)
                    e.wait_ge(s_sc.h, s_sc.n + 2)
                    e.activation(rcol[:, i:i + 1], scr1[:, i:i + 1], AF.Exp, scale=konst[:, 1:2]).then_inc(s_sc.h, 1)
                    s_sc.n += 3

            @b.vector
            def _(e):
                e.wait_ge(s_c.h, 64)
                e.wait_ge(s_cg.h, 16)
                e.tensor_add(bdt2[:], bdt1[:], bdt1[:]).then_inc(s_v.h, 1)
                s_v.n += 1
                for i in range(NT):
                    p = i % 2
                    e.wait_ge(s_x[p].h, x_marks[i])
                    e.reduce_sum(scr1[:, i:i + 1], xt[p][:], axis=AX.X).then_inc(s_v.h, 1)
                    e.wait_ge(s_v.h, s_v.n + 1)
                    e.tensor_scalar_mul(mcol[:, i:i + 1], scr1[:, i:i + 1], 1.0 / D).then_inc(s_v.h, 1)
                    e.wait_ge(s_sc.h, 3 * i + 1)
                    e.wait_ge(s_v.h, s_v.n + 2)
                    e.reduce_sum(scr2[:, i:i + 1], sq[p][:], axis=AX.X).then_inc(s_v.h, 1)
                    e.wait_ge(s_v.h, s_v.n + 3)
                    e.tensor_scalar_mul(scr2[:, i:i + 1], scr2[:, i:i + 1], 1.0 / D)
                    e.tensor_mul(scr1[:, i:i + 1], mcol[:, i:i + 1], mcol[:, i:i + 1]).then_inc(s_v.h, 1)
                    e.wait_ge(s_v.h, s_v.n + 4)
                    e.tensor_sub(scr2[:, i:i + 1], scr2[:, i:i + 1], scr1[:, i:i + 1]).then_inc(s_v.h, 1)
                    e.wait_ge(s_sc.h, 3 * i + 3)
                    if i >= 2:
                        e.wait_ge(s_mm.h, 16 * (i - 1))
                    e.tensor_scalar(xnb[p][:], xt[p][:], mcol[:, i:i + 1], rcol[:, i:i + 1],
                                    ALU.subtract, ALU.mult).then_inc(s_v.h, 1)
                    s_v.n += 6
                    for j in range(NDC):
                        e.wait_ge(s_mm.h, 16 * i + j + 1)
                        e.tensor_copy(xnT[j][:, i * 128:(i + 1) * 128],
                                      ptp[(16 * i + j) % 2][:, 0:128]).then_inc(s_v.h, 1)
                        s_v.n += 1

            @b.tensor
            def _(e):
                e.wait_ge(s_c.h, 64)
                for i in range(NT):
                    e.wait_ge(s_v.h, 22 * i + 6 + 1)
                    for j in range(NDC):
                        k = 16 * i + j
                        if k >= 2:
                            i2, j2 = divmod(k - 2, 16)
                            e.wait_ge(s_v.h, 22 * i2 + 6 + j2 + 1 + 1)
                        e.transpose(ptp[k % 2][:, 0:128], xnb[i % 2][:, j * 128:(j + 1) * 128],
                                    ident[:]).then_inc(s_mm.h, 1)
                        s_mm.n += 1

    mm0 = s_mm.n
    v0 = s_v.n

    with nc.Block() as b:
        @b.gpsimd
        def _(e):
            for j in range(1, NJ_IN):
                p = j % 2
                if j >= 2:
                    e.wait_ge(s_mm.h, mm0 + 32 * (j - 1))
                e.dma_start(wbuf[p][:], w_in_t[j]).then_inc(s_wq[p].h, 16)
                s_wq[p].n += 16
                wq_marks[("in", j)] = (p, s_wq[p].n)

        @b.tensor
        def _(e):
            e.wait_ge(s_v.h, v0)
            for j in range(NJ_IN):
                p, mark = wq_marks[("in", j)]
                e.wait_ge(s_wq[p].h, mark)
                pa, pb = (pp[0], pp[1]) if j % 2 == 0 else (pp[2], pp[3])
                if j >= 2:
                    e.wait_ge(s_v.h, v0 + 2 * (j - 1))
                for i in range(NDC):
                    st_, sp_ = (i == 0), (i == NDC - 1)
                    e.matmul(pa[:], wbuf[p][:, i * 128:(i + 1) * 128], xnT[i][:, 0:512],
                             start=st_, stop=sp_)
                    mi = e.matmul(pb[:], wbuf[p][:, i * 128:(i + 1) * 128], xnT[i][:, 512:1024],
                                  start=st_, stop=sp_)
                    if sp_:
                        mi.then_inc(s_mm.h, 32)
                s_mm.n += 32

        @b.vector
        def _(e):
            for j in range(NJ_IN):
                p = j % 2
                pa, pb = (pp[0], pp[1]) if p == 0 else (pp[2], pp[3])
                e.wait_ge(s_mm.h, mm0 + 32 * (j + 1))
                if j >= 2:
                    e.wait_ge(s_sb[p].h, 16 * (j // 2))
                e.tensor_copy(stgb[p][:, 0:512], pa[:])
                e.tensor_copy(stgb[p][:, 512:1024], pb[:]).then_inc(s_v.h, 2)
                s_v.n += 2
                if j == NJ_IN - 1:
                    e.tensor_copy(dtl_sb[:, 0:512], pa[:])
                    e.tensor_copy(dtl_sb[:, 512:1024], pb[:]).then_inc(s_ms.h, 1)
                    s_ms.n += 1

        @b.sync
        def _(e):
            for j in range(NJ_IN):
                p = j % 2
                e.wait_ge(s_v.h, v0 + 2 * (j + 1))
                e.dma_start(zx_out[j * 128:(j + 1) * 128, :], stgb[p][:]).then_inc(s_sb[p].h, 16)
                s_sb[p].n += 16

    mm1 = s_mm.n
    v1 = s_v.n
    sc1 = s_sc.n
    sb1 = [s_sb[0].n, s_sb[1].n]

    with nc.Block() as b:
        @b.gpsimd
        def _(e):
            for jj in range(NJ_DT):
                p = jj % 2
                if jj >= 2:
                    e.wait_ge(s_mm.h, mm1 + 2 * (jj - 1))
                e.dma_start(wbuf[p][:, 0:128],
                            w_dtT[:, jj * 128:(jj + 1) * 128]).then_inc(s_wq[p].h, 16)
                s_wq[p].n += 16
                wq_marks[("dt", jj)] = (p, s_wq[p].n)

        @b.tensor
        def _(e):
            e.wait_ge(s_ms.h, 2)
            e.wait_ge(s_v.h, v1)
            for jj in range(NJ_DT):
                p, mark = wq_marks[("dt", jj)]
                e.wait_ge(s_wq[p].h, mark)
                pa, pb = (pp[0], pp[1]) if jj % 2 == 0 else (pp[2], pp[3])
                if jj >= 2:
                    e.wait_ge(s_sc.h, sc1 + 4 * (jj - 1))
                e.matmul(pa[:], wbuf[p][:, 0:128], dtl_sb[:, 0:512], start=True, stop=True)
                e.matmul(pb[:], wbuf[p][:, 0:128], dtl_sb[:, 512:1024],
                         start=True, stop=True).then_inc(s_mm.h, 2)
                s_mm.n += 2

        @b.scalar
        def _(e):
            for jj in range(NJ_DT):
                p = jj % 2
                pa, pb = (pp[0], pp[1]) if p == 0 else (pp[2], pp[3])
                e.wait_ge(s_mm.h, mm1 + 2 * (jj + 1))
                if jj >= 2:
                    e.wait_ge(s_v.h, v1 + 2 * (jj - 1))
                    e.wait_ge(s_sg[p].h, 16 * (jj // 2))
                e.add(stg[p][:, 0:512], pa[:], bdt2[:, jj:jj + 1])
                e.add(stg[p][:, 512:1024], pb[:], bdt2[:, jj:jj + 1]).then_inc(s_sc.h, 1)
                e.wait_ge(s_sc.h, s_sc.n + 1)
                e.activation(sgst[p][:], stg[p][:], AF.Abs).then_inc(s_sc.h, 1)
                e.wait_ge(s_sc.h, s_sc.n + 2)
                e.activation(sgst[p][:], sgst[p][:], AF.Exp, scale=-1.0).then_inc(s_sc.h, 1)
                e.wait_ge(s_sc.h, s_sc.n + 3)
                e.activation(sgst[p][:], sgst[p][:], AF.Ln, bias=1.0).then_inc(s_sc.h, 1)
                s_sc.n += 4

        @b.vector
        def _(e):
            for jj in range(NJ_DT):
                p = jj % 2
                e.wait_ge(s_sc.h, sc1 + 4 * (jj + 1))
                e.tensor_scalar_max(stg[p][:], stg[p][:], 0.0).then_inc(s_v.h, 1)
                e.wait_ge(s_v.h, s_v.n + 1)
                e.tensor_add(stg[p][:], stg[p][:], sgst[p][:]).then_inc(s_v.h, 1)
                s_v.n += 2

        @b.sync
        def _(e):
            for jj in range(NJ_DT):
                p = jj % 2
                e.wait_ge(s_v.h, v1 + 2 * (jj + 1))
                e.dma_start(delta_out[jj * 128:(jj + 1) * 128, :], stg[p][:]).then_inc(s_sg[p].h, 16)
                s_sg[p].n += 16

    mm2 = s_mm.n
    v2 = s_v.n
    sc2 = s_sc.n
    sb2 = [s_sb[0].n, s_sb[1].n]

    with nc.Block() as b:
        @b.gpsimd
        def _(e):
            for j in range(NJ_F):
                if j >= 1:
                    e.wait_ge(s_mm.h, mm2 + 64 * j)
                e.dma_start(wbuf[0][:], gw_t[j]).then_inc(s_wq[0].h, 16)
                s_wq[0].n += 16
                e.dma_start(wbuf[1][:], uw_t[j]).then_inc(s_wq[1].h, 16)
                s_wq[1].n += 16
                wq_marks[("f", j)] = (s_wq[0].n, s_wq[1].n)

        @b.tensor
        def _(e):
            e.wait_ge(s_sc.h, sc2)
            for j in range(NJ_F):
                m0, m1 = wq_marks[("f", j)]
                e.wait_ge(s_wq[0].h, m0)
                e.wait_ge(s_wq[1].h, m1)
                if j >= 1:
                    e.wait_ge(s_v.h, v2 + 2 * j)
                for i in range(NDC):
                    st_, sp_ = (i == 0), (i == NDC - 1)
                    e.matmul(pp[0][:], wbuf[0][:, i * 128:(i + 1) * 128], xnT[i][:, 0:512], start=st_, stop=sp_)
                    e.matmul(pp[1][:], wbuf[0][:, i * 128:(i + 1) * 128], xnT[i][:, 512:1024], start=st_, stop=sp_)
                    e.matmul(pp[2][:], wbuf[1][:, i * 128:(i + 1) * 128], xnT[i][:, 0:512], start=st_, stop=sp_)
                    mi = e.matmul(pp[3][:], wbuf[1][:, i * 128:(i + 1) * 128], xnT[i][:, 512:1024],
                                  start=st_, stop=sp_)
                    if sp_:
                        mi.then_inc(s_mm.h, 64)
                s_mm.n += 64

        @b.scalar
        def _(e):
            for j in range(NJ_F):
                p = j % 2
                e.wait_ge(s_mm.h, mm2 + 64 * (j + 1))
                if j >= 2:
                    e.wait_ge(s_v.h, v2 + 2 * (j - 1))
                e.activation(sgst[p][:, 0:512], pp[0][:], AF.Silu, scale=gscol[:, j:j + 1])
                e.activation(sgst[p][:, 512:1024], pp[1][:], AF.Silu, scale=gscol[:, j:j + 1]).then_inc(s_sc.h, 2)
                s_sc.n += 2

        @b.vector
        def _(e):
            for j in range(NJ_F):
                p = j % 2
                e.wait_ge(s_sc.h, sc2 + 2 * (j + 1))
                e.wait_ge(s_sb[p].h, sb2[p] + 16 * (j // 2))
                e.scalar_tensor_tensor(stgb[p][:, 0:512], pp[2][:], uscol[:, j:j + 1],
                                       sgst[p][:, 0:512], ALU.mult, ALU.mult)
                e.scalar_tensor_tensor(stgb[p][:, 512:1024], pp[3][:], uscol[:, j:j + 1],
                                       sgst[p][:, 512:1024], ALU.mult, ALU.mult).then_inc(s_v.h, 2)
                s_v.n += 2

        @b.sync
        def _(e):
            for j in range(NJ_F):
                p = j % 2
                e.wait_ge(s_v.h, v2 + 2 * (j + 1))
                e.dma_start(h_scr[:, j, :], stgb[p][:]).then_inc(s_sb[p].h, 16)
                s_sb[p].n += 16

    mm3 = s_mm.n
    v3 = s_v.n
    sb3 = [s_sb[0].n, s_sb[1].n]
    sg3 = [s_sg[0].n, s_sg[1].n]

    with nc.Block() as b:
        @b.tensor
        def _(e):
            e.wait_ge(s_ms.h, 1)
            e.wait_ge(s_cg.h, 16)
            e.wait_ge(s_v.h, v3)
            for q in range(4):
                mi = e.matmul(pp[q][:], ones1[:], dsrow_sb[:, q * 512:(q + 1) * 512], start=True, stop=True)
            mi.then_inc(s_mm.h, 1)
            s_mm.n += 1

        @b.vector
        def _(e):
            e.wait_ge(s_mm.h, mm3 + 1)
            for q in range(4):
                ci = e.tensor_copy(dsb[:, q * 512:(q + 1) * 512], pp[q][:])
            ci.then_inc(s_ms.h, 1)
            s_ms.n += 1

    mm4 = s_mm.n
    ms4 = s_ms.n

    HBS = 2
    with nc.sbuf_tensor("dwh", [128, NJ_F * 1024], BF16) as dwh, \
         nc.sbuf_tensor("hb0", [128, NJ_F * 128], BF16) as hb0, \
         nc.sbuf_tensor("hb1", [128, NJ_F * 128], BF16) as hb1:
        hb = [hb0, hb1]
        s_dw = Sem(es, nc, "s_dw")
        dw_marks = {}
        with nc.Block() as b:
            @b.gpsimd
            def _(e):
                for dh_ in range(2):
                    if dh_ == 1:
                        e.wait_ge(s_mm.h, mm4 + 88 * NT)
                    e.dma_start(dwh[:], dw_t[dh_]).then_inc(s_dw.h, 16)
                    s_dw.n += 16
                    dw_marks[dh_] = s_dw.n
                    for t in range(NT):
                        it_ = dh_ * NT + t
                        p = t % 2
                        if dh_ == 0 and t == 0:
                            e.wait_ge(s_sb[0].h, sb3[0])
                            e.wait_ge(s_sb[1].h, sb3[1])
                        if it_ >= 2:
                            e.wait_ge(s_mm.h, mm4 + 88 * (it_ - 1))
                        half = NJ_F // HBS
                        for q in range(HBS):
                            e.dma_start(
                                hb[p][:, q * half * 128:(q + 1) * half * 128]
                                .rearrange("p (i o) -> p i o", i=half),
                                h_scr[:, q * half:(q + 1) * half, t * 128:(t + 1) * 128],
                            ).then_inc(s_hb[p].h, 16)
                            s_hb[p].n += 16
                        hb_marks[it_] = (p, s_hb[p].n)

            @b.tensor
            def _(e):
                e.wait_ge(s_ms.h, ms4)
                for dh_ in range(2):
                    for t in range(NT):
                        it_ = dh_ * NT + t
                        p, mark = hb_marks[it_]
                        e.wait_ge(s_dw.h, dw_marks[dh_])
                        e.wait_ge(s_hb[p].h, mark)
                        pa, pb = (pp[0], pp[1]) if t % 2 == 0 else (pp[2], pp[3])
                        if it_ >= 2:
                            e.wait_ge(s_v.h, v3 + 2 * (it_ - 1))
                        for j in range(NJ_F):
                            st_, sp_ = (j == 0), (j == NJ_F - 1)
                            e.matmul(pa[:], hb[p][:, j * 128:(j + 1) * 128],
                                     dwh[:, j * 1024:j * 1024 + 512], start=st_, stop=sp_)
                            mi = e.matmul(pb[:], hb[p][:, j * 128:(j + 1) * 128],
                                          dwh[:, j * 1024 + 512:(j + 1) * 1024], start=st_, stop=sp_)
                            if sp_:
                                mi.then_inc(s_mm.h, 88)
                        s_mm.n += 88

            @b.vector
            def _(e):
                e.wait_ge(s_ms.h, ms4)
                for dh_ in range(2):
                    for t in range(NT):
                        it_ = dh_ * NT + t
                        p = it_ % 2
                        pa, pb = (pp[0], pp[1]) if t % 2 == 0 else (pp[2], pp[3])
                        e.wait_ge(s_mm.h, mm4 + 88 * (it_ + 1))
                        e.wait_ge(s_sg[p].h, sg3[p] + 16 * (it_ // 2))
                        e.tensor_tensor(stg[p][:, 0:512], pa[:], dsb[:, dh_ * 1024:dh_ * 1024 + 512], ALU.mult)
                        e.tensor_tensor(stg[p][:, 512:1024], pb[:], dsb[:, dh_ * 1024 + 512:(dh_ + 1) * 1024],
                                        ALU.mult).then_inc(s_v.h, 2)
                        s_v.n += 2

            @b.sync
            def _(e):
                for dh_ in range(2):
                    for t in range(NT):
                        it_ = dh_ * NT + t
                        p = it_ % 2
                        e.wait_ge(s_v.h, v3 + 2 * (it_ + 1))
                        e.dma_start(ffn0_out[t * 128:(t + 1) * 128, dh_ * 1024:(dh_ + 1) * 1024],
                                    stg[p][:]).then_inc(s_sg[p].h, 16)
                        s_sg[p].n += 16
                e.wait_ge(s_sg[0].h, s_sg[0].n)
                e.wait_ge(s_sg[1].h, s_sg[1].n)
                e.wait_ge(s_sb[0].h, s_sb[0].n)
                e.wait_ge(s_sb[1].h, s_sb[1].n)

    es.close()
    return nc


def host_inputs_l1(inputs):
    shared = dict(
        w_in_t=tile_wT(np.asarray(inputs["L0_attn_W_in"]).T, NJ_IN),
        w_dtT=f32(np.asarray(inputs["L0_attn_W_dt"]).T),
        bdt_pc=col_major_pc(inputs["L0_attn_b_dt"], NJ_DT),
        gw_t=tile_wT(np.asarray(inputs["L0_ffn_gate_w"]).T, NJ_F),
        uw_t=tile_wT(np.asarray(inputs["L0_ffn_up_w"]).T, NJ_F),
        dw_t=np.ascontiguousarray(
            f32(np.asarray(inputs["L0_ffn_down_w"]).T).reshape(NJ_F, 128, 2, 1024)
            .transpose(2, 1, 0, 3).reshape(2, 128, NJ_F * 1024)),
        gs_pc=col_major_pc(inputs["L0_ffn_gate_s"], NJ_F),
        us_pc=col_major_pc(inputs["L0_ffn_up_s"], NJ_F),
        ds_row=f32(np.asarray(inputs["L0_ffn_down_s"]).reshape(1, D)),
        ident=np.eye(128, dtype=BF),
    )
    x = np.asarray(inputs["L0_attn_hidden_states"])
    xt = x.reshape(NB, 2, TC, D)
    per_core = []
    for c in range(NCORE):
        m = dict(shared)
        m["x"] = f32(xt[c // 2, c % 2])
        per_core.append(m)
    return per_core



NCH = CSL // 128
NSEQ = NB
NTile = NCH * NSEQ


def build_launch2():
    nc = bass.Bass("TRN2", target_bir_lowering=False)
    xsrc_in = nc.dram_tensor("xsrc", [64, NB * L], BF16, kind="ExternalInput").ap()
    bm_in = nc.dram_tensor("bm", [64, NB * L], BF16, kind="ExternalInput").ap()
    cm_in = nc.dram_tensor("cm", [CSL, NB * L], BF16, kind="ExternalInput").ap()
    z_in = nc.dram_tensor("z", [CSL, NB * L], BF16, kind="ExternalInput").ap()
    dl_in = nc.dram_tensor("dl", [CSL, NB * L], F32, kind="ExternalInput").ap()
    alog_pc = nc.dram_tensor("alog_pc", [128, NCH * DST], F32, kind="ExternalInput").ap()
    cw_pc = nc.dram_tensor("cw_pc", [128, NCH * DCONV], F32, kind="ExternalInput").ap()
    cb_pc = nc.dram_tensor("cb_pc", [128, NCH], F32, kind="ExternalInput").ap()
    dp_pc = nc.dram_tensor("dp_pc", [128, NCH], F32, kind="ExternalInput").ap()
    eexp_in = nc.dram_tensor("eexp", [NCH, 64, 128], BF16, kind="ExternalInput").ap()
    ebc_in = nc.dram_tensor("ebc", [64, 64, 128], BF16, kind="ExternalInput").ap()
    scs_in = nc.dram_tensor("scs", [DST, 128, 128], BF16, kind="ExternalInput").ap()
    ident_in = nc.dram_tensor("ident", [128, 128], BF16, kind="ExternalInput").ap()

    y2_out = nc.dram_tensor("y2", [CSL, NB * L], BF16, kind="ExternalOutput").ap()

    es = ExitStack()
    xsrc = es.enter_context(nc.sbuf_tensor("xsrc_s", [64, NB * L], BF16))
    bm = es.enter_context(nc.sbuf_tensor("bm_s", [64, NB * L], BF16))
    apc = es.enter_context(nc.sbuf_tensor("apc", [128, NCH * DST], F32))
    atmp = es.enter_context(nc.sbuf_tensor("atmp", [128, NCH * DST], F32))
    cw = es.enter_context(nc.sbuf_tensor("cw", [128, NCH * DCONV], F32))
    cb = es.enter_context(nc.sbuf_tensor("cb", [128, NCH], F32))
    dp = es.enter_context(nc.sbuf_tensor("dp", [128, NCH], F32))
    eexp = es.enter_context(nc.sbuf_tensor("eexp_s", [64, NCH * 128], BF16))
    ebc = es.enter_context(nc.sbuf_tensor("ebc_s", [64, 64 * 128], BF16))
    scs = es.enter_context(nc.sbuf_tensor("scs_s", [128, DST * 128], BF16))
    identb = es.enter_context(nc.sbuf_tensor("identb", [128, 128], BF16))
    xe = es.enter_context(nc.sbuf_tensor("xe", [128, L + 3], F32))
    u_sb = es.enter_context(nc.sbuf_tensor("u_sb", [128, L], F32))
    du = es.enter_context(nc.sbuf_tensor("du", [128, L], F32))
    dls = [es.enter_context(nc.sbuf_tensor(f"dls{k}", [128, L], F32)) for k in range(2)]
    cms = [es.enter_context(nc.sbuf_tensor(f"cms{k}", [128, L], BF16)) for k in range(2)]
    zs = [es.enter_context(nc.sbuf_tensor(f"zs{k}", [128, L], BF16)) for k in range(2)]
    da = [es.enter_context(nc.sbuf_tensor(f"da{k}", [128, L], F32)) for k in range(2)]
    db = [es.enter_context(nc.sbuf_tensor(f"db{k}", [128, L], F32)) for k in range(2)]
    scn = [es.enter_context(nc.sbuf_tensor(f"scn{k}", [128, L], F32)) for k in range(2)]
    mb = [es.enter_context(nc.sbuf_tensor(f"mb{k}", [128, L], BF16)) for k in range(2)]
    szb = es.enter_context(nc.sbuf_tensor("szb", [128, L], F32))
    t1b = es.enter_context(nc.sbuf_tensor("t1b", [128, L], F32))
    y2st = [es.enter_context(nc.sbuf_tensor(f"y2st{k}", [128, L], BF16)) for k in range(2)]

    psA = es.enter_context(nc.psum_tensor("psA", [128, L], F32))
    py = es.enter_context(nc.psum_tensor("py", [128, L], F32))

    s_cst = Sem(es, nc, "s_cst")
    s_dl = [Sem(es, nc, f"s_dl{k}") for k in range(2)]
    s_cz = [Sem(es, nc, f"s_cz{k}") for k in range(2)]
    s_y2 = [Sem(es, nc, f"s_y2{k}") for k in range(2)]
    s_v = Sem(es, nc, "s_v")
    s_sc = Sem(es, nc, "s_sc")
    s_mm = Sem(es, nc, "s_mm")
    s_pad = Sem(es, nc, "s_pad")
    s_ax1 = Sem(es, nc, "s_ax1")
    s_ax2 = Sem(es, nc, "s_ax2")

    dl_marks = {}
    cz_marks = {}

    VT = 56
    MT = 49
    ST = 18

    with nc.Block() as b:
        @b.sync
        def _(e):
            e.dma_start(xsrc[:], xsrc_in).then_inc(s_cst.h, 16)
            e.dma_start(bm[:], bm_in).then_inc(s_cst.h, 16)
            e.dma_start(atmp[:], alog_pc).then_inc(s_cst.h, 16)
            e.dma_start(cw[:], cw_pc).then_inc(s_cst.h, 16)
            e.dma_start(cb[:], cb_pc).then_inc(s_cst.h, 16)
            e.dma_start(dp[:], dp_pc).then_inc(s_cst.h, 16)
            e.dma_start(eexp[:].rearrange("p (j o) -> p j o", j=NCH), eexp_in.rearrange("j p o -> p j o")).then_inc(s_cst.h, 16)
            e.dma_start(ebc[:].rearrange("p (j o) -> p j o", j=64), ebc_in.rearrange("j p o -> p j o")).then_inc(s_cst.h, 16)
            e.dma_start(scs[:].rearrange("p (j o) -> p j o", j=DST), scs_in.rearrange("j p o -> p j o")).then_inc(s_cst.h, 16)
            e.dma_start(identb[:], ident_in).then_inc(s_cst.h, 16)
            s_cst.n += 160
            for k in range(NTile):
                jj, bq = divmod(k, NSEQ)
                p = k % 2
                if k >= 2:
                    e.wait_ge(s_sc.h, ST * (k - 1))
                    e.wait_ge(s_mm.h, MT * (k - 1))
                e.dma_start(dls[p][:], dl_in[jj * 128:(jj + 1) * 128, bq * L:(bq + 1) * L]).then_inc(s_dl[p].h, 16)
                s_dl[p].n += 16
                dl_marks[k] = s_dl[p].n
                e.dma_start(cms[p][:], cm_in[jj * 128:(jj + 1) * 128, bq * L:(bq + 1) * L]).then_inc(s_cz[p].h, 16)
                e.dma_start(zs[p][:], z_in[jj * 128:(jj + 1) * 128, bq * L:(bq + 1) * L]).then_inc(s_cz[p].h, 16)
                s_cz[p].n += 32
                cz_marks[k] = s_cz[p].n

        @b.gpsimd
        def _(e):
            e.memset(xe[:, 0:3], 0.0).then_inc(s_pad.h, 1)
            s_pad.n += 1
            for k in range(NTile):
                jj, bq = divmod(k, NSEQ)
                p = k % 2
                e.wait_ge(s_v.h, VT * k + 56)
                e.dma_start(y2_out[jj * 128:(jj + 1) * 128, bq * L:(bq + 1) * L],
                            y2st[p][:]).then_inc(s_y2[p].h, 16)
                s_y2[p].n += 16
            e.wait_ge(s_y2[0].h, s_y2[0].n)
            e.wait_ge(s_y2[1].h, s_y2[1].n)

        @b.scalar
        def _(e):
            e.wait_ge(s_cst.h, 160)
            e.activation(apc[:], atmp[:], AF.Exp).then_inc(s_ax1.h, 1)
            s_ax1.n += 1
            for k in range(NTile):
                jj = k // NSEQ
                S0, V0 = ST * k, VT * k
                e.wait_ge(s_v.h, V0 + 5)
                e.activation(u_sb[:], u_sb[:], AF.Silu, bias=cb[:, jj:jj + 1]).then_inc(s_sc.h, 1)
                s_sc.n += 1
                e.wait_ge(s_ax2.h, 1)
                for n in range(DST):
                    if n >= 2:
                        e.wait_ge(s_v.h, V0 + 8 + 3 * (n - 2))
                    e.activation(da[n % 2][:], dls[k % 2][:], AF.Exp,
                                 scale=apc[:, jj * DST + n:jj * DST + n + 1]).then_inc(s_sc.h, 1)
                    s_sc.n += 1
                e.wait_ge(s_cz[k % 2].h, cz_marks[k])
                e.activation(szb[:], zs[k % 2][:], AF.Silu).then_inc(s_sc.h, 1)
                s_sc.n += 1

        @b.vector
        def _(e):
            e.wait_ge(s_ax1.h, 1)
            e.tensor_scalar_mul(apc[:], apc[:], -1.0).then_inc(s_ax2.h, 1)
            s_ax2.n += 1
            for k in range(NTile):
                jj = k // NSEQ
                p = k % 2
                S0, V0, M0 = ST * k, VT * k, MT * k
                e.wait_ge(s_mm.h, M0 + 1)
                for q in range(4):
                    ci = e.tensor_copy(xe[:, 3 + q * 512:3 + (q + 1) * 512], psA[:, q * 512:(q + 1) * 512])
                ci.then_inc(s_v.h, 1)
                e.wait_ge(s_v.h, V0 + 1)
                e.wait_ge(s_pad.h, 1)
                e.tensor_scalar(u_sb[:], xe[:, 3:3 + L], cw[:, jj * 4 + 3:jj * 4 + 4], None,
                                ALU.mult).then_inc(s_v.h, 1)
                for q in range(3):
                    e.wait_ge(s_v.h, V0 + 2 + q)
                    e.scalar_tensor_tensor(u_sb[:], xe[:, q:q + L], cw[:, jj * 4 + q:jj * 4 + q + 1],
                                           u_sb[:], ALU.mult, ALU.add).then_inc(s_v.h, 1)
                s_v.n += 5
                e.wait_ge(s_sc.h, S0 + 1)
                e.wait_ge(s_dl[p].h, dl_marks[k])
                e.tensor_tensor(du[:], dls[p][:], u_sb[:], ALU.mult).then_inc(s_v.h, 1)
                s_v.n += 1
                for n in range(DST):
                    e.wait_ge(s_mm.h, M0 + 2 + 3 * n)
                    if n == 0:
                        e.wait_ge(s_v.h, V0 + 6)
                    e.tensor_tensor(db[n % 2][:], psA[:], du[:], ALU.mult).then_inc(s_v.h, 1)
                    e.wait_ge(s_v.h, V0 + 7 + 3 * n)
                    e.wait_ge(s_sc.h, S0 + 2 + n)
                    e.tensor_tensor_scan(scn[n % 2][:], da[n % 2][:], db[n % 2][:], 0.0,
                                         ALU.mult, ALU.add).then_inc(s_v.h, 1)
                    e.wait_ge(s_v.h, V0 + 8 + 3 * n)
                    e.wait_ge(s_mm.h, M0 + 3 + 3 * n)
                    e.tensor_tensor(mb[n % 2][:], psA[:], scn[n % 2][:], ALU.mult).then_inc(s_v.h, 1)
                    s_v.n += 3
                e.wait_ge(s_mm.h, M0 + 4 + 3 * 15)
                e.scalar_tensor_tensor(t1b[:], u_sb[:], dp[:, jj:jj + 1], py[:], ALU.mult, ALU.add).then_inc(s_v.h, 1)
                e.wait_ge(s_v.h, V0 + 55)
                e.wait_ge(s_sc.h, S0 + 18)
                e.wait_ge(s_y2[p].h, 16 * (k // 2))
                e.tensor_tensor(y2st[p][:], t1b[:], szb[:], ALU.mult).then_inc(s_v.h, 1)
                s_v.n += 2

        @b.tensor
        def _(e):
            e.wait_ge(s_cst.h, 160)
            for k in range(NTile):
                jj, bq = divmod(k, NSEQ)
                S0, V0, M0 = ST * k, VT * k, MT * k
                if k >= 1:
                    e.wait_ge(s_v.h, V0)
                for q in range(4):
                    mi = e.matmul(psA[:, q * 512:(q + 1) * 512], eexp[:, jj * 128:(jj + 1) * 128],
                                  xsrc[:, bq * L + q * 512:bq * L + (q + 1) * 512], start=True, stop=True)
                mi.then_inc(s_mm.h, 1)
                s_mm.n += 1
                for n in range(DST):
                    if n == 0:
                        e.wait_ge(s_v.h, V0 + 1)
                    else:
                        e.wait_ge(s_v.h, V0 + 9 + 3 * (n - 1))
                    idx = jj * DST + n
                    for q in range(4):
                        mi = e.matmul(psA[:, q * 512:(q + 1) * 512], ebc[:, idx * 128:(idx + 1) * 128],
                                      bm[:, bq * L + q * 512:bq * L + (q + 1) * 512], start=True, stop=True)
                    mi.then_inc(s_mm.h, 1)
                    e.wait_ge(s_v.h, V0 + 7 + 3 * n)
                    e.wait_ge(s_cz[k % 2].h, cz_marks[k])
                    for q in range(4):
                        mi = e.matmul(psA[:, q * 512:(q + 1) * 512], scs[:, n * 128:(n + 1) * 128],
                                      cms[k % 2][:, q * 512:(q + 1) * 512], start=True, stop=True)
                    mi.then_inc(s_mm.h, 1)
                    e.wait_ge(s_v.h, V0 + 9 + 3 * n)
                    if n == 0 and k >= 1:
                        e.wait_ge(s_v.h, V0 + 1 - VT + 54)
                    for q in range(4):
                        mi = e.matmul(py[:, q * 512:(q + 1) * 512], identb[:],
                                      mb[n % 2][:, q * 512:(q + 1) * 512],
                                      start=(n == 0), stop=(n == DST - 1))
                    mi.then_inc(s_mm.h, 1)
                    s_mm.n += 3

    es.close()
    return nc


def host_inputs_l2(inputs, zx_all, delta_all):
    x_orig = zx_all[OFF_X:OFF_X + DXB]
    bm_g = zx_all[OFF_B:OFF_B + DXB]
    cm_g = zx_all[OFF_C:OFF_C + DI]
    z_g = zx_all[OFF_Z:DI]
    alog = np.asarray(inputs["L0_attn_A_log"], np.float32)
    cwf = np.asarray(inputs["L0_attn_conv_w"], np.float32).reshape(DI, DCONV)
    cbf = np.asarray(inputs["L0_attn_conv_b"], np.float32)
    dpf = np.asarray(inputs["L0_attn_D_param"], np.float32)

    eexp = np.zeros((NCH, 64, 128), dtype=BF)
    for j in range(NCH):
        for p_ in range(128):
            eexp[j, 16 * j + p_ % 16, p_] = 1
    ebc = np.zeros((64, 64, 128), dtype=BF)
    for i in range(64):
        ebc[i, i, :] = 1
    scs = np.zeros((DST, 128, 128), dtype=BF)
    for n in range(DST):
        for p_ in range(128):
            scs[n, 16 * (p_ // 16) + n, p_] = 1
    ident = np.eye(128, dtype=BF)

    per_core = []
    for c in range(NCORE):
        cs = slice(CSL * c, CSL * (c + 1))
        m = dict(
            xsrc=np.ascontiguousarray(x_orig[64 * c:64 * (c + 1)]),
            bm=np.ascontiguousarray(bm_g[64 * c:64 * (c + 1)]),
            cm=np.ascontiguousarray(cm_g[cs]),
            z=np.ascontiguousarray(z_g[cs]),
            dl=np.ascontiguousarray(delta_all[cs]),
            alog_pc=np.ascontiguousarray(alog[cs].reshape(NCH, 128, DST).transpose(1, 0, 2).reshape(128, NCH * DST)),
            cw_pc=np.ascontiguousarray(cwf[cs].reshape(NCH, 128, DCONV).transpose(1, 0, 2).reshape(128, NCH * DCONV)),
            cb_pc=col_major_pc(cbf[cs], NCH),
            dp_pc=col_major_pc(dpf[cs], NCH),
            eexp=eexp, ebc=ebc, scs=scs, ident=ident,
        )
        per_core.append(m)
    return per_core



def build_launch3():
    nc = bass.Bass("TRN2", target_bir_lowering=False)
    x_in = nc.dram_tensor("x", [TC, D], F32, kind="ExternalInput").ap()
    ffn0_in = nc.dram_tensor("ffn0", [TC, D], F32, kind="ExternalInput").ap()
    y2t_in = nc.dram_tensor("y2t", [128, (DI // 128) * TC], BF16, kind="ExternalInput").ap()
    wo_t = nc.dram_tensor("wo_t", [2, 128, (DI // 128) * 1024], F32, kind="ExternalInput").ap()
    wqk_t = nc.dram_tensor("wqk_t", [2 * NDC, 128, NDC * 128], F32, kind="ExternalInput").ap()
    wv_t = nc.dram_tensor("wv_t", [128, NDC * D], F32, kind="ExternalInput").ap()
    ident_in = nc.dram_tensor("ident", [128, 128], BF16, kind="ExternalInput").ap()

    x1_out = nc.dram_tensor("x1", [TC, D], F32, kind="ExternalOutput").ap()
    xn1_out = nc.dram_tensor("xn1", [D, TC], BF16, kind="ExternalOutput").ap()
    q_out = nc.dram_tensor("qf", [D, TC], BF16, kind="ExternalOutput").ap()
    k_out = nc.dram_tensor("kf", [D, TC], BF16, kind="ExternalOutput").ap()
    v_out = nc.dram_tensor("vt", [TC, D], BF16, kind="ExternalOutput").ap()

    NIY = DI // 128

    es = ExitStack()
    wbuf = [es.enter_context(nc.sbuf_tensor(f"wbuf{k}", [128, NDC * 128], BF16)) for k in range(2)]
    xts = [es.enter_context(nc.sbuf_tensor(f"xts{k}", [128, D], F32)) for k in range(2)]
    fts = [es.enter_context(nc.sbuf_tensor(f"fts{k}", [128, D], F32)) for k in range(2)]
    stg = [es.enter_context(nc.sbuf_tensor(f"stg{k}", [128, TC], F32)) for k in range(2)]
    stgb = [es.enter_context(nc.sbuf_tensor(f"stgb{k}", [128, TC], BF16)) for k in range(2)]
    x1ts = [es.enter_context(nc.sbuf_tensor(f"x1ts{k}", [128, D], F32)) for k in range(2)]
    ident = es.enter_context(nc.sbuf_tensor("identt", [128, 128], BF16))
    mcol = es.enter_context(nc.sbuf_tensor("mcol", [128, NT], F32))
    rcol = es.enter_context(nc.sbuf_tensor("rcol", [128, NT], F32))
    scr1 = es.enter_context(nc.sbuf_tensor("scr1", [128, NT], F32))
    scr2 = es.enter_context(nc.sbuf_tensor("scr2", [128, NT], F32))
    konst = es.enter_context(nc.sbuf_tensor("konst", [128, 2], F32))


    s_c = Sem(es, nc, "s_c")
    s_xl = [Sem(es, nc, f"s_xl{k}") for k in range(2)]
    s_wq = [Sem(es, nc, f"s_wq{k}") for k in range(2)]
    s_wb = Sem(es, nc, "s_wb")
    s_x1s = [Sem(es, nc, f"s_x1s{k}") for k in range(2)]
    s_x1l = [Sem(es, nc, f"s_x1l{k}") for k in range(2)]
    s_so = [Sem(es, nc, f"s_so{k}") for k in range(2)]
    s_sv = [Sem(es, nc, f"s_sv{k}") for k in range(2)]
    s_v = Sem(es, nc, "s_v")
    s_sc = Sem(es, nc, "s_sc")
    s_mm = Sem(es, nc, "s_mm")
    s_ms = Sem(es, nc, "s_ms")

    xl_marks = {}
    wq_marks = {}
    wb_marks = {}
    x1l_marks = {}

    es1 = ExitStack()
    y2s = es1.enter_context(nc.sbuf_tensor("y2s", [128, NIY * TC], BF16))
    woh = es1.enter_context(nc.sbuf_tensor("woh", [128, NIY * 1024], BF16))
    x1s_marks = {}
    with nc.Block() as b:
        @b.sync
        def _(e):
            e.dma_start(ident[:], ident_in).then_inc(s_c.h, 16)
            e.dma_start(y2s[:], y2t_in).then_inc(s_c.h, 16)
            s_c.n += 32
            for dh_ in range(2):
                for t in range(NT):
                    it_ = dh_ * NT + t
                    p = it_ % 2
                    if it_ >= 2:
                        e.wait_ge(s_v.h, 2 * (it_ - 1))
                    e.dma_start(xts[p][:, 0:1024],
                                x_in[t * 128:(t + 1) * 128, dh_ * 1024:(dh_ + 1) * 1024]).then_inc(s_xl[p].h, 16)
                    e.dma_start(fts[p][:, 0:1024],
                                ffn0_in[t * 128:(t + 1) * 128, dh_ * 1024:(dh_ + 1) * 1024]).then_inc(s_xl[p].h, 16)
                    s_xl[p].n += 32
                    xl_marks[it_] = s_xl[p].n

        @b.gpsimd
        def _(e):
            for dh_ in range(2):
                if dh_ == 1:
                    e.wait_ge(s_mm.h, NT)
                e.dma_start(woh[:], wo_t[dh_]).then_inc(s_wb.h, 16)
                s_wb.n += 16
                wb_marks[("wo", dh_)] = s_wb.n
                for t in range(NT):
                    it_ = dh_ * NT + t
                    p = it_ % 2
                    e.wait_ge(s_v.h, 2 * (it_ + 1))
                    e.dma_start(x1_out[t * 128:(t + 1) * 128, dh_ * 1024:(dh_ + 1) * 1024],
                                stg[p][:]).then_inc(s_x1s[p].h, 16)
                    s_x1s[p].n += 16
                    x1s_marks[it_] = s_x1s[p].n

        @b.tensor
        def _(e):
            e.wait_ge(s_c.h, 32)
            for dh_ in range(2):
                e.wait_ge(s_wb.h, wb_marks[("wo", dh_)])
                for t in range(NT):
                    it_ = dh_ * NT + t
                    pa, pb = (pp[0], pp[1]) if it_ % 2 == 0 else (pp[2], pp[3])
                    if it_ >= 2:
                        e.wait_ge(s_v.h, 2 * (it_ - 1))
                    for i in range(NIY):
                        st_, sp_ = (i == 0), (i == NIY - 1)
                        e.matmul(pa[:], y2s[:, i * TC + t * 128:i * TC + (t + 1) * 128],
                                 woh[:, i * 1024:i * 1024 + 512], start=st_, stop=sp_)
                        mi = e.matmul(pb[:], y2s[:, i * TC + t * 128:i * TC + (t + 1) * 128],
                                      woh[:, i * 1024 + 512:(i + 1) * 1024], start=st_, stop=sp_)
                        if sp_:
                            mi.then_inc(s_mm.h, 1)
                    s_mm.n += 1

        @b.vector
        def _(e):
            for dh_ in range(2):
                for t in range(NT):
                    it_ = dh_ * NT + t
                    p = it_ % 2
                    pa, pb = (pp[0], pp[1]) if p == 0 else (pp[2], pp[3])
                    e.wait_ge(s_xl[p].h, xl_marks[it_])
                    e.wait_ge(s_mm.h, it_ + 1)
                    if it_ >= 2:
                        e.wait_ge(s_x1s[p].h, 16 * (it_ // 2))
                    e.tensor_tensor(stg[p][:, 0:512], xts[p][:, 0:512], pa[:], ALU.add)
                    e.tensor_tensor(stg[p][:, 512:1024], xts[p][:, 512:1024], pb[:], ALU.add).then_inc(s_v.h, 1)
                    e.wait_ge(s_v.h, 2 * it_ + 1)
                    e.tensor_tensor(stg[p][:], stg[p][:], fts[p][:, 0:1024], ALU.add).then_inc(s_v.h, 1)
                    s_v.n += 2

    es1.close()
    v1 = s_v.n
    mm1 = s_mm.n

    es2 = ExitStack()
    wvs = es2.enter_context(nc.sbuf_tensor("wvs", [128, NDC * D], BF16))
    xnT = [es2.enter_context(nc.sbuf_tensor(f"xnT{j}", [128, TC], BF16)) for j in range(NDC)]
    sq = [es2.enter_context(nc.sbuf_tensor(f"sq{k}", [128, D], F32)) for k in range(2)]
    xnb = [es2.enter_context(nc.sbuf_tensor(f"xnb{k}", [128, D], BF16)) for k in range(2)]
    with nc.Block() as b:
        @b.sync
        def _(e):
            for i in range(NT):
                p = i % 2
                e.wait_ge(s_x1s[i % 2].h, x1s_marks[i])
                e.wait_ge(s_x1s[(NT + i) % 2].h, x1s_marks[NT + i])
                if i < 2:
                    e.wait_ge(s_v.h, v1)
                if i >= 2:
                    e.wait_ge(s_v.h, v1 + 22 * (i - 1))
                    e.wait_ge(s_sc.h, 3 * (i - 1))
                e.dma_start(xts[p][:], x1_out[i * 128:(i + 1) * 128, :]).then_inc(s_x1l[p].h, 16)
                s_x1l[p].n += 16
                x1l_marks[i] = s_x1l[p].n

        @b.gpsimd
        def _(e):
            e.memset(konst[:, 0:1], EPS)
            e.memset(konst[:, 1:2], -0.5).then_inc(s_ms.h, 1)
            s_ms.n += 1
            e.dma_start(wbuf[0][:], wqk_t[0]).then_inc(s_wq[0].h, 16)
            s_wq[0].n += 16
            wq_marks[("qk", 0)] = (0, s_wq[0].n)
            e.dma_start(wvs[:], wv_t).then_inc(s_wb.h, 16)
            s_wb.n += 16
            wb_marks["wv"] = s_wb.n

        @b.scalar
        def _(e):
            e.wait_ge(s_ms.h, 1)
            for i in range(NT):
                p = i % 2
                e.wait_ge(s_x1l[p].h, x1l_marks[i])
                e.square(sq[p][:], xts[p][:]).then_inc(s_sc.h, 1)
                e.wait_ge(s_v.h, v1 + 22 * i + 5)
                e.activation(scr1[:, i:i + 1], scr2[:, i:i + 1], AF.Ln, bias=konst[:, 0:1]).then_inc(s_sc.h, 1)
                e.wait_ge(s_sc.h, 3 * i + 2)
                e.activation(rcol[:, i:i + 1], scr1[:, i:i + 1], AF.Exp, scale=konst[:, 1:2]).then_inc(s_sc.h, 1)
                s_sc.n += 3

        @b.vector
        def _(e):
            for i in range(NT):
                p = i % 2
                e.wait_ge(s_x1l[p].h, x1l_marks[i])
                e.reduce_sum(scr1[:, i:i + 1], xts[p][:], axis=AX.X).then_inc(s_v.h, 1)
                e.wait_ge(s_v.h, v1 + 22 * i + 1)
                e.tensor_scalar_mul(mcol[:, i:i + 1], scr1[:, i:i + 1], 1.0 / D).then_inc(s_v.h, 1)
                e.wait_ge(s_sc.h, 3 * i + 1)
                e.wait_ge(s_v.h, v1 + 22 * i + 2)
                e.reduce_sum(scr2[:, i:i + 1], sq[p][:], axis=AX.X).then_inc(s_v.h, 1)
                e.wait_ge(s_v.h, v1 + 22 * i + 3)
                e.tensor_scalar_mul(scr2[:, i:i + 1], scr2[:, i:i + 1], 1.0 / D)
                e.tensor_mul(scr1[:, i:i + 1], mcol[:, i:i + 1], mcol[:, i:i + 1]).then_inc(s_v.h, 1)
                e.wait_ge(s_v.h, v1 + 22 * i + 4)
                e.tensor_sub(scr2[:, i:i + 1], scr2[:, i:i + 1], scr1[:, i:i + 1]).then_inc(s_v.h, 1)
                e.wait_ge(s_sc.h, 3 * i + 3)
                if i >= 2:
                    e.wait_ge(s_mm.h, mm1 + 16 * (i - 1))
                e.tensor_scalar(xnb[p][:], xts[p][:], mcol[:, i:i + 1], rcol[:, i:i + 1],
                                ALU.subtract, ALU.mult).then_inc(s_v.h, 1)
                s_v.n += 6
                for j in range(NDC):
                    e.wait_ge(s_mm.h, mm1 + 16 * i + j + 1)
                    e.tensor_copy(xnT[j][:, i * 128:(i + 1) * 128],
                                  ptp[(16 * i + j) % 2][:, 0:128]).then_inc(s_v.h, 1)
                    s_v.n += 1

        @b.tensor
        def _(e):
            for i in range(NT):
                e.wait_ge(s_v.h, v1 + 22 * i + 6)
                for j in range(NDC):
                    k = 16 * i + j
                    if k >= 2:
                        i2, j2 = divmod(k - 2, 16)
                        e.wait_ge(s_v.h, v1 + 22 * i2 + 6 + j2 + 1)
                    e.transpose(ptp[k % 2][:, 0:128], xnb[i % 2][:, j * 128:(j + 1) * 128],
                                ident[:]).then_inc(s_mm.h, 1)
                    s_mm.n += 1

    v2 = s_v.n
    mm2 = s_mm.n
    sc2 = s_sc.n

    NQK = 2 * NDC
    with nc.Block() as b:
        @b.gpsimd
        def _(e):
            for j in range(1, NQK):
                p = j % 2
                if j >= 2:
                    e.wait_ge(s_mm.h, mm2 + 32 * (j - 1))
                e.dma_start(wbuf[p][:], wqk_t[j]).then_inc(s_wq[p].h, 16)
                s_wq[p].n += 16
                wq_marks[("qk", j)] = (p, s_wq[p].n)

        @b.tensor
        def _(e):
            e.wait_ge(s_v.h, v2)
            for j in range(NQK):
                p, mark = wq_marks[("qk", j)]
                e.wait_ge(s_wq[p].h, mark)
                pa, pb = (pp[0], pp[1]) if j % 2 == 0 else (pp[2], pp[3])
                if j >= 2:
                    e.wait_ge(s_v.h, v2 + 2 * (j - 1))
                for i in range(NDC):
                    st_, sp_ = (i == 0), (i == NDC - 1)
                    e.matmul(pa[:], wbuf[p][:, i * 128:(i + 1) * 128], xnT[i][:, 0:512],
                             start=st_, stop=sp_)
                    mi = e.matmul(pb[:], wbuf[p][:, i * 128:(i + 1) * 128], xnT[i][:, 512:1024],
                                  start=st_, stop=sp_)
                    if sp_:
                        mi.then_inc(s_mm.h, 32)
                s_mm.n += 32
            e.wait_ge(s_wb.h, wb_marks["wv"])
            for t in range(NT):
                it_ = NQK + t
                pa, pb = (pp[0], pp[1]) if t % 2 == 0 else (pp[2], pp[3])
                e.wait_ge(s_v.h, v2 + 2 * (it_ - 1))
                for i in range(NDC):
                    st_, sp_ = (i == 0), (i == NDC - 1)
                    e.matmul(pa[:], xnT[i][:, t * 128:(t + 1) * 128],
                             wvs[:, i * D:i * D + 512], start=st_, stop=sp_)
                    mi = e.matmul(pb[:], xnT[i][:, t * 128:(t + 1) * 128],
                                  wvs[:, i * D + 512:i * D + 1024], start=st_, stop=sp_)
                    if sp_:
                        mi.then_inc(s_mm.h, 32)
                s_mm.n += 32
            for t in range(NT):
                it_ = NQK + NT + t
                pa, pb = (pp[0], pp[1]) if t % 2 == 0 else (pp[2], pp[3])
                e.wait_ge(s_v.h, v2 + 2 * (it_ - 1))
                for i in range(NDC):
                    st_, sp_ = (i == 0), (i == NDC - 1)
                    e.matmul(pa[:], xnT[i][:, t * 128:(t + 1) * 128],
                             wvs[:, i * D + 1024:i * D + 1536], start=st_, stop=sp_)
                    mi = e.matmul(pb[:], xnT[i][:, t * 128:(t + 1) * 128],
                                  wvs[:, i * D + 1536:(i + 1) * D], start=st_, stop=sp_)
                    if sp_:
                        mi.then_inc(s_mm.h, 32)
                s_mm.n += 32

        @b.vector
        def _(e):
            NTOT = NQK + 2 * NT
            for k in range(NTOT):
                p = k % 2
                pa, pb = (pp[0], pp[1]) if p == 0 else (pp[2], pp[3])
                e.wait_ge(s_mm.h, mm2 + 32 * (k + 1))
                e.wait_ge(s_so[p].h, 16 * (k // 2))
                if k < NDC:
                    e.tensor_scalar_mul(stgb[p][:, 0:512], pa[:], SCALE)
                    e.tensor_scalar_mul(stgb[p][:, 512:1024], pb[:], SCALE).then_inc(s_v.h, 2)
                else:
                    e.tensor_copy(stgb[p][:, 0:512], pa[:])
                    e.tensor_copy(stgb[p][:, 512:1024], pb[:]).then_inc(s_v.h, 2)
                s_v.n += 2

        @b.scalar
        def _(e):
            NTOT = NQK + 2 * NT
            for k in range(NTOT):
                p = k % 2
                e.wait_ge(s_v.h, v2 + 2 * (k + 1))
                if k < NDC:
                    dst = q_out[k * 128:(k + 1) * 128, :]
                elif k < NQK:
                    dst = k_out[(k - NDC) * 128:(k - NDC + 1) * 128, :]
                elif k < NQK + NT:
                    t = k - NQK
                    dst = v_out[t * 128:(t + 1) * 128, 0:1024]
                else:
                    t = k - NQK - NT
                    dst = v_out[t * 128:(t + 1) * 128, 1024:2048]
                e.dma_start(dst, stgb[p][:]).then_inc(s_so[p].h, 16)
                s_so[p].n += 16
            for j in range(NDC):
                pq = j % 2
                e.wait_ge(s_v.h, v2)
                e.dma_start(xn1_out[j * 128:(j + 1) * 128, :], xnT[j][:]).then_inc(s_sv[pq].h, 16)
                s_sv[pq].n += 16
            e.wait_ge(s_so[0].h, s_so[0].n)
            e.wait_ge(s_so[1].h, s_so[1].n)
            e.wait_ge(s_sv[0].h, s_sv[0].n)
            e.wait_ge(s_sv[1].h, s_sv[1].n)
            e.wait_ge(s_x1s[0].h, s_x1s[0].n)
            e.wait_ge(s_x1s[1].h, s_x1s[1].n)

    es2.close()
    es.close()
    return nc


def host_inputs_l3(inputs, x_full, ffn0_all, y2_all):
    woT = f32(np.asarray(inputs["L0_attn_W_out"]).T)
    wo_tiled = np.ascontiguousarray(
        woT.reshape(32, 128, 2, 1024).transpose(2, 1, 0, 3).reshape(2, 128, 32 * 1024))
    wqkv = np.asarray(inputs["L1_attn_w_qkv"])
    wqkT = f32(wqkv[0:2 * D].T)
    wqk_tiled = tile_wT(wqkT, 2 * NDC)
    wvT = f32(wqkv[2 * D:3 * D].T)
    wv_tiled = np.ascontiguousarray(
        wvT.reshape(NDC, 128, D).transpose(1, 0, 2).reshape(128, NDC * D))
    ident = np.eye(128, dtype=BF)
    per_core = []
    for c in range(NCORE):
        ts = slice(TC * c, TC * (c + 1))
        y2c = np.ascontiguousarray(y2_all[:, ts])
        y2tile = np.ascontiguousarray(
            y2c.reshape(32, 128, TC).transpose(1, 0, 2).reshape(128, 32 * TC))
        m = dict(
            x=np.ascontiguousarray(x_full[ts]),
            ffn0=np.ascontiguousarray(ffn0_all[ts]),
            y2t=y2tile,
            wo_t=wo_tiled, wqk_t=wqk_tiled, wv_t=wv_tiled, ident=ident,
        )
        per_core.append(m)
    return per_core



NJ1 = FI1 // 128
NKV = 12


def tile_wT_i32(wT, nj):
    a = np.ascontiguousarray(np.asarray(wT, dtype=np.int32))
    ni = a.shape[0] // 128
    a = a.reshape(ni, 128, nj, 128).transpose(2, 1, 0, 3).reshape(nj, 128, ni * 128)
    return np.ascontiguousarray(a)


def build_launch4():
    nc = bass.Bass("TRN2", target_bir_lowering=False)
    qt_in = nc.dram_tensor("qt", [128, NH * TC], BF16, kind="ExternalInput").ap()
    kxt_in = nc.dram_tensor("kxt", [128, NH * 1536], BF16, kind="ExternalInput").ap()
    vxt_in = nc.dram_tensor("vxt", [128, NKV * NH * 128], BF16, kind="ExternalInput").ap()
    msk_in = nc.dram_tensor("msk", [128, 8 * 640], BF16, kind="ExternalInput").ap()
    x1_in = nc.dram_tensor("x1", [TC, D], F32, kind="ExternalInput").ap()
    xn1t_in = nc.dram_tensor("xn1t", [128, NDC * TC], BF16, kind="ExternalInput").ap()
    g1_t = nc.dram_tensor("g1_t", [NJ1, 128, NDC * 128], I32, kind="ExternalInput").ap()
    u1_t = nc.dram_tensor("u1_t", [NJ1, 128, NDC * 128], I32, kind="ExternalInput").ap()
    d1_t = nc.dram_tensor("d1_t", [8, 128, NJ1 * 256], I32, kind="ExternalInput").ap()
    gs1_pc = nc.dram_tensor("gs1_pc", [128, NJ1], F32, kind="ExternalInput").ap()
    us1_pc = nc.dram_tensor("us1_pc", [128, NJ1], F32, kind="ExternalInput").ap()
    ds1_row = nc.dram_tensor("ds1_row", [1, D], F32, kind="ExternalInput").ap()
    wo1_t = nc.dram_tensor("wo1_t", [2, 128, NDC * 1024], F32, kind="ExternalInput").ap()
    ident_in = nc.dram_tensor("ident", [128, 128], BF16, kind="ExternalInput").ap()

    out_f = nc.dram_tensor("out", [TC, D], F32, kind="ExternalOutput").ap()
    ffn1_scr = nc.dram_tensor("ffn1_scr", [TC, D], F32, kind="Internal").ap()

    es = ExitStack()
    stgb = [es.enter_context(nc.sbuf_tensor(f"stgb{k}", [128, TC], BF16)) for k in range(2)]
    sgst = [es.enter_context(nc.sbuf_tensor(f"sgst{k}", [128, TC], BF16)) for k in range(2)]
    onescol = es.enter_context(nc.sbuf_tensor("onescol", [128, 1], BF16))
    negones = es.enter_context(nc.sbuf_tensor("negones", [1, 128], BF16))
    ones1r = es.enter_context(nc.sbuf_tensor("ones1r", [1, 128], BF16))
    gscol = es.enter_context(nc.sbuf_tensor("gscol", [128, NJ1], F32))
    uscol = es.enter_context(nc.sbuf_tensor("uscol", [128, NJ1], F32))
    ds1row_sb = es.enter_context(nc.sbuf_tensor("ds1row_sb", [1, D], BF16))
    Rb = es.enter_context(nc.sbuf_tensor("Rb", [1, TC], BF16))
    identt = es.enter_context(nc.sbuf_tensor("identt", [128, 128], BF16))
    R2c = es.enter_context(nc.sbuf_tensor("R2c", [128, NT], F32))
    mR2c = es.enter_context(nc.sbuf_tensor("mR2c", [128, NT], F32))

    s_c = Sem(es, nc, "s_c")
    s_cg = Sem(es, nc, "s_cg")
    s_wq = [Sem(es, nc, f"s_wq{k}") for k in range(2)]
    s_uq = [Sem(es, nc, f"s_uq{k}") for k in range(2)]
    s_wb = Sem(es, nc, "s_wb")
    s_fs = [Sem(es, nc, f"s_fs{k}") for k in range(2)]
    s_xl = [Sem(es, nc, f"s_xl{k}") for k in range(2)]
    s_os = [Sem(es, nc, f"s_os{k}") for k in range(2)]
    s_v = Sem(es, nc, "s_v")
    s_sc = Sem(es, nc, "s_sc")
    s_mm = Sem(es, nc, "s_mm")
    s_ms = Sem(es, nc, "s_ms")

    wq_marks = {}
    uq_marks = {}
    wb_marks = {}
    xl_marks = {}
    fs_marks = {}

    esA = ExitStack()
    h_all = [esA.enter_context(nc.sbuf_tensor(f"h{j}", [128, TC], BF16)) for j in range(NJ1)]
    ppf = [esA.enter_context(nc.psum_tensor(f"ppf{k}", [128, 512], F32)) for k in range(4)]
    pR = esA.enter_context(nc.psum_tensor("pR", [1, TC], F32))
    pRc = esA.enter_context(nc.psum_tensor("pRc", [128, 1], F32))

    esB = ExitStack()
    xn1T = esB.enter_context(nc.sbuf_tensor("xn1T", [128, NDC * TC], BF16))
    wbg = [esB.enter_context(nc.sbuf_tensor(f"wbg{k}", [128, NDC * 128], BF16)) for k in range(2)]
    wbu = [esB.enter_context(nc.sbuf_tensor(f"wbu{k}", [128, NDC * 128], BF16)) for k in range(2)]

    with nc.Block() as b:
        @b.sync
        def _(e):
            e.dma_start(xn1T[:], xn1t_in).then_inc(s_c.h, 16)
            e.dma_start(gscol[:], gs1_pc).then_inc(s_c.h, 16)
            e.dma_start(uscol[:], us1_pc).then_inc(s_c.h, 16)
            e.dma_start(identt[:], ident_in).then_inc(s_c.h, 16)
            s_c.n += 64

        @b.gpsimd
        def _(e):
            e.memset(onescol[:], 1.0)
            e.memset(negones[:], -128.0)
            e.memset(ones1r[:], 1.0)
            e.dma_start(ds1row_sb[:], ds1_row).then_inc(s_cg.h, 16)
            s_cg.n += 16
            for j in range(NJ1):
                pj = j % 2
                if j >= 2:
                    e.wait_ge(s_mm.h, 1 + (j - 1))
                e.dma_start(wbg[pj][:], g1_t[j]).then_inc(s_wq[pj].h, 16)
                s_wq[pj].n += 16
                wq_marks[j] = s_wq[pj].n
                e.dma_start(wbu[pj][:], u1_t[j]).then_inc(s_uq[pj].h, 16)
                s_uq[pj].n += 16
                uq_marks[j] = s_uq[pj].n

        @b.tensor
        def _(e):
            e.wait_ge(s_c.h, 64)
            e.wait_ge(s_cg.h, 16)
            for i in range(NDC):
                st_, sp_ = (i == 0), (i == NDC - 1)
                e.matmul(pR[:, 0:512], onescol[:], xn1T[:, i * TC:i * TC + 512], start=st_, stop=sp_)
                mi = e.matmul(pR[:, 512:1024], onescol[:], xn1T[:, i * TC + 512:(i + 1) * TC], start=st_, stop=sp_)
                if sp_:
                    mi.then_inc(s_mm.h, 1)
            s_mm.n += 1
            for j in range(NJ1):
                pj = j % 2
                e.wait_ge(s_wq[pj].h, wq_marks[j])
                e.wait_ge(s_uq[pj].h, uq_marks[j])
                e.wait_ge(s_ms.h, 1)
                if j >= 1:
                    e.wait_ge(s_sc.h, 2 * j)
                    e.wait_ge(s_v.h, 2 * j + 1)
                for i in range(NDC):
                    st_ = (i == 0)
                    e.matmul(ppf[0][:], wbg[pj][:, i * 128:(i + 1) * 128], xn1T[:, i * TC:i * TC + 512],
                             start=st_, stop=False)
                    e.matmul(ppf[1][:], wbg[pj][:, i * 128:(i + 1) * 128], xn1T[:, i * TC + 512:(i + 1) * TC],
                             start=st_, stop=False)
                    e.matmul(ppf[2][:], wbu[pj][:, i * 128:(i + 1) * 128], xn1T[:, i * TC:i * TC + 512],
                             start=st_, stop=False)
                    e.matmul(ppf[3][:], wbu[pj][:, i * 128:(i + 1) * 128], xn1T[:, i * TC + 512:(i + 1) * TC],
                             start=st_, stop=False)
                e.matmul(ppf[0][:], negones[:], Rb[:, 0:512], start=False, stop=True)
                e.matmul(ppf[1][:], negones[:], Rb[:, 512:1024], start=False, stop=True)
                e.matmul(ppf[2][:], negones[:], Rb[:, 0:512], start=False, stop=True)
                mi = e.matmul(ppf[3][:], negones[:], Rb[:, 512:1024], start=False, stop=True)
                mi.then_inc(s_mm.h, 1)
                s_mm.n += 1

        @b.vector
        def _(e):
            e.wait_ge(s_mm.h, 1)
            e.tensor_copy(Rb[:, :], pR[:, :]).then_inc(s_ms.h, 1)
            s_ms.n += 1
            s_v.n += 1
            e.nop().then_inc(s_v.h, 1)
            for j in range(NJ1):
                e.wait_ge(s_sc.h, 2 * (j + 1))
                e.scalar_tensor_tensor(h_all[j][:, 0:512], ppf[2][:], uscol[:, j:j + 1],
                                       sgst[j % 2][:, 0:512], ALU.mult, ALU.mult).then_inc(s_v.h, 1)
                e.scalar_tensor_tensor(h_all[j][:, 512:1024], ppf[3][:], uscol[:, j:j + 1],
                                       sgst[j % 2][:, 512:1024], ALU.mult, ALU.mult).then_inc(s_v.h, 1)
                s_v.n += 2

        @b.scalar
        def _(e):
            for j in range(NJ1):
                e.wait_ge(s_mm.h, 2 + j)
                if j >= 2:
                    e.wait_ge(s_v.h, 1 + 2 * (j - 1))
                e.activation(sgst[j % 2][:, 0:512], ppf[0][:], AF.Silu, scale=gscol[:, j:j + 1]).then_inc(s_sc.h, 1)
                e.activation(sgst[j % 2][:, 512:1024], ppf[1][:], AF.Silu, scale=gscol[:, j:j + 1]).then_inc(s_sc.h, 1)
                s_sc.n += 2

    esB.close()
    mmF = s_mm.n
    vF = s_v.n
    scF = s_sc.n
    msF = s_ms.n

    esC = ExitStack()
    dwq = esC.enter_context(nc.sbuf_tensor("dwq", [128, NJ1 * 256], BF16))
    ds1b = esC.enter_context(nc.sbuf_tensor("ds1b", [128, D], F32))
    with nc.Block() as b:
        @b.tensor
        def _(e):
            e.wait_ge(s_v.h, vF)
            e.wait_ge(s_sc.h, scF)
            for t in range(NT):
                if t >= 1:
                    e.wait_ge(s_v.h, vF + t)
                for j in range(NJ1):
                    mi = e.matmul(pRc[:], h_all[j][:, t * 128:(t + 1) * 128], onescol[:],
                                  start=(j == 0), stop=(j == NJ1 - 1))
                mi.then_inc(s_mm.h, 1)
                s_mm.n += 1
            for q in range(4):
                mi = e.matmul(ppf[q][:], ones1r[:], ds1row_sb[:, q * 512:(q + 1) * 512], start=True, stop=True)
            mi.then_inc(s_mm.h, 1)
            s_mm.n += 1

        @b.vector
        def _(e):
            for t in range(NT):
                e.wait_ge(s_mm.h, mmF + t + 1)
                e.tensor_copy(R2c[:, t:t + 1], pRc[:]).then_inc(s_v.h, 1)
                s_v.n += 1
            e.wait_ge(s_v.h, vF + NT)
            e.tensor_scalar_mul(mR2c[:], R2c[:], -128.0).then_inc(s_ms.h, 1)
            s_ms.n += 1
            e.wait_ge(s_mm.h, mmF + NT + 1)
            for q in range(4):
                ci = e.tensor_copy(ds1b[:, q * 512:(q + 1) * 512], ppf[q][:])
            ci.then_inc(s_ms.h, 1)
            s_ms.n += 1

    mmR = s_mm.n
    vR = s_v.n

    with nc.Block() as b:
        @b.gpsimd
        def _(e):
            e.wait_ge(s_mm.h, mmF)
            for dq in range(8):
                if dq >= 1:
                    e.wait_ge(s_mm.h, mmR + NT * dq)
                e.dma_start(dwq[:], d1_t[dq]).then_inc(s_wb.h, 16)
                s_wb.n += 16
                wb_marks[("d1", dq)] = s_wb.n
                for t in range(NT):
                    it_ = dq * NT + t
                    p = it_ % 2
                    e.wait_ge(s_v.h, vR + it_ + 1)
                    e.dma_start(ffn1_scr[t * 128:(t + 1) * 128, dq * 256:(dq + 1) * 256],
                                stgb[p][:, 0:256]).then_inc(s_fs[p].h, 16)
                    s_fs[p].n += 16
                    fs_marks[it_] = s_fs[p].n

        @b.tensor
        def _(e):
            e.wait_ge(s_ms.h, msF + 2)
            for dq in range(8):
                e.wait_ge(s_wb.h, wb_marks[("d1", dq)])
                for t in range(NT):
                    it_ = dq * NT + t
                    pf = ppf[it_ % 2]
                    if it_ >= 2:
                        e.wait_ge(s_v.h, vR + (it_ - 1))
                    for j in range(NJ1):
                        mi = e.matmul(pf[:, 0:256], h_all[j][:, t * 128:(t + 1) * 128],
                                      dwq[:, j * 256:(j + 1) * 256], start=(j == 0), stop=(j == NJ1 - 1))
                    mi.then_inc(s_mm.h, 1)
                    s_mm.n += 1

        @b.vector
        def _(e):
            for dq in range(8):
                for t in range(NT):
                    it_ = dq * NT + t
                    p = it_ % 2
                    pf = ppf[it_ % 2]
                    e.wait_ge(s_mm.h, mmR + it_ + 1)
                    if it_ >= 2:
                        e.wait_ge(s_fs[p].h, fs_marks[it_ - 2])
                    e.scalar_tensor_tensor(stgb[p][:, 0:256], pf[:, 0:256], mR2c[:, t:t + 1],
                                           ds1b[:, dq * 256:(dq + 1) * 256], ALU.add, ALU.mult).then_inc(s_v.h, 1)
                    s_v.n += 1

    esC.close()
    esA.close()
    mmD = s_mm.n
    vD = s_v.n

    o_sb = es.enter_context(nc.sbuf_tensor("o_sb", [128, NH * TC], BF16))
    esD = ExitStack()
    qts = esD.enter_context(nc.sbuf_tensor("qts", [128, NH * TC], BF16))
    kxs = esD.enter_context(nc.sbuf_tensor("kxs", [128, NH * 1536], BF16))
    vxs = esD.enter_context(nc.sbuf_tensor("vxs", [128, NKV * NH * 128], BF16))
    msks = esD.enter_context(nc.sbuf_tensor("msks", [128, 8 * 640], BF16))
    ssb = [esD.enter_context(nc.sbuf_tensor(f"ssb{k}", [128, 640], F32)) for k in range(2)]
    pbb = [esD.enter_context(nc.sbuf_tensor(f"pbb{k}", [128, 640], BF16)) for k in range(2)]
    pnb = [esD.enter_context(nc.sbuf_tensor(f"pnb{k}", [128, 640], BF16)) for k in range(2)]
    ptsb = [esD.enter_context(nc.sbuf_tensor(f"ptsb{k}", [128, 640], BF16)) for k in range(2)]
    mxc = esD.enter_context(nc.sbuf_tensor("mxc", [128, 2], F32))
    smc = esD.enter_context(nc.sbuf_tensor("smc", [128, 2], F32))
    rcc = esD.enter_context(nc.sbuf_tensor("rcc", [128, 2], F32))
    psc = [esD.enter_context(nc.psum_tensor(f"psc{k}", [128, 1024], F32)) for k in range(2)]
    ptt = [esD.enter_context(nc.psum_tensor(f"ptt{k}", [128, 1024], BF16)) for k in range(2)]
    pov = [esD.enter_context(nc.psum_tensor(f"pov{k}", [128, 512], F32)) for k in range(2)]

    NIT = NH * 8
    with nc.Block() as b:
        @b.sync
        def _(e):
            e.wait_ge(s_v.h, vD)
            e.dma_start(qts[:], qt_in).then_inc(s_c.h, 16)
            e.dma_start(kxs[:], kxt_in).then_inc(s_c.h, 16)
            e.dma_start(vxs[:], vxt_in).then_inc(s_c.h, 16)
            e.dma_start(msks[:], msk_in).then_inc(s_c.h, 16)
            s_c.n += 64

        @b.tensor
        def _(e):
            e.wait_ge(s_c.h, 64 + 64)
            e.wait_ge(s_v.h, vD)
            for it in range(NIT):
                h_, qb = divmod(it, 8)
                p = it % 2
                if it >= 2:
                    e.wait_ge(s_v.h, vD + 6 * (it - 2) + 1)
                e.matmul(psc[p][:, 0:512], qts[:, h_ * TC + qb * 128:h_ * TC + (qb + 1) * 128],
                         kxs[:, h_ * 1536 + qb * 128:h_ * 1536 + qb * 128 + 512], start=True, stop=True)
                mi = e.matmul(psc[p][:, 512:640], qts[:, h_ * TC + qb * 128:h_ * TC + (qb + 1) * 128],
                              kxs[:, h_ * 1536 + qb * 128 + 512:h_ * 1536 + qb * 128 + 640],
                              start=True, stop=True)
                mi.then_inc(s_mm.h, 1)
                e.wait_ge(s_v.h, vD + 6 * it + 4)
                if it >= 2:
                    e.wait_ge(s_v.h, vD + 6 * (it - 2) + 5)
                for kb in range(5):
                    mi = e.transpose(ptt[p][:, kb * 128:(kb + 1) * 128],
                                     pnb[p][:, kb * 128:(kb + 1) * 128], identt[:])
                mi.then_inc(s_mm.h, 1)
                e.wait_ge(s_v.h, vD + 6 * it + 5)
                if it >= 2:
                    e.wait_ge(s_v.h, vD + 6 * (it - 2) + 6)
                for kb in range(5):
                    mi = e.matmul(pov[p][:, 0:128],
                                  vxs[:, ((qb + kb) * NH + h_) * 128:((qb + kb) * NH + h_ + 1) * 128],
                                  ptsb[p][:, kb * 128:(kb + 1) * 128], start=(kb == 0), stop=(kb == 4))
                mi.then_inc(s_mm.h, 1)
                s_mm.n += 3

        @b.vector
        def _(e):
            for it in range(NIT):
                h_, qb = divmod(it, 8)
                p = it % 2
                e.wait_ge(s_mm.h, mmD + 3 * it + 1)
                e.tensor_tensor(ssb[p][:, 0:640], psc[p][:, 0:640], msks[:, qb * 640:(qb + 1) * 640],
                                ALU.add).then_inc(s_v.h, 1)
                e.wait_ge(s_v.h, vD + 6 * it + 1)
                e.reduce_max(mxc[:, p:p + 1], ssb[p][:, 0:640], axis=AX.X, negate=True).then_inc(s_v.h, 1)
                e.wait_ge(s_sc.h, scF + it + 1)
                e.reciprocal(rcc[:, p:p + 1], smc[:, p:p + 1]).then_inc(s_v.h, 1)
                e.wait_ge(s_v.h, vD + 6 * it + 3)
                e.tensor_scalar_mul(pnb[p][:, 0:640], pbb[p][:, 0:640], rcc[:, p:p + 1]).then_inc(s_v.h, 1)
                e.wait_ge(s_mm.h, mmD + 3 * it + 2)
                e.tensor_copy(ptsb[p][:, 0:640], ptt[p][:, 0:640]).then_inc(s_v.h, 1)
                e.wait_ge(s_mm.h, mmD + 3 * it + 3)
                e.tensor_copy(o_sb[:, h_ * TC + qb * 128:h_ * TC + (qb + 1) * 128],
                              pov[p][:, 0:128]).then_inc(s_v.h, 1)
                s_v.n += 6

        @b.scalar
        def _(e):
            for it in range(NIT):
                p = it % 2
                e.wait_ge(s_v.h, vD + 6 * it + 2)
                e.activation(pbb[p][:, 0:640], ssb[p][:, 0:640], AF.Exp, bias=mxc[:, p:p + 1],
                             accum_out=smc[:, p:p + 1]).then_inc(s_sc.h, 1)
                s_sc.n += 1

    esD.close()
    mmA = s_mm.n
    vA = s_v.n

    esE = ExitStack()
    stg = [esE.enter_context(nc.sbuf_tensor(f"stg{k}", [128, TC], F32)) for k in range(2)]
    woh = esE.enter_context(nc.sbuf_tensor("woh", [128, NDC * 1024], BF16))
    xts = [esE.enter_context(nc.sbuf_tensor(f"xts{k}", [128, 1024], F32)) for k in range(2)]
    fts = [esE.enter_context(nc.sbuf_tensor(f"fts{k}", [128, 1024], F32)) for k in range(2)]
    ppo = [esE.enter_context(nc.psum_tensor(f"ppo{k}", [128, 512], F32)) for k in range(4)]
    with nc.Block() as b:
        @b.sync
        def _(e):
            e.wait_ge(s_v.h, vA)
            e.wait_ge(s_fs[0].h, s_fs[0].n)
            e.wait_ge(s_fs[1].h, s_fs[1].n)
            for dh_ in range(2):
                for t in range(NT):
                    it_ = dh_ * NT + t
                    p = it_ % 2
                    if it_ >= 2:
                        e.wait_ge(s_v.h, vA + 2 * (it_ - 1))
                    e.dma_start(xts[p][:], x1_in[t * 128:(t + 1) * 128, dh_ * 1024:(dh_ + 1) * 1024]).then_inc(s_xl[p].h, 16)
                    e.dma_start(fts[p][:], ffn1_scr[t * 128:(t + 1) * 128, dh_ * 1024:(dh_ + 1) * 1024]).then_inc(s_xl[p].h, 16)
                    s_xl[p].n += 32
                    xl_marks[it_] = s_xl[p].n

        @b.gpsimd
        def _(e):
            e.wait_ge(s_v.h, vA)
            for dh_ in range(2):
                if dh_ == 1:
                    e.wait_ge(s_mm.h, mmA + NT)
                e.dma_start(woh[:], wo1_t[dh_]).then_inc(s_wb.h, 16)
                s_wb.n += 16
                wb_marks[("wo1", dh_)] = s_wb.n
                for t in range(NT):
                    it_ = dh_ * NT + t
                    p = it_ % 2
                    e.wait_ge(s_v.h, vA + 2 * (it_ + 1))
                    e.dma_start(out_f[t * 128:(t + 1) * 128, dh_ * 1024:(dh_ + 1) * 1024],
                                stg[p][:]).then_inc(s_os[p].h, 16)
                    s_os[p].n += 16
            e.wait_ge(s_os[0].h, s_os[0].n)
            e.wait_ge(s_os[1].h, s_os[1].n)

        @b.tensor
        def _(e):
            for dh_ in range(2):
                e.wait_ge(s_wb.h, wb_marks[("wo1", dh_)])
                for t in range(NT):
                    it_ = dh_ * NT + t
                    pa, pb = (ppo[0], ppo[1]) if it_ % 2 == 0 else (ppo[2], ppo[3])
                    if it_ >= 2:
                        e.wait_ge(s_v.h, vA + 2 * (it_ - 1))
                    for i in range(NDC):
                        st_, sp_ = (i == 0), (i == NDC - 1)
                        e.matmul(pa[:], o_sb[:, i * TC + t * 128:i * TC + (t + 1) * 128],
                                 woh[:, i * 1024:i * 1024 + 512], start=st_, stop=sp_)
                        mi = e.matmul(pb[:], o_sb[:, i * TC + t * 128:i * TC + (t + 1) * 128],
                                      woh[:, i * 1024 + 512:(i + 1) * 1024], start=st_, stop=sp_)
                        if sp_:
                            mi.then_inc(s_mm.h, 1)
                    s_mm.n += 1

        @b.vector
        def _(e):
            for dh_ in range(2):
                for t in range(NT):
                    it_ = dh_ * NT + t
                    p = it_ % 2
                    pa, pb = (ppo[0], ppo[1]) if p == 0 else (ppo[2], ppo[3])
                    e.wait_ge(s_xl[p].h, xl_marks[it_])
                    e.wait_ge(s_mm.h, mmA + it_ + 1)
                    if it_ >= 2:
                        e.wait_ge(s_os[p].h, 16 * (it_ // 2))
                    e.tensor_tensor(stg[p][:, 0:512], xts[p][:, 0:512], pa[:], ALU.add)
                    e.tensor_tensor(stg[p][:, 512:1024], xts[p][:, 512:1024], pb[:], ALU.add).then_inc(s_v.h, 1)
                    e.wait_ge(s_v.h, vA + 2 * it_ + 1)
                    e.tensor_tensor(stg[p][:], stg[p][:], fts[p][:], ALU.add).then_inc(s_v.h, 1)
                    s_v.n += 2

    esE.close()
    es.close()
    return nc


def host_inputs_l4(inputs, x1_all, xn1_all, q_all, k_all, v_all):
    g1 = tile_wT_i32(np.asarray(inputs["L1_ffn_gate_w"]).T, NJ1)
    u1 = tile_wT_i32(np.asarray(inputs["L1_ffn_up_w"]).T, NJ1)
    d1T = np.ascontiguousarray(np.asarray(inputs["L1_ffn_down_w"], np.int32).T)
    d1 = np.ascontiguousarray(d1T.reshape(NJ1, 128, 8, 256).transpose(2, 1, 0, 3).reshape(8, 128, NJ1 * 256))
    wo1T = f32(np.asarray(inputs["L1_attn_w_out"]).T)
    wo1 = np.ascontiguousarray(wo1T.reshape(NDC, 128, 2, 1024).transpose(2, 1, 0, 3).reshape(2, 128, NDC * 1024))
    gs1 = col_major_pc(inputs["L1_ffn_gate_scale"], NJ1)
    us1 = col_major_pc(inputs["L1_ffn_up_scale"], NJ1)
    ds1 = f32(np.asarray(inputs["L1_ffn_down_scale"]).reshape(1, D))

    per_core = []
    jj = np.arange(640)
    pp_ = np.arange(128)
    for c in range(NCORE):
        bq, hh = c // 2, c % 2
        base = bq * 2048 + hh * 1024
        if hh == 0:
            kx = np.concatenate([np.zeros((D, 512), dtype=BF), k_all[:, base:base + 1024]], axis=1)
            vx = np.concatenate([np.zeros((512, D), dtype=BF), v_all[base:base + 1024]], axis=0)
        else:
            kx = np.ascontiguousarray(k_all[:, base - 512:base + 1024])
            vx = np.ascontiguousarray(v_all[base - 512:base + 1024])
        qf = np.ascontiguousarray(q_all[:, base:base + 1024])
        msk = np.zeros((8, 128, 640), np.float32)
        for qb in range(8):
            kpos = hh * 1024 + qb * 128 - 512 + jj
            band = (jj[None, :] >= pp_[:, None]) & (jj[None, :] <= pp_[:, None] + 512)
            ok = band & (kpos >= 0)[None, :]
            msk[qb] = np.where(ok, 0.0, -1e30).astype(np.float32)
        m = dict(
            qt=np.ascontiguousarray(qf.reshape(NH, 128, TC).transpose(1, 0, 2).reshape(128, NH * TC)),
            kxt=np.ascontiguousarray(kx.reshape(NH, 128, 1536).transpose(1, 0, 2).reshape(128, NH * 1536)),
            vxt=np.ascontiguousarray(vx.reshape(NKV, 128, NH, 128).transpose(1, 0, 2, 3).reshape(128, NKV * NH * 128)),
            msk=np.ascontiguousarray(msk.transpose(1, 0, 2).reshape(128, 8 * 640)).astype(BF),
            x1=np.ascontiguousarray(x1_all[base:base + 1024]),
            xn1t=np.ascontiguousarray(xn1_all[:, base:base + 1024].reshape(NDC, 128, TC).transpose(1, 0, 2).reshape(128, NDC * TC)),
            g1_t=g1, u1_t=u1, d1_t=d1,
            gs1_pc=gs1, us1_pc=us1, ds1_row=ds1, wo1_t=wo1,
            ident=np.eye(128, dtype=BF),
        )
        per_core.append(m)
    return per_core



def kernel(**inputs):
    return kernel_fused(**inputs)


def kernel_unfused(**inputs):
    cores = list(range(NCORE))

    nc1 = build_launch1()
    r1 = run_bass_kernel_spmd(nc1, host_inputs_l1(inputs), core_ids=cores).results
    zx_all = np.concatenate([r1[c]["zx"] for c in range(NCORE)], axis=1)
    delta_all = np.concatenate([r1[c]["delta"] for c in range(NCORE)], axis=1)
    ffn0_all = np.concatenate([r1[c]["ffn0"] for c in range(NCORE)], axis=0)

    nc2 = build_launch2()
    r2 = run_bass_kernel_spmd(nc2, host_inputs_l2(inputs, zx_all, delta_all), core_ids=cores).results
    y2_all = np.concatenate([r2[c]["y2"] for c in range(NCORE)], axis=0)

    x_full = f32(np.asarray(inputs["L0_attn_hidden_states"])).reshape(NB * L, D)
    nc3 = build_launch3()
    r3 = run_bass_kernel_spmd(nc3, host_inputs_l3(inputs, x_full, ffn0_all, y2_all), core_ids=cores).results
    x1_all = np.concatenate([r3[c]["x1"] for c in range(NCORE)], axis=0)
    xn1_all = np.concatenate([r3[c]["xn1"] for c in range(NCORE)], axis=1)
    q_all = np.concatenate([r3[c]["qf"] for c in range(NCORE)], axis=1)
    k_all = np.concatenate([r3[c]["kf"] for c in range(NCORE)], axis=1)
    v_all = np.concatenate([r3[c]["vt"] for c in range(NCORE)], axis=0)

    nc4 = build_launch4()
    r4 = run_bass_kernel_spmd(nc4, host_inputs_l4(inputs, x1_all, xn1_all, q_all, k_all, v_all),
                              core_ids=cores).results
    out = np.concatenate([r4[c]["out"] for c in range(NCORE)], axis=0)
    return np.ascontiguousarray(out.reshape(NB, L, D), dtype=np.float32)





TT = 2048
TH = 1536
HOFF = 512
NTT = TT // 128
NTH = TH // 128
NIY = DI // 128


def build_fused():
    nc = bass.Bass("TRN2", target_bir_lowering=False)
    x_in = nc.dram_tensor("x", [TT, D], F32, kind="ExternalInput").ap()
    w_in_t = nc.dram_tensor("w_in_t", [NJ_IN, 128, NDC * 128], BF16, kind="ExternalInput").ap()
    w_dtT = nc.dram_tensor("w_dtT", [DTR, DI], BF16, kind="ExternalInput").ap()
    bdt_pc = nc.dram_tensor("bdt_pc", [128, NJ_DT], F32, kind="ExternalInput").ap()
    gw_t = nc.dram_tensor("gw_t", [NJ_F, 128, NDC * 128], BF16, kind="ExternalInput").ap()
    uw_t = nc.dram_tensor("uw_t", [NJ_F, 128, NDC * 128], BF16, kind="ExternalInput").ap()
    dw_t = nc.dram_tensor("dw_t", [2, 128, NJ_F * 1024], BF16, kind="ExternalInput").ap()
    gs_pc = nc.dram_tensor("gs_pc", [128, NJ_F], F32, kind="ExternalInput").ap()
    us_pc = nc.dram_tensor("us_pc", [128, NJ_F], F32, kind="ExternalInput").ap()
    ds_row = nc.dram_tensor("ds_row", [1, D], F32, kind="ExternalInput").ap()
    ident_in = nc.dram_tensor("ident", [128, 128], BF16, kind="ExternalInput").ap()
    alog_pc = nc.dram_tensor("alog_pc", [128, NIY * DST], F32, kind="ExternalInput").ap()
    cw_pc = nc.dram_tensor("cw_pc", [128, NIY * DCONV], F32, kind="ExternalInput").ap()
    cb_pc = nc.dram_tensor("cb_pc", [128, NIY], F32, kind="ExternalInput").ap()
    dp_pc = nc.dram_tensor("dp_pc", [128, NIY], F32, kind="ExternalInput").ap()
    eexp_in = nc.dram_tensor("eexp", [16, 128], BF16, kind="ExternalInput").ap()
    ebc_in = nc.dram_tensor("ebc", [DST, 16, 128], BF16, kind="ExternalInput").ap()
    scs_in = nc.dram_tensor("scs", [DST, 128, 128], BF16, kind="ExternalInput").ap()
    wo_t = nc.dram_tensor("wo_t", [2, 128, NIY * 1024], BF16, kind="ExternalInput").ap()
    wqk_t = nc.dram_tensor("wqk_t", [2 * NDC, 128, NDC * 128], BF16, kind="ExternalInput").ap()
    wv_t = nc.dram_tensor("wv_t", [128, NDC * D], BF16, kind="ExternalInput").ap()
    msk_in = nc.dram_tensor("msk", [128, 8 * 640], BF16, kind="ExternalInput").ap()
    g1_t = nc.dram_tensor("g1_t", [NJ1, 128, NDC * 128], I32, kind="ExternalInput").ap()
    u1_t = nc.dram_tensor("u1_t", [NJ1, 128, NDC * 128], I32, kind="ExternalInput").ap()
    d1_t = nc.dram_tensor("d1_t", [8, 128, NJ1 * 256], I32, kind="ExternalInput").ap()
    gs1_pc = nc.dram_tensor("gs1_pc", [128, NJ1], F32, kind="ExternalInput").ap()
    us1_pc = nc.dram_tensor("us1_pc", [128, NJ1], F32, kind="ExternalInput").ap()
    ds1_row = nc.dram_tensor("ds1_row", [1, D], F32, kind="ExternalInput").ap()
    wo1_t = nc.dram_tensor("wo1_t", [2, 128, NDC * 1024], BF16, kind="ExternalInput").ap()

    out_f = nc.dram_tensor("out", [TC, D], F32, kind="ExternalOutput").ap()

    zx_scr = nc.dram_tensor("zx_scr", [ZXROWS, TT], BF16, kind="Internal").ap()
    dl_scr = nc.dram_tensor("dl_scr", [DI, TT], F32, kind="Internal").ap()
    h_scr = nc.dram_tensor("h_scr", [128, NJ_F, TH], BF16, kind="Internal").ap()
    ffn0_scr = nc.dram_tensor("ffn0_scr", [TH, D], F32, kind="Internal").ap()
    y2_scr = nc.dram_tensor("y2_scr", [128, NIY, TH], BF16, kind="Internal").ap()
    x1_scr = nc.dram_tensor("x1_scr", [TH, D], F32, kind="Internal").ap()
    xn1_scr = nc.dram_tensor("xn1_scr", [128, NDC * TH], BF16, kind="Internal").ap()
    q_scr = nc.dram_tensor("q_scr", [128, NH * TC], BF16, kind="Internal").ap()
    k_scr = nc.dram_tensor("k_scr", [128, NH * TH], BF16, kind="Internal").ap()
    v_scr = nc.dram_tensor("v_scr", [128, NKV * NH * 128], BF16, kind="Internal").ap()
    ffn1_scr = nc.dram_tensor("ffn1_scr", [TC, D], F32, kind="Internal").ap()

    es = ExitStack()
    ident = es.enter_context(nc.sbuf_tensor("identt", [128, 128], BF16))
    wbuf = [es.enter_context(nc.sbuf_tensor(f"wbuf{k}", [128, NDC * 128], BF16)) for k in range(2)]
    stgb = [es.enter_context(nc.sbuf_tensor(f"stgb{k}", [128, TT], BF16)) for k in range(2)]
    konst = es.enter_context(nc.sbuf_tensor("konst", [128, 2], F32))
    onescol = es.enter_context(nc.sbuf_tensor("onescol", [128, 1], BF16))
    negones = es.enter_context(nc.sbuf_tensor("negones", [1, 128], BF16))
    ones1r = es.enter_context(nc.sbuf_tensor("ones1r", [1, 128], BF16))
    mcol = es.enter_context(nc.sbuf_tensor("mcol", [128, NTT], F32))
    rcol = es.enter_context(nc.sbuf_tensor("rcol", [128, NTT], F32))
    scr1 = es.enter_context(nc.sbuf_tensor("scr1", [128, NTT], F32))
    scr2 = es.enter_context(nc.sbuf_tensor("scr2", [128, NTT], F32))


    s_c = Sem(es, nc, "s_c")
    s_cg = Sem(es, nc, "s_cg")
    s_x = [Sem(es, nc, f"s_x{k}") for k in range(2)]
    s_wq = [Sem(es, nc, f"s_wq{k}") for k in range(2)]
    s_uq = [Sem(es, nc, f"s_uq{k}") for k in range(2)]
    s_wb = Sem(es, nc, "s_wb")
    s_sb = [Sem(es, nc, f"s_sb{k}") for k in range(2)]
    s_sg = [Sem(es, nc, f"s_sg{k}") for k in range(2)]
    s_go = [Sem(es, nc, f"s_go{k}") for k in range(2)]
    s_hb = [Sem(es, nc, f"s_hb{k}") for k in range(2)]
    s_dl = [Sem(es, nc, f"s_dl{k}") for k in range(2)]
    s_cz = [Sem(es, nc, f"s_cz{k}") for k in range(2)]
    s_v = Sem(es, nc, "s_v")
    s_sc = Sem(es, nc, "s_sc")
    s_mm = Sem(es, nc, "s_mm")
    s_ms = Sem(es, nc, "s_ms")
    s_pad = Sem(es, nc, "s_pad")
    s_ax1 = Sem(es, nc, "s_ax1")
    s_ax2 = Sem(es, nc, "s_ax2")
    s_gm = Sem(es, nc, "s_gm")
    s_xe = [Sem(es, nc, f"s_xe{k}") for k in range(2)]

    x_marks = {}
    wq_marks = {}
    uq_marks = {}
    wb_marks = {}
    sb_marks = {}
    sg_marks = {}
    go_marks = {}
    hb_marks = {}
    dl_marks = {}
    cz_marks = {}

    es1b = ExitStack()
    dsb = es1b.enter_context(nc.sbuf_tensor("dsb", [128, D], F32))
    dsrow_sb = es1b.enter_context(nc.sbuf_tensor("dsrow_sb", [1, D], BF16))
    ptp = [es1b.enter_context(nc.psum_tensor(f"ptp{k}", [128, 1024], BF16)) for k in range(2)]
    pp = [es1b.enter_context(nc.psum_tensor(f"pp{k}", [128, 512], F32)) for k in range(4)]
    es1 = ExitStack()
    stg = [es1.enter_context(nc.sbuf_tensor(f"stg{k}", [128, TT], F32)) for k in range(2)]
    es1x = ExitStack()
    sgst = [es1x.enter_context(nc.sbuf_tensor(f"sgst{k}", [128, TT], BF16)) for k in range(2)]
    xnTall = es1x.enter_context(nc.sbuf_tensor("xnTall", [128, NDC * TT], BF16))
    dtl_sb = es1x.enter_context(nc.sbuf_tensor("dtl_sb", [128, TT], BF16))
    bdt1 = es1x.enter_context(nc.sbuf_tensor("bdt1", [128, NJ_DT], F32))
    bdt2 = es1x.enter_context(nc.sbuf_tensor("bdt2", [128, NJ_DT], F32))
    gscol = es1x.enter_context(nc.sbuf_tensor("gscol", [128, NJ_F], F32))
    uscol = es1x.enter_context(nc.sbuf_tensor("uscol", [128, NJ_F], F32))
    xnT = [xnTall[:, j * TT:(j + 1) * TT] for j in range(NDC)]
    xnTH = [xnTall[:, j * TT + HOFF:(j + 1) * TT] for j in range(NDC)]

    with nc.sbuf_tensor("xt0", [128, D], F32) as xt0, \
         nc.sbuf_tensor("xt1", [128, D], F32) as xt1, \
         nc.sbuf_tensor("sq0", [128, D], F32) as sq0, \
         nc.sbuf_tensor("sq1", [128, D], F32) as sq1, \
         nc.sbuf_tensor("xnb0", [128, D], BF16) as xnb0, \
         nc.sbuf_tensor("xnb1", [128, D], BF16) as xnb1:
        xt = [xt0, xt1]
        sq = [sq0, sq1]
        xnb = [xnb0, xnb1]

        with nc.Block() as b:
            @b.sync
            def _(e):
                e.dma_start(ident[:], ident_in).then_inc(s_c.h, 16)
                e.dma_start(bdt1[:], bdt_pc).then_inc(s_c.h, 16)
                e.dma_start(gscol[:], gs_pc).then_inc(s_c.h, 16)
                e.dma_start(uscol[:], us_pc).then_inc(s_c.h, 16)
                s_c.n += 64
                for i in range(NTT):
                    p = i % 2
                    if i >= 2:
                        e.wait_ge(s_v.h, 22 * (i - 1))
                        e.wait_ge(s_sc.h, 3 * (i - 1))
                    e.dma_start(xt[p][:], x_in[i * 128:(i + 1) * 128, :]).then_inc(s_x[p].h, 16)
                    s_x[p].n += 16
                    x_marks[("ln0", i)] = s_x[p].n

            @b.gpsimd
            def _(e):
                e.dma_start(dsrow_sb[:], ds_row).then_inc(s_cg.h, 16)
                e.memset(ones1r[:], 1.0)
                e.memset(onescol[:], 1.0)
                e.memset(negones[:], -128.0)
                e.memset(konst[:, 0:1], EPS)
                e.memset(konst[:, 1:2], -0.5).then_inc(s_pad.h, 1)
                s_pad.n += 1
                s_cg.n += 16
                e.dma_start(wbuf[0][:], w_in_t[NJ_IN - 1]).then_inc(s_wq[0].h, 16)
                s_wq[0].n += 16
                wq_marks[("in", NJ_IN - 1)] = (0, s_wq[0].n)

            @b.scalar
            def _(e):
                e.wait_ge(s_pad.h, 1)
                for i in range(NTT):
                    p = i % 2
                    e.wait_ge(s_x[p].h, x_marks[("ln0", i)])
                    e.square(sq[p][:], xt[p][:]).then_inc(s_sc.h, 1)
                    e.wait_ge(s_v.h, 22 * i + 5)
                    e.activation(scr1[:, i:i + 1], scr2[:, i:i + 1], AF.Ln, bias=konst[:, 0:1]).then_inc(s_sc.h, 1)
                    e.wait_ge(s_sc.h, 3 * i + 2)
                    e.activation(rcol[:, i:i + 1], scr1[:, i:i + 1], AF.Exp, scale=konst[:, 1:2]).then_inc(s_sc.h, 1)
                    s_sc.n += 3

            @b.vector
            def _(e):
                e.wait_ge(s_c.h, 64)
                e.tensor_add(bdt2[:], bdt1[:], bdt1[:]).then_inc(s_ms.h, 1)
                s_ms.n += 1
                for i in range(NTT):
                    p = i % 2
                    e.wait_ge(s_x[p].h, x_marks[("ln0", i)])
                    e.reduce_sum(scr1[:, i:i + 1], xt[p][:], axis=AX.X).then_inc(s_v.h, 1)
                    e.wait_ge(s_v.h, 22 * i + 1)
                    e.tensor_scalar_mul(mcol[:, i:i + 1], scr1[:, i:i + 1], 1.0 / D).then_inc(s_v.h, 1)
                    e.wait_ge(s_sc.h, 3 * i + 1)
                    e.wait_ge(s_v.h, 22 * i + 2)
                    e.reduce_sum(scr2[:, i:i + 1], sq[p][:], axis=AX.X).then_inc(s_v.h, 1)
                    e.wait_ge(s_v.h, 22 * i + 3)
                    e.tensor_scalar_mul(scr2[:, i:i + 1], scr2[:, i:i + 1], 1.0 / D)
                    e.tensor_mul(scr1[:, i:i + 1], mcol[:, i:i + 1], mcol[:, i:i + 1]).then_inc(s_v.h, 1)
                    e.wait_ge(s_v.h, 22 * i + 4)
                    e.tensor_sub(scr2[:, i:i + 1], scr2[:, i:i + 1], scr1[:, i:i + 1]).then_inc(s_v.h, 1)
                    e.wait_ge(s_sc.h, 3 * i + 3)
                    if i >= 2:
                        e.wait_ge(s_mm.h, 16 * (i - 1))
                    e.tensor_scalar(xnb[p][:], xt[p][:], mcol[:, i:i + 1], rcol[:, i:i + 1],
                                    ALU.subtract, ALU.mult).then_inc(s_v.h, 1)
                    s_v.n += 6
                    for j in range(NDC):
                        e.wait_ge(s_mm.h, 16 * i + j + 1)
                        e.tensor_copy(xnT[j][:, i * 128:(i + 1) * 128],
                                      ptp[(16 * i + j) % 2][:, 0:128]).then_inc(s_v.h, 1)
                        s_v.n += 1

            @b.tensor
            def _(e):
                e.wait_ge(s_c.h, 64)
                for i in range(NTT):
                    e.wait_ge(s_v.h, 22 * i + 6)
                    for j in range(NDC):
                        k = 16 * i + j
                        if k >= 2:
                            i2, j2 = divmod(k - 2, 16)
                            e.wait_ge(s_v.h, 22 * i2 + 6 + j2 + 1)
                        e.transpose(ptp[k % 2][:, 0:128], xnb[i % 2][:, j * 128:(j + 1) * 128],
                                    ident[:]).then_inc(s_mm.h, 1)
                        s_mm.n += 1

    mm0 = s_mm.n
    v0 = s_v.n

    jorder = [NJ_IN - 1] + list(range(NJ_IN - 1))
    with nc.Block() as b:
        @b.gpsimd
        def _(e):
            for idx in range(1, NJ_IN):
                j = jorder[idx]
                p = idx % 2
                if idx >= 2:
                    e.wait_ge(s_mm.h, mm0 + 64 * (idx - 1))
                e.dma_start(wbuf[p][:], w_in_t[j]).then_inc(s_wq[p].h, 16)
                s_wq[p].n += 16
                wq_marks[("in", j)] = (p, s_wq[p].n)

        @b.tensor
        def _(e):
            e.wait_ge(s_v.h, v0)
            for idx in range(NJ_IN):
                j = jorder[idx]
                p, mark = wq_marks[("in", j)]
                e.wait_ge(s_wq[p].h, mark)
                pa, pb = (pp[0], pp[1]) if idx % 2 == 0 else (pp[2], pp[3])
                for half in range(2):
                    if half == 1:
                        e.wait_ge(s_v.h, v0 + 2 * idx + 1)
                    elif idx >= 2:
                        e.wait_ge(s_v.h, v0 + 2 * (idx - 1))
                    for i in range(NDC):
                        st_, sp_ = (i == 0), (i == NDC - 1)
                        e.matmul(pa[:], wbuf[p][:, i * 128:(i + 1) * 128],
                                 xnT[i][:, half * 1024:half * 1024 + 512], start=st_, stop=sp_)
                        mi = e.matmul(pb[:], wbuf[p][:, i * 128:(i + 1) * 128],
                                      xnT[i][:, half * 1024 + 512:(half + 1) * 1024], start=st_, stop=sp_)
                        if sp_:
                            mi.then_inc(s_mm.h, 32)
                    s_mm.n += 32

        @b.vector
        def _(e):
            for idx in range(NJ_IN):
                p = idx % 2
                pa, pb = (pp[0], pp[1]) if p == 0 else (pp[2], pp[3])
                for half in range(2):
                    e.wait_ge(s_mm.h, mm0 + 64 * idx + 32 * (half + 1))
                    if idx >= 2 and half == 0:
                        e.wait_ge(s_sb[p].h, 16 * (idx // 2))
                    if idx == 0:
                        e.tensor_copy(dtl_sb[:, half * 1024:half * 1024 + 512], pa[:])
                        ci = e.tensor_copy(dtl_sb[:, half * 1024 + 512:(half + 1) * 1024], pb[:])
                        if half == 1:
                            ci.then_inc(s_ms.h, 1)
                            s_ms.n += 1
                    e.tensor_copy(stgb[p][:, half * 1024:half * 1024 + 512], pa[:])
                    e.tensor_copy(stgb[p][:, half * 1024 + 512:(half + 1) * 1024], pb[:]).then_inc(s_v.h, 1)
                    s_v.n += 1

        @b.sync
        def _(e):
            for idx in range(NJ_IN):
                j = jorder[idx]
                p = idx % 2
                e.wait_ge(s_v.h, v0 + 2 * (idx + 1))
                e.dma_start(zx_scr[j * 128:(j + 1) * 128, :], stgb[p][:]).then_inc(s_sb[p].h, 16)
                s_sb[p].n += 16
                sb_marks[("zx", j)] = s_sb[p].n

    c_dtl = s_c.n

    mm1 = s_mm.n
    v1 = s_v.n
    sc1 = s_sc.n

    wdtall = es1x.enter_context(nc.sbuf_tensor("wdtall", [128, NJ_DT * 128], BF16))
    spA = es1x.enter_context(nc.sbuf_tensor("spA", [128, TT], F32))
    spB = es1x.enter_context(nc.sbuf_tensor("spB", [128, TT], F32))
    spx = [spA, spB]
    pp4 = es1b.enter_context(nc.psum_tensor("pp4", [128, 512], F32))
    pp5 = es1b.enter_context(nc.psum_tensor("pp5", [128, 512], F32))
    with nc.Block() as b:
        @b.gpsimd
        def _(e):
            e.dma_start(wdtall[:], w_dtT).then_inc(s_uq[0].h, 16)
            s_uq[0].n += 16
            uq_marks["wdt"] = s_uq[0].n

        @b.tensor
        def _(e):
            e.wait_ge(s_ms.h, 2)
            e.wait_ge(s_c.h, c_dtl)
            e.wait_ge(s_v.h, v1)
            e.wait_ge(s_uq[0].h, uq_marks["wdt"])
            for jj in range(NJ_DT):
                for half in range(2):
                    if half == 1:
                        e.wait_ge(s_sc.h, sc1 + 8 * jj + 2)
                        e.wait_ge(s_v.h, v1 + 4 * jj + 1)
                    elif jj >= 1:
                        e.wait_ge(s_sc.h, sc1 + 8 * (jj - 1) + 6)
                        e.wait_ge(s_v.h, v1 + 4 * (jj - 1) + 3)
                    e.matmul(pp4[:], wdtall[:, jj * 128:(jj + 1) * 128], dtl_sb[:, half * 1024:half * 1024 + 512],
                             start=True, stop=True)
                    e.matmul(pp5[:], wdtall[:, jj * 128:(jj + 1) * 128], dtl_sb[:, half * 1024 + 512:(half + 1) * 1024],
                             start=True, stop=True).then_inc(s_mm.h, 2)
                    s_mm.n += 2

        @b.scalar
        def _(e):
            for jj in range(NJ_DT):
                p = jj % 2
                for half in range(2):
                    e.wait_ge(s_mm.h, mm1 + 4 * jj + 2 * (half + 1))
                    if jj >= 2 and half == 0:
                        e.wait_ge(s_v.h, v1 + 4 * (jj - 1))
                    e.activation(spx[p][:, half * 1024:half * 1024 + 512], pp4[:],
                                 AF.Abs, bias=bdt2[:, jj:jj + 1]).then_inc(s_sc.h, 1)
                    e.activation(spx[p][:, half * 1024 + 512:(half + 1) * 1024], pp5[:],
                                 AF.Abs, bias=bdt2[:, jj:jj + 1]).then_inc(s_sc.h, 1)
                    e.wait_ge(s_sc.h, sc1 + 8 * jj + 4 * half + 2)
                    e.activation(spx[p][:, half * 1024:(half + 1) * 1024],
                                 spx[p][:, half * 1024:(half + 1) * 1024], AF.Exp, scale=-1.0).then_inc(s_sc.h, 1)
                    e.wait_ge(s_sc.h, sc1 + 8 * jj + 4 * half + 3)
                    e.activation(spx[p][:, half * 1024:(half + 1) * 1024],
                                 spx[p][:, half * 1024:(half + 1) * 1024], AF.Ln, bias=1.0).then_inc(s_sc.h, 1)
                    s_sc.n += 4

        @b.vector
        def _(e):
            for jj in range(NJ_DT):
                p = jj % 2
                for half in range(2):
                    e.wait_ge(s_mm.h, mm1 + 4 * jj + 2 * (half + 1))
                    if jj >= 2 and half == 0:
                        e.wait_ge(s_sg[p].h, 16 * (jj // 2))
                    e.tensor_scalar(stg[p][:, half * 1024:half * 1024 + 512], pp4[:],
                                    bdt2[:, jj:jj + 1], 0.0, ALU.add, ALU.max)
                    e.tensor_scalar(stg[p][:, half * 1024 + 512:(half + 1) * 1024], pp5[:],
                                    bdt2[:, jj:jj + 1], 0.0, ALU.add, ALU.max).then_inc(s_v.h, 1)
                    e.wait_ge(s_sc.h, sc1 + 8 * jj + 4 * half + 4)
                    e.wait_ge(s_v.h, v1 + 4 * jj + 2 * half + 1)
                    e.tensor_add(stg[p][:, half * 1024:(half + 1) * 1024],
                                 stg[p][:, half * 1024:(half + 1) * 1024],
                                 spx[p][:, half * 1024:(half + 1) * 1024]).then_inc(s_v.h, 1)
                    s_v.n += 2

        @b.sync
        def _(e):
            for jj in range(NJ_DT):
                p = jj % 2
                e.wait_ge(s_v.h, v1 + 4 * (jj + 1))
                e.dma_start(dl_scr[jj * 128:(jj + 1) * 128, :], stg[p][:]).then_inc(s_sg[p].h, 16)
                s_sg[p].n += 16
                sg_marks[("dl", jj)] = s_sg[p].n

    mm2 = s_mm.n
    v2 = s_v.n
    sc2 = s_sc.n
    sb2 = [s_sb[0].n, s_sb[1].n]
    sg2 = [s_sg[0].n, s_sg[1].n]

    with nc.Block() as b:
        @b.gpsimd
        def _(e):
            for j in range(NJ_F):
                pj = j % 2
                if j >= 1:
                    e.wait_ge(s_mm.h, mm2 + 6 * j)
                e.dma_start(wbuf[pj][:], gw_t[j]).then_inc(s_wq[pj].h, 16)
                s_wq[pj].n += 16
                wq_marks[("g0", j)] = s_wq[pj].n
                e.dma_start(sgst[pj][:, 0:NDC * 128].rearrange("p (i o) -> p i o", i=NDC), uw_t[j]) \
                    .then_inc(s_uq[pj].h, 16)
                s_uq[pj].n += 16
                uq_marks[("u0", j)] = s_uq[pj].n

        @b.tensor
        def _(e):
            e.wait_ge(s_sc.h, sc2)
            for j in range(NJ_F):
                pj = j % 2
                e.wait_ge(s_wq[pj].h, wq_marks[("g0", j)])
                e.wait_ge(s_uq[pj].h, uq_marks[("u0", j)])
                if j >= 1:
                    e.wait_ge(s_v.h, v2 + 4 * j)
                for q in range(3):
                    if q == 2:
                        e.wait_ge(s_sc.h, sc2 + 3 * j + 1)
                        e.wait_ge(s_v.h, v2 + 4 * j + 1)
                    for i in range(NDC):
                        st_, sp_ = (i == 0), (i == NDC - 1)
                        e.matmul(pp[q % 2][:], wbuf[pj][:, i * 128:(i + 1) * 128],
                                 xnTH[i][:, q * 512:(q + 1) * 512], start=st_, stop=sp_)
                        mi = e.matmul(pp[2 + q % 2][:], sgst[pj][:, i * 128:(i + 1) * 128],
                                      xnTH[i][:, q * 512:(q + 1) * 512], start=st_, stop=sp_)
                        if sp_:
                            mi.then_inc(s_mm.h, 2)
                    s_mm.n += 2

        @b.scalar
        def _(e):
            for j in range(NJ_F):
                pj = j % 2
                if j < 2:
                    e.wait_ge(s_sg[pj].h, sg2[pj])
                for q in range(3):
                    e.wait_ge(s_mm.h, mm2 + 6 * j + 2 * (q + 1))
                    if j >= 1 or q >= 2:
                        e.wait_ge(s_v.h, v2 + 4 * (j - 1) + q + 2 if j >= 1 else v2)
                    e.activation(stg[pj][:, q * 512:(q + 1) * 512], pp[q % 2][:], AF.Silu,
                                 scale=gscol[:, j:j + 1]).then_inc(s_sc.h, 1)
                    s_sc.n += 1

        @b.vector
        def _(e):
            for j in range(NJ_F):
                pj = j % 2
                for q in range(3):
                    e.wait_ge(s_sc.h, sc2 + 3 * j + q + 1)
                    if q == 0:
                        e.wait_ge(s_sb[pj].h, sb2[pj] + 16 * (j // 2))
                    e.scalar_tensor_tensor(stgb[pj][:, q * 512:(q + 1) * 512], pp[2 + q % 2][:],
                                           uscol[:, j:j + 1], stg[pj][:, q * 512:(q + 1) * 512],
                                           ALU.mult, ALU.mult).then_inc(s_v.h, 2 if q == 2 else 1)
                    s_v.n += 2 if q == 2 else 1

        @b.sync
        def _(e):
            for j in range(NJ_F):
                pj = j % 2
                e.wait_ge(s_v.h, v2 + 4 * (j + 1))
                e.dma_start(h_scr[:, j, :], stgb[pj][:, 0:TH]).then_inc(s_sb[pj].h, 16)
                s_sb[pj].n += 16
                sb_marks[("h", j)] = s_sb[pj].n

    mm3 = s_mm.n
    v3 = s_v.n
    sb3 = [s_sb[0].n, s_sb[1].n]
    sg3 = [s_sg[0].n, s_sg[1].n]

    with nc.Block() as b:
        @b.tensor
        def _(e):
            e.wait_ge(s_cg.h, 16)
            e.wait_ge(s_v.h, v3)
            for q in range(4):
                mi = e.matmul(pp[q][:], ones1r[:], dsrow_sb[:, q * 512:(q + 1) * 512], start=True, stop=True)
            mi.then_inc(s_mm.h, 1)
            s_mm.n += 1

        @b.vector
        def _(e):
            e.wait_ge(s_mm.h, mm3 + 1)
            for q in range(4):
                ci = e.tensor_copy(dsb[:, q * 512:(q + 1) * 512], pp[q][:])
            ci.then_inc(s_ms.h, 1)
            s_ms.n += 1

    mm4 = s_mm.n
    ms4 = s_ms.n
    es1x.close()

    with nc.sbuf_tensor("dwh", [128, NJ_F * 1024], BF16) as dwh, \
         nc.sbuf_tensor("hb0", [128, NJ_F * 128], BF16) as hb0, \
         nc.sbuf_tensor("hb1", [128, NJ_F * 128], BF16) as hb1:
        hb = [hb0, hb1]
        with nc.Block() as b:
            @b.gpsimd
            def _(e):
                for dh_ in range(2):
                    if dh_ == 1:
                        e.wait_ge(s_mm.h, mm4 + 88 * NTH)
                    e.dma_start(dwh[:], dw_t[dh_]).then_inc(s_wb.h, 16)
                    s_wb.n += 16
                    wb_marks[("dw0", dh_)] = s_wb.n
                    for t in range(NTH):
                        it_ = dh_ * NTH + t
                        p = t % 2
                        if dh_ == 0 and t == 0:
                            e.wait_ge(s_sb[0].h, sb3[0])
                            e.wait_ge(s_sb[1].h, sb3[1])
                        if it_ >= 2:
                            e.wait_ge(s_mm.h, mm4 + 88 * (it_ - 1))
                        for q in range(2):
                            e.dma_start(
                                hb[p][:, q * 22 * 128:(q + 1) * 22 * 128].rearrange("p (i o) -> p i o", i=22),
                                h_scr[:, q * 22:(q + 1) * 22, t * 128:(t + 1) * 128],
                            ).then_inc(s_hb[p].h, 16)
                            s_hb[p].n += 16
                        hb_marks[("h0", it_)] = (p, s_hb[p].n)

            @b.tensor
            def _(e):
                e.wait_ge(s_ms.h, ms4)
                for dh_ in range(2):
                    for t in range(NTH):
                        it_ = dh_ * NTH + t
                        p, mark = hb_marks[("h0", it_)]
                        e.wait_ge(s_wb.h, wb_marks[("dw0", dh_)])
                        e.wait_ge(s_hb[p].h, mark)
                        pa, pb = (pp[0], pp[1]) if t % 2 == 0 else (pp[2], pp[3])
                        if it_ >= 2:
                            e.wait_ge(s_v.h, v3 + 2 * (it_ - 1))
                        for j in range(NJ_F):
                            st_, sp_ = (j == 0), (j == NJ_F - 1)
                            e.matmul(pa[:], hb[p][:, j * 128:(j + 1) * 128],
                                     dwh[:, j * 1024:j * 1024 + 512], start=st_, stop=sp_)
                            mi = e.matmul(pb[:], hb[p][:, j * 128:(j + 1) * 128],
                                          dwh[:, j * 1024 + 512:(j + 1) * 1024], start=st_, stop=sp_)
                            if sp_:
                                mi.then_inc(s_mm.h, 88)
                        s_mm.n += 88

            @b.vector
            def _(e):
                e.wait_ge(s_ms.h, ms4)
                for dh_ in range(2):
                    for t in range(NTH):
                        it_ = dh_ * NTH + t
                        p = it_ % 2
                        pa, pb = (pp[0], pp[1]) if t % 2 == 0 else (pp[2], pp[3])
                        e.wait_ge(s_mm.h, mm4 + 88 * (it_ + 1))
                        e.wait_ge(s_sg[p].h, sg3[p] + 16 * (it_ // 2))
                        e.tensor_tensor(stg[p][:, 0:512], pa[:], dsb[:, dh_ * 1024:dh_ * 1024 + 512], ALU.mult)
                        e.tensor_tensor(stg[p][:, 512:1024], pb[:], dsb[:, dh_ * 1024 + 512:(dh_ + 1) * 1024],
                                        ALU.mult).then_inc(s_v.h, 2)
                        s_v.n += 2

            @b.sync
            def _(e):
                for dh_ in range(2):
                    for t in range(NTH):
                        it_ = dh_ * NTH + t
                        p = it_ % 2
                        e.wait_ge(s_v.h, v3 + 2 * (it_ + 1))
                        e.dma_start(ffn0_scr[t * 128:(t + 1) * 128, dh_ * 1024:(dh_ + 1) * 1024],
                                    stg[p][:, 0:1024]).then_inc(s_sg[p].h, 16)
                        s_sg[p].n += 16
                        sg_marks[("f0", it_)] = s_sg[p].n

    es1.close()
    es1b.close()
    mm5 = s_mm.n
    v5 = s_v.n
    sc5 = s_sc.n

    es2 = ExitStack()
    apc = es2.enter_context(nc.sbuf_tensor("apc", [128, NIY * DST], F32))
    atmp = es2.enter_context(nc.sbuf_tensor("atmp", [128, NIY * DST], F32))
    cw = es2.enter_context(nc.sbuf_tensor("cw", [128, NIY * DCONV], F32))
    cbk = es2.enter_context(nc.sbuf_tensor("cbk", [128, NIY], F32))
    dpk = es2.enter_context(nc.sbuf_tensor("dpk", [128, NIY], F32))
    ebc = es2.enter_context(nc.sbuf_tensor("ebc_s", [16, DST * 128], BF16))
    scs = es2.enter_context(nc.sbuf_tensor("scs_s", [128, DST * 128], BF16))
    bs16 = [es2.enter_context(nc.sbuf_tensor(f"bs16_{k}", [16, TT], BF16)) for k in range(2)]
    xe = [es2.enter_context(nc.sbuf_tensor(f"xe{k}", [128, TT + 4], BF16)) for k in range(2)]
    u_sb2 = [es2.enter_context(nc.sbuf_tensor(f"u_sb{kk}", [128, TT], F32)) for kk in range(2)]
    du = es2.enter_context(nc.sbuf_tensor("du", [128, TT], F32))
    dls = [es2.enter_context(nc.sbuf_tensor(f"dls{k}", [128, TT], F32)) for k in range(2)]
    cms = [es2.enter_context(nc.sbuf_tensor(f"cms{k}", [128, TH], BF16)) for k in range(2)]
    zs = [es2.enter_context(nc.sbuf_tensor(f"zs{k}", [128, TH], BF16)) for k in range(2)]
    da = [es2.enter_context(nc.sbuf_tensor(f"da{k}", [128, TT], BF16)) for k in range(2)]
    db = [es2.enter_context(nc.sbuf_tensor(f"db{k}", [128, TT], BF16)) for k in range(2)]
    scn = [es2.enter_context(nc.sbuf_tensor(f"scn{k}", [128, TT], BF16)) for k in range(2)]
    csb = [es2.enter_context(nc.sbuf_tensor(f"csb{k}", [128, TH], BF16)) for k in range(2)]
    mb = [es2.enter_context(nc.sbuf_tensor(f"mb{k}", [128, TH], BF16)) for k in range(2)]
    szb = es2.enter_context(nc.sbuf_tensor("szb", [128, TH], BF16))
    t1b = es2.enter_context(nc.sbuf_tensor("t1b", [128, TH], BF16))
    y2st = [es2.enter_context(nc.sbuf_tensor(f"y2st{k}", [128, TH], BF16)) for k in range(2)]
    psA = es2.enter_context(nc.psum_tensor("psA", [128, TT], F32))
    py = es2.enter_context(nc.psum_tensor("py", [128, TH], F32))
    psC = es2.enter_context(nc.psum_tensor("psC", [128, 512], F32))

    vS = s_v.n
    mmS = s_mm.n
    scS = s_sc.n
    gmS = s_gm.n
    sgS = [s_sg[0].n, s_sg[1].n]
    NT2 = NIY
    pm = {}

    _v, _sc, _mm, _gm = vS, scS, mmS, gmS
    for k in range(NT2):
        _v += 4; pm[("conv", k)] = _v
        if k >= 1:
            _v += 1; pm[("t1b", k - 1)] = _v
            _v += 1; pm[("y2st", k - 1)] = _v
        _v += 1; pm[("du", k)] = _v
        for n in range(DST):
            _v += 4; pm[("db", k, n)] = _v
            _v += 1; pm[("scan", k, n)] = _v
        if k == NT2 - 1:
            _v += 1; pm[("t1b", k)] = _v
            _v += 1; pm[("y2st", k)] = _v
        _sc += 1; pm[("da", k, 0)] = _sc
        _sc += 1; pm[("da", k, 1)] = _sc
        _sc += 1; pm[("usilu", k)] = _sc
        for n in range(DST):
            for c in range(3):
                _sc += 1; pm[("ev", k, n, c)] = _sc
            if n == 7:
                _sc += 1; pm[("zsilu", k)] = _sc
            if n + 2 < DST:
                _sc += 1; pm[("da", k, n + 2)] = _sc
        for n in range(DST):
            for q in range(4):
                _mm += 1; pm[("Bq", k, n, q)] = _mm
            pm[("B", k, n)] = _mm
            for c in range(3):
                _mm += 1; pm[("C", k, n, c)] = _mm
            if n >= 1:
                _mm += 1; pm[("py", k, n - 1)] = _mm
        _mm += 1; pm[("py", k, DST - 1)] = _mm
        for n in range(DST):
            _gm += 1; pm[("mb", k, n)] = _gm

    with nc.Block() as b:
        @b.sync
        def _(e):
            e.wait_ge(s_v.h, vS)
            e.wait_ge(s_sg[0].h, sgS[0])
            e.wait_ge(s_sg[1].h, sgS[1])
            e.dma_start(atmp[:], alog_pc).then_inc(s_c.h, 16)
            e.dma_start(cw[:], cw_pc).then_inc(s_c.h, 16)
            e.dma_start(cbk[:], cb_pc).then_inc(s_c.h, 16)
            e.dma_start(dpk[:], dp_pc).then_inc(s_c.h, 16)
            e.dma_start(ebc[:].rearrange("p (j o) -> p j o", j=DST), ebc_in.rearrange("j p o -> p j o")).then_inc(s_c.h, 16)
            e.dma_start(scs[:].rearrange("p (j o) -> p j o", j=DST), scs_in.rearrange("j p o -> p j o")).then_inc(s_c.h, 16)
            s_c.n += 96
            for k in range(NT2):
                p = k % 2
                if k >= 2:
                    e.wait_ge(s_sc.h, pm[("da", k - 2, DST - 1)])
                    e.wait_ge(s_v.h, pm[("du", k - 2)])
                    e.wait_ge(s_mm.h, pm[("C", k - 2, DST - 1, 2)])
                e.dma_start(dls[p][:], dl_scr[k * 128:(k + 1) * 128, :]).then_inc(s_dl[p].h, 16)
                s_dl[p].n += 16
                dl_marks[k] = s_dl[p].n
                e.dma_start(cms[p][:], zx_scr[OFF_C + k * 128:OFF_C + (k + 1) * 128, HOFF:TT]).then_inc(s_cz[p].h, 16)
                e.dma_start(zs[p][:], zx_scr[OFF_Z + k * 128:OFF_Z + (k + 1) * 128, HOFF:TT]).then_inc(s_cz[p].h, 16)
                s_cz[p].n += 32
                cz_marks[k] = s_cz[p].n
                for g in range(8):
                    e.dma_start(xe[p][16 * g:16 * (g + 1), 3:3 + TT],
                                zx_scr[OFF_X + k * 16:OFF_X + (k + 1) * 16, :]).then_inc(s_xe[p].h, 16)
                s_xe[p].n += 128
                cz_marks[("xe", k)] = s_xe[p].n

        @b.gpsimd
        def _(e):
            e.wait_ge(s_v.h, vS)
            e.wait_ge(s_sg[0].h, sgS[0])
            e.wait_ge(s_sg[1].h, sgS[1])
            e.memset(xe[0][:, 0:3], 0.0).then_inc(s_pad.h, 1)
            e.memset(xe[1][:, 0:3], 0.0).then_inc(s_pad.h, 1)
            s_pad.n += 2
            for k in range(NT2):
                p = k % 2
                if k >= 2:
                    e.wait_ge(s_mm.h, pm[("B", k - 2, DST - 1)])
                e.dma_start(bs16[p][:], zx_scr[OFF_B + k * 16:OFF_B + (k + 1) * 16, :]).then_inc(s_wq[p].h, 16)
                s_wq[p].n += 16
                wq_marks[("bs", k)] = s_wq[p].n
                if k >= 1:
                    kk = k - 1
                    e.wait_ge(s_v.h, pm[("y2st", kk)])
                    e.dma_start(y2_scr[:, kk, :], y2st[kk % 2][:]).then_inc(s_go[kk % 2].h, 16)
                    s_go[kk % 2].n += 16
                    go_marks[("y2", kk)] = s_go[kk % 2].n
                for n in range(DST):
                    e.wait_ge(s_sc.h, pm[("ev", k, n, 2)])
                    e.wait_ge(s_v.h, pm[("scan", k, n)])
                    if (k, n) != (0, 0) and (k, n) != (0, 1):
                        kk2, nn2 = (k, n - 2) if n >= 2 else (k - 1, DST - 2 + n)
                        e.wait_ge(s_mm.h, pm[("py", kk2, nn2)])
                    e.tensor_tensor(mb[n % 2][:], csb[n % 2][:], scn[n % 2][:, HOFF:TT],
                                    ALU.mult).then_inc(s_gm.h, 1)
                    s_gm.n += 1
            kk = NT2 - 1
            e.wait_ge(s_v.h, pm[("y2st", kk)])
            e.dma_start(y2_scr[:, kk, :], y2st[kk % 2][:]).then_inc(s_go[kk % 2].h, 16)
            s_go[kk % 2].n += 16
            go_marks[("y2", kk)] = s_go[kk % 2].n

        @b.scalar
        def _(e):
            e.wait_ge(s_c.h, s_c.n)
            e.activation(apc[:], atmp[:], AF.Exp).then_inc(s_ax1.h, 1)
            s_ax1.n += 1
            e.wait_ge(s_ax2.h, 1)
            for k in range(NT2):
                p = k % 2
                for n in range(2):
                    e.wait_ge(s_dl[p].h, dl_marks[k])
                    if k >= 1:
                        e.wait_ge(s_v.h, pm[("scan", k - 1, DST - 2 + n)])
                    e.activation(da[n % 2][:], dls[p][:], AF.Exp,
                                 scale=apc[:, k * DST + n:k * DST + n + 1]).then_inc(s_sc.h, 1)
                    s_sc.n += 1
                e.wait_ge(s_v.h, pm[("conv", k)])
                e.activation(u_sb2[k % 2][:], u_sb2[k % 2][:], AF.Silu, bias=cbk[:, k:k + 1]).then_inc(s_sc.h, 1)
                s_sc.n += 1
                for n in range(DST):
                    for c in range(3):
                        e.wait_ge(s_mm.h, pm[("C", k, n, c)])
                        if n >= 2 and c == 0:
                            e.wait_ge(s_gm.h, pm[("mb", k, n - 2)])
                        elif k >= 1 and n < 2 and c == 0:
                            e.wait_ge(s_gm.h, pm[("mb", k - 1, DST - 2 + n)])
                        e.activation(csb[n % 2][:, c * 512:(c + 1) * 512], psC[:],
                                     AF.Copy).then_inc(s_sc.h, 1)
                        s_sc.n += 1
                    if n == 7:
                        e.wait_ge(s_cz[p].h, cz_marks[k])
                        if k >= 1:
                            e.wait_ge(s_v.h, pm[("y2st", k - 1)])
                        e.activation(szb[:], zs[p][:], AF.Silu).then_inc(s_sc.h, 1)
                        s_sc.n += 1
                    if n + 2 < DST:
                        e.wait_ge(s_v.h, pm[("scan", k, n)])
                        e.activation(da[n % 2][:], dls[p][:], AF.Exp,
                                     scale=apc[:, k * DST + n + 2:k * DST + n + 3]).then_inc(s_sc.h, 1)
                        s_sc.n += 1

        @b.vector
        def _(e):
            e.wait_ge(s_ax1.h, 1)
            e.tensor_scalar_mul(apc[:], apc[:], -1.0).then_inc(s_ax2.h, 1)
            s_ax2.n += 1
            def _tail(kk):
                pk = kk % 2
                e.wait_ge(s_mm.h, pm[("py", kk, DST - 1)])
                e.scalar_tensor_tensor(t1b[:], u_sb2[kk % 2][:, HOFF:TT], dpk[:, kk:kk + 1], py[:],
                                       ALU.mult, ALU.add).then_inc(s_v.h, 1)
                e.wait_ge(s_sc.h, pm[("zsilu", kk)])
                if kk >= 2:
                    e.wait_ge(s_go[pk].h, go_marks[("y2", kk - 2)])
                e.tensor_tensor(y2st[pk][:], t1b[:], szb[:], ALU.mult).then_inc(s_v.h, 1)
                s_v.n += 2

            for k in range(NT2):
                p = k % 2
                u_sb = u_sb2[k % 2]
                e.wait_ge(s_xe[p].h, cz_marks[("xe", k)])
                e.wait_ge(s_pad.h, 3)
                e.tensor_scalar(u_sb[:], xe[p][:, 3:3 + TT], cw[:, k * 4 + 3:k * 4 + 4], None,
                                ALU.mult)
                for q in range(3):
                    ci = e.scalar_tensor_tensor(u_sb[:], xe[p][:, q:q + TT], cw[:, k * 4 + q:k * 4 + q + 1],
                                                u_sb[:], ALU.mult, ALU.add)
                    ci.then_inc(s_v.h, 2 if q == 2 else 1)
                s_v.n += 4
                if k >= 1:
                    _tail(k - 1)
                e.wait_ge(s_sc.h, pm[("usilu", k)])
                e.wait_ge(s_dl[p].h, dl_marks[k])
                e.tensor_tensor(du[:], dls[p][:], u_sb[:], ALU.mult).then_inc(s_v.h, 1)
                s_v.n += 1
                for n in range(DST):
                    for q in range(4):
                        e.wait_ge(s_mm.h, pm[("Bq", k, n, q)])
                        e.tensor_tensor(db[n % 2][:, q * 512:(q + 1) * 512],
                                        psA[:, q * 512:(q + 1) * 512],
                                        du[:, q * 512:(q + 1) * 512], ALU.mult).then_inc(s_v.h, 1)
                        s_v.n += 1
                    e.wait_ge(s_sc.h, pm[("da", k, n)])
                    if n >= 2 or k >= 1:
                        kk2, nn2 = (k, n - 2) if n >= 2 else (k - 1, DST - 2 + n)
                        e.wait_ge(s_gm.h, pm[("mb", kk2, nn2)])
                    e.tensor_tensor_scan(scn[n % 2][:], da[n % 2][:], db[n % 2][:], 0.0,
                                         ALU.mult, ALU.add).then_inc(s_v.h, 1)
                    s_v.n += 1
            _tail(NT2 - 1)

        @b.tensor
        def _(e):
            e.wait_ge(s_c.h, s_c.n)
            e.wait_ge(s_v.h, vS)
            for k in range(NT2):
                p = k % 2
                for n in range(DST):
                    if n == 0:
                        e.wait_ge(s_wq[p].h, wq_marks[("bs", k)])
                        if k >= 1:
                            e.wait_ge(s_v.h, pm[("db", k - 1, DST - 1)])
                    else:
                        e.wait_ge(s_v.h, pm[("db", k, n - 1)])
                    for q in range(4):
                        mi = e.matmul(psA[:, q * 512:(q + 1) * 512], ebc[:, n * 128:(n + 1) * 128],
                                      bs16[p][:, q * 512:(q + 1) * 512], start=True, stop=True)
                        mi.then_inc(s_mm.h, 1)
                        s_mm.n += 1
                    if n == 0:
                        e.wait_ge(s_cz[p].h, cz_marks[k])
                    for c in range(3):
                        if (k, n, c) != (0, 0, 0):
                            kk2, nn2, cc2 = (k, n, c - 1) if c >= 1 else \
                                ((k, n - 1, 2) if n >= 1 else (k - 1, DST - 1, 2))
                            e.wait_ge(s_sc.h, pm[("ev", kk2, nn2, cc2)])
                        mi = e.matmul(psC[:], scs[:, n * 128:(n + 1) * 128],
                                      cms[p][:, c * 512:(c + 1) * 512], start=True, stop=True)
                        mi.then_inc(s_mm.h, 1)
                        s_mm.n += 1
                    if n >= 1:
                        nn = n - 1
                        e.wait_ge(s_gm.h, pm[("mb", k, nn)])
                        if nn == 0 and k >= 1:
                            e.wait_ge(s_v.h, pm[("t1b", k - 1)])
                        for q in range(3):
                            mi = e.matmul(py[:, q * 512:(q + 1) * 512], ident[:],
                                          mb[nn % 2][:, q * 512:(q + 1) * 512],
                                          start=(nn == 0), stop=(nn == DST - 1))
                        mi.then_inc(s_mm.h, 1)
                        s_mm.n += 1
                nn = DST - 1
                e.wait_ge(s_gm.h, pm[("mb", k, nn)])
                for q in range(3):
                    mi = e.matmul(py[:, q * 512:(q + 1) * 512], ident[:],
                                  mb[nn % 2][:, q * 512:(q + 1) * 512],
                                  start=(nn == 0), stop=(nn == DST - 1))
                mi.then_inc(s_mm.h, 1)
                s_mm.n += 1

    assert s_v.n == _v and s_sc.n == _sc and s_mm.n == _mm and s_gm.n == _gm, \
        (s_v.n, _v, s_sc.n, _sc, s_mm.n, _mm, s_gm.n, _gm)
    es2.close()
    vP2 = s_v.n
    mmP2 = s_mm.n

    es3 = ExitStack()
    stg3 = [es3.enter_context(nc.sbuf_tensor(f"stg3{k}", [128, 1024], F32)) for k in range(2)]
    stq3 = [es3.enter_context(nc.sbuf_tensor(f"stq3{k}", [128, TH], BF16)) for k in range(2)]
    xts = [es3.enter_context(nc.sbuf_tensor(f"xts3{k}", [128, 1024], F32)) for k in range(2)]
    fts = [es3.enter_context(nc.sbuf_tensor(f"fts3{k}", [128, 1024], F32)) for k in range(2)]
    ptp3 = [es3.enter_context(nc.psum_tensor(f"ptp3{k}", [128, 1024], BF16)) for k in range(2)]
    pp3 = [es3.enter_context(nc.psum_tensor(f"pp3{k}", [128, 512], F32)) for k in range(4)]

    vW = s_v.n
    mmW = s_mm.n

    es3a = ExitStack()
    woh = es3a.enter_context(nc.sbuf_tensor("woh", [128, NIY * 1024], BF16))
    y2b = [es3a.enter_context(nc.sbuf_tensor(f"y2b{k}", [128, NIY * 128], BF16)) for k in range(2)]
    with nc.Block() as b:
        @b.gpsimd
        def _(e):
            e.wait_ge(s_v.h, vW)
            for dh_ in range(2):
                if dh_ == 1:
                    e.wait_ge(s_mm.h, mmW + NTH)
                e.dma_start(woh[:], wo_t[dh_]).then_inc(s_wb.h, 16)
                s_wb.n += 16
                wb_marks[("wo", dh_)] = s_wb.n
                for t in range(NTH):
                    it_ = dh_ * NTH + t
                    p = t % 2
                    if dh_ == 0 and t == 0:
                        e.wait_ge(s_go[0].h, s_go[0].n)
                        e.wait_ge(s_go[1].h, s_go[1].n)
                    if it_ >= 2:
                        e.wait_ge(s_mm.h, mmW + (it_ - 1))
                    for q in range(2):
                        e.dma_start(
                            y2b[p][:, q * 16 * 128:(q + 1) * 16 * 128].rearrange("p (i o) -> p i o", i=16),
                            y2_scr[:, q * 16:(q + 1) * 16, t * 128:(t + 1) * 128],
                        ).then_inc(s_hb[p].h, 16)
                        s_hb[p].n += 16
                    hb_marks[("y2", it_)] = (p, s_hb[p].n)

        @b.sync
        def _(e):
            e.wait_ge(s_v.h, vW)
            for dh_ in range(2):
                for t in range(NTH):
                    it_ = dh_ * NTH + t
                    p = it_ % 2
                    if it_ >= 2:
                        e.wait_ge(s_v.h, vW + 2 * (it_ - 1))
                    e.dma_start(xts[p][:], x_in[HOFF + t * 128:HOFF + (t + 1) * 128,
                                                dh_ * 1024:(dh_ + 1) * 1024]).then_inc(s_x[p].h, 16)
                    e.dma_start(fts[p][:], ffn0_scr[t * 128:(t + 1) * 128,
                                                    dh_ * 1024:(dh_ + 1) * 1024]).then_inc(s_x[p].h, 16)
                    s_x[p].n += 32
                    x_marks[("x1", it_)] = s_x[p].n

        @b.tensor
        def _(e):
            e.wait_ge(s_v.h, vW)
            for dh_ in range(2):
                e.wait_ge(s_wb.h, wb_marks[("wo", dh_)])
                for t in range(NTH):
                    it_ = dh_ * NTH + t
                    pq, mark = hb_marks[("y2", it_)]
                    e.wait_ge(s_hb[pq].h, mark)
                    pa, pb = (pp3[0], pp3[1]) if it_ % 2 == 0 else (pp3[2], pp3[3])
                    if it_ >= 2:
                        e.wait_ge(s_v.h, vW + 2 * (it_ - 1))
                    for i in range(NIY):
                        st_, sp_ = (i == 0), (i == NIY - 1)
                        e.matmul(pa[:], y2b[pq][:, i * 128:(i + 1) * 128],
                                 woh[:, i * 1024:i * 1024 + 512], start=st_, stop=sp_)
                        mi = e.matmul(pb[:], y2b[pq][:, i * 128:(i + 1) * 128],
                                      woh[:, i * 1024 + 512:(i + 1) * 1024], start=st_, stop=sp_)
                        if sp_:
                            mi.then_inc(s_mm.h, 1)
                    s_mm.n += 1

        @b.vector
        def _(e):
            for dh_ in range(2):
                for t in range(NTH):
                    it_ = dh_ * NTH + t
                    p = it_ % 2
                    pa, pb = (pp3[0], pp3[1]) if p == 0 else (pp3[2], pp3[3])
                    e.wait_ge(s_x[p].h, x_marks[("x1", it_)])
                    e.wait_ge(s_mm.h, mmW + it_ + 1)
                    if it_ >= 2:
                        e.wait_ge(s_sg[p].h, sgS[p] + 16 * (it_ // 2))
                    e.tensor_tensor(stg3[p][:, 0:512], xts[p][:, 0:512], pa[:], ALU.add)
                    e.tensor_tensor(stg3[p][:, 512:1024], xts[p][:, 512:1024], pb[:], ALU.add).then_inc(s_v.h, 1)
                    e.wait_ge(s_v.h, vW + 2 * it_ + 1)
                    e.tensor_tensor(stg3[p][:], stg3[p][:], fts[p][:], ALU.add).then_inc(s_v.h, 1)
                    s_v.n += 2

        @b.scalar
        def _(e):
            for dh_ in range(2):
                for t in range(NTH):
                    it_ = dh_ * NTH + t
                    p = it_ % 2
                    e.wait_ge(s_v.h, vW + 2 * (it_ + 1))
                    e.dma_start(x1_scr[t * 128:(t + 1) * 128, dh_ * 1024:(dh_ + 1) * 1024],
                                stg3[p][:]).then_inc(s_sg[p].h, 16)
                    s_sg[p].n += 16
                    sg_marks[("x1", it_)] = s_sg[p].n

    es3a.close()
    vL = s_v.n
    mmL = s_mm.n
    scL = s_sc.n

    es3b = ExitStack()
    xn1Tall = es3b.enter_context(nc.sbuf_tensor("xn1Tall", [128, NDC * TH], BF16))
    xn1T = [xn1Tall[:, j * TH:(j + 1) * TH] for j in range(NDC)]
    x1t = [es3b.enter_context(nc.sbuf_tensor(f"x1t{k}", [128, D], F32)) for k in range(2)]
    sq1 = [es3b.enter_context(nc.sbuf_tensor(f"sq1_{k}", [128, D], F32)) for k in range(2)]
    xnb1 = [es3b.enter_context(nc.sbuf_tensor(f"xnb1_{k}", [128, D], BF16)) for k in range(2)]
    with nc.Block() as b:
        @b.sync
        def _(e):
            for i in range(NTH):
                p = i % 2
                e.wait_ge(s_sg[i % 2].h, sg_marks[("x1", i)])
                e.wait_ge(s_sg[(NTH + i) % 2].h, sg_marks[("x1", NTH + i)])
                if i < 2:
                    e.wait_ge(s_v.h, vL)
                if i >= 2:
                    e.wait_ge(s_v.h, vL + 22 * (i - 1))
                    e.wait_ge(s_sc.h, scL + 3 * (i - 1))
                e.dma_start(x1t[p][:], x1_scr[i * 128:(i + 1) * 128, :]).then_inc(s_x[p].h, 16)
                s_x[p].n += 16
                x_marks[("ln1", i)] = s_x[p].n

        @b.scalar
        def _(e):
            for i in range(NTH):
                p = i % 2
                e.wait_ge(s_x[p].h, x_marks[("ln1", i)])
                e.square(sq1[p][:], x1t[p][:]).then_inc(s_sc.h, 1)
                e.wait_ge(s_v.h, vL + 22 * i + 5)
                e.activation(scr1[:, i:i + 1], scr2[:, i:i + 1], AF.Ln, bias=konst[:, 0:1]).then_inc(s_sc.h, 1)
                e.wait_ge(s_sc.h, scL + 3 * i + 2)
                e.activation(rcol[:, i:i + 1], scr1[:, i:i + 1], AF.Exp, scale=konst[:, 1:2]).then_inc(s_sc.h, 1)
                s_sc.n += 3

        @b.vector
        def _(e):
            for i in range(NTH):
                p = i % 2
                e.wait_ge(s_x[p].h, x_marks[("ln1", i)])
                e.reduce_sum(scr1[:, i:i + 1], x1t[p][:], axis=AX.X).then_inc(s_v.h, 1)
                e.wait_ge(s_v.h, vL + 22 * i + 1)
                e.tensor_scalar_mul(mcol[:, i:i + 1], scr1[:, i:i + 1], 1.0 / D).then_inc(s_v.h, 1)
                e.wait_ge(s_sc.h, scL + 3 * i + 1)
                e.wait_ge(s_v.h, vL + 22 * i + 2)
                e.reduce_sum(scr2[:, i:i + 1], sq1[p][:], axis=AX.X).then_inc(s_v.h, 1)
                e.wait_ge(s_v.h, vL + 22 * i + 3)
                e.tensor_scalar_mul(scr2[:, i:i + 1], scr2[:, i:i + 1], 1.0 / D)
                e.tensor_mul(scr1[:, i:i + 1], mcol[:, i:i + 1], mcol[:, i:i + 1]).then_inc(s_v.h, 1)
                e.wait_ge(s_v.h, vL + 22 * i + 4)
                e.tensor_sub(scr2[:, i:i + 1], scr2[:, i:i + 1], scr1[:, i:i + 1]).then_inc(s_v.h, 1)
                e.wait_ge(s_sc.h, scL + 3 * i + 3)
                if i >= 2:
                    e.wait_ge(s_mm.h, mmL + 16 * (i - 1))
                e.tensor_scalar(xnb1[p][:], x1t[p][:], mcol[:, i:i + 1], rcol[:, i:i + 1],
                                ALU.subtract, ALU.mult).then_inc(s_v.h, 1)
                s_v.n += 6
                for j in range(NDC):
                    e.wait_ge(s_mm.h, mmL + 16 * i + j + 1)
                    e.tensor_copy(xn1T[j][:, i * 128:(i + 1) * 128],
                                  ptp3[(16 * i + j) % 2][:, 0:128]).then_inc(s_v.h, 1)
                    s_v.n += 1

        @b.tensor
        def _(e):
            for i in range(NTH):
                e.wait_ge(s_v.h, vL + 22 * i + 6)
                for j in range(NDC):
                    k = 16 * i + j
                    if k >= 2:
                        i2, j2 = divmod(k - 2, 16)
                        e.wait_ge(s_v.h, vL + 22 * i2 + 6 + j2 + 1)
                    e.transpose(ptp3[k % 2][:, 0:128], xnb1[i % 2][:, j * 128:(j + 1) * 128],
                                ident[:]).then_inc(s_mm.h, 1)
                    s_mm.n += 1

        @b.gpsimd
        def _(e):
            for j in range(NDC):
                e.wait_ge(s_v.h, vL + 22 * (NTH - 1) + 7 + (j + 1) * 0 + 22 * 0)
                e.wait_ge(s_v.h, vL + 22 * NTH)
                e.dma_start(xn1_scr[:, j * TH:(j + 1) * TH], xn1T[j][:]).then_inc(s_go[j % 2].h, 16)
                s_go[j % 2].n += 16
                go_marks[("xn1", j)] = s_go[j % 2].n

    es3b.close()

    es3c = ExitStack()
    xn1Tb = es3c.enter_context(nc.sbuf_tensor("xn1Tb", [128, NDC * TH], BF16))
    wvs = es3c.enter_context(nc.sbuf_tensor("wvs", [128, NDC * D], BF16))
    xq1 = [xn1Tb[:, j * TH:(j + 1) * TH] for j in range(NDC)]
    vQ = s_v.n
    mmQ = s_mm.n
    with nc.Block() as b:
        @b.sync
        def _(e):
            e.wait_ge(s_go[0].h, s_go[0].n)
            e.wait_ge(s_go[1].h, s_go[1].n)
            e.dma_start(xn1Tb[:], xn1_scr[:, :]).then_inc(s_c.h, 16)
            s_c.n += 16
            cQ = s_c.n

        @b.gpsimd
        def _(e):
            e.wait_ge(s_v.h, vQ)
            e.dma_start(wvs[:], wv_t).then_inc(s_wb.h, 16)
            s_wb.n += 16
            wb_marks["wv"] = s_wb.n
            for j in range(2 * NDC):
                p = j % 2
                if j >= 2:
                    e.wait_ge(s_mm.h, mmQ + 3 * (j - 1))
                e.dma_start(wbuf[p][:], wqk_t[j]).then_inc(s_wq[p].h, 16)
                s_wq[p].n += 16
                wq_marks[("qk", j)] = (p, s_wq[p].n)

        @b.tensor
        def _(e):
            e.wait_ge(s_c.h, s_c.n)
            e.wait_ge(s_v.h, vQ)
            for j in range(2 * NDC):
                p, mark = wq_marks[("qk", j)]
                e.wait_ge(s_wq[p].h, mark)
                pa, pb = (pp3[0], pp3[1]) if j % 2 == 0 else (pp3[2], pp3[3])
                if j >= 2:
                    e.wait_ge(s_v.h, vQ + 3 * (j - 1))
                nsl_ = 3
                for q in range(nsl_):
                    dst = pa if q % 2 == 0 else pb
                    if q == 2:
                        e.wait_ge(s_v.h, vQ + 3 * j + 1)
                    for i in range(NDC):
                        st_, sp_ = (i == 0), (i == NDC - 1)
                        mi = e.matmul(dst[:], wbuf[p][:, i * 128:(i + 1) * 128],
                                      xq1[i][:, q * 512:(q + 1) * 512], start=st_, stop=sp_)
                        if sp_:
                            mi.then_inc(s_mm.h, 1)
                    s_mm.n += 1
            e.wait_ge(s_wb.h, wb_marks["wv"])
            for t in range(NTH):
                for vh in range(2):
                    it_ = 2 * NDC * 3 + t * 2 + vh
                    pa, pb = (pp3[0], pp3[1]) if it_ % 2 == 0 else (pp3[2], pp3[3])
                    e.wait_ge(s_v.h, vQ + it_ - 1)
                    for i in range(NDC):
                        st_, sp_ = (i == 0), (i == NDC - 1)
                        e.matmul(pa[:], xq1[i][:, t * 128:(t + 1) * 128],
                                 wvs[:, i * D + vh * 1024:i * D + vh * 1024 + 512], start=st_, stop=sp_)
                        mi = e.matmul(pb[:], xq1[i][:, t * 128:(t + 1) * 128],
                                      wvs[:, i * D + vh * 1024 + 512:i * D + (vh + 1) * 1024],
                                      start=st_, stop=sp_)
                        if sp_:
                            mi.then_inc(s_mm.h, 2)
                    s_mm.n += 2

        @b.vector
        def _(e):
            for j in range(2 * NDC):
                p = j % 2
                pa, pb = (pp3[0], pp3[1]) if p == 0 else (pp3[2], pp3[3])
                for q in range(3):
                    src = pa if q % 2 == 0 else pb
                    e.wait_ge(s_mm.h, mmQ + 3 * j + q + 1)
                    if q == 0 and j >= 2:
                        e.wait_ge(s_sb[p].h, sb3[p] + 16 * (j // 2))
                    if j < NDC:
                        e.tensor_scalar_mul(stq3[p][:, q * 512:(q + 1) * 512], src[:], SCALE).then_inc(s_v.h, 1)
                    else:
                        e.tensor_copy(stq3[p][:, q * 512:(q + 1) * 512], src[:]).then_inc(s_v.h, 1)
                    s_v.n += 1
            base = mmQ + 6 * NDC
            for t in range(NTH):
                for vh in range(2):
                    it_ = 2 * NDC * 3 + t * 2 + vh
                    p = it_ % 2
                    pa, pb = (pp3[0], pp3[1]) if p == 0 else (pp3[2], pp3[3])
                    e.wait_ge(s_mm.h, base + 2 * (t * 2 + vh + 1))
                    e.wait_ge(s_sb[p].h, sb3[p] + 16 * ((2 * NDC + t * 2 + vh) // 2))
                    e.tensor_copy(stq3[p][:, 0:512], pa[:])
                    e.tensor_copy(stq3[p][:, 512:1024], pb[:]).then_inc(s_v.h, 1)
                    s_v.n += 1

        @b.scalar
        def _(e):
            for j in range(2 * NDC):
                p = j % 2
                e.wait_ge(s_v.h, vQ + 3 * (j + 1))
                if j < NDC:
                    e.dma_start(q_scr[:, j * TC:(j + 1) * TC], stq3[p][:, HOFF:TH]).then_inc(s_sb[p].h, 16)
                else:
                    jk = j - NDC
                    e.dma_start(k_scr[:, jk * TH:(jk + 1) * TH], stq3[p][:, 0:TH]).then_inc(s_sb[p].h, 16)
                s_sb[p].n += 16
                sb_marks[("qk", j)] = s_sb[p].n
            for t in range(NTH):
                for vh in range(2):
                    it_ = t * 2 + vh
                    p = (2 * NDC * 3 + it_) % 2
                    e.wait_ge(s_v.h, vQ + 6 * NDC + it_ + 1)
                    e.dma_start(v_scr[:, t * D + vh * 1024:t * D + (vh + 1) * 1024],
                                stq3[p][:, 0:1024]).then_inc(s_sb[p].h, 16)
                    s_sb[p].n += 16
                    sb_marks[("v", it_)] = s_sb[p].n

    sbQ = [s_sb[0].n, s_sb[1].n]
    es3c.close()
    es3.close()
    vP3 = s_v.n
    mmP3 = s_mm.n
    scP3 = s_sc.n

    es4 = ExitStack()
    Rb = es4.enter_context(nc.sbuf_tensor("Rb", [1, TC], BF16))
    R2c = es4.enter_context(nc.sbuf_tensor("R2c", [128, 8], F32))
    mR2c = es4.enter_context(nc.sbuf_tensor("mR2c", [128, 8], F32))
    gscol1 = es4.enter_context(nc.sbuf_tensor("gscol1", [128, NJ1], F32))
    uscol1 = es4.enter_context(nc.sbuf_tensor("uscol1", [128, NJ1], F32))
    ds1row_sb = es4.enter_context(nc.sbuf_tensor("ds1row_sb", [1, D], BF16))
    sg4 = [es4.enter_context(nc.sbuf_tensor(f"sg4_{k}", [128, TC], BF16)) for k in range(2)]
    st4 = [es4.enter_context(nc.sbuf_tensor(f"st4_{k}", [128, TC], BF16)) for k in range(2)]
    h_all = es4.enter_context(nc.sbuf_tensor("h_allt", [128, NJ1 * TC], BF16))
    hav = [h_all[:, j * TC:(j + 1) * TC] for j in range(NJ1)]
    ppf = [es4.enter_context(nc.psum_tensor(f"ppf{k}", [128, 512], F32)) for k in range(6)]
    pRc = es4.enter_context(nc.psum_tensor("pRc", [128, 1], F32))

    esB4 = ExitStack()
    xn1To = esB4.enter_context(nc.sbuf_tensor("xn1To", [128, NDC * TC], BF16))
    wbg = [esB4.enter_context(nc.sbuf_tensor(f"wbg{k}", [128, NDC * 128], BF16)) for k in range(2)]
    wbu = [esB4.enter_context(nc.sbuf_tensor(f"wbu{k}", [128, NDC * 128], BF16)) for k in range(2)]

    vF = s_v.n
    mmF = s_mm.n
    scF = s_sc.n
    msF = s_ms.n

    with nc.Block() as b:
        @b.sync
        def _(e):
            e.wait_ge(s_v.h, vF)
            e.wait_ge(s_sb[0].h, sbQ[0])
            e.wait_ge(s_sb[1].h, sbQ[1])
            e.dma_start(xn1To[:].rearrange("p (i o) -> p i o", i=NDC),
                        xn1_scr.rearrange("p (i o) -> p i o", i=NDC)[:, :, HOFF:TH]).then_inc(s_c.h, 16)
            e.dma_start(gscol1[:], gs1_pc).then_inc(s_c.h, 16)
            e.dma_start(uscol1[:], us1_pc).then_inc(s_c.h, 16)
            s_c.n += 48
            c4 = s_c.n

        @b.gpsimd
        def _(e):
            e.wait_ge(s_v.h, vF)
            e.wait_ge(s_sb[0].h, sbQ[0])
            e.wait_ge(s_sb[1].h, sbQ[1])
            e.dma_start(ds1row_sb[:], ds1_row).then_inc(s_cg.h, 16)
            s_cg.n += 16
            for j in range(NJ1):
                pj = j % 2
                if j >= 2:
                    e.wait_ge(s_mm.h, mmF + 1 + (j - 1))
                e.dma_start(wbg[pj][:], g1_t[j]).then_inc(s_wq[pj].h, 16)
                s_wq[pj].n += 16
                wq_marks[("g1", j)] = s_wq[pj].n
                e.dma_start(wbu[pj][:], u1_t[j]).then_inc(s_uq[pj].h, 16)
                s_uq[pj].n += 16
                uq_marks[("u1", j)] = s_uq[pj].n

        @b.tensor
        def _(e):
            e.wait_ge(s_c.h, s_c.n)
            e.wait_ge(s_v.h, vF)
            for i in range(NDC):
                st_, sp_ = (i == 0), (i == NDC - 1)
                e.matmul(ppf[4][0:1, :], onescol[:], xn1To[:, i * TC:i * TC + 512], start=st_, stop=sp_)
                mi = e.matmul(ppf[5][0:1, :], onescol[:], xn1To[:, i * TC + 512:(i + 1) * TC], start=st_, stop=sp_)
                if sp_:
                    mi.then_inc(s_mm.h, 1)
            s_mm.n += 1
            for j in range(NJ1):
                pj = j % 2
                e.wait_ge(s_wq[pj].h, wq_marks[("g1", j)])
                e.wait_ge(s_uq[pj].h, uq_marks[("u1", j)])
                e.wait_ge(s_ms.h, msF + 1)
                ga, gb = ppf[2 * pj], ppf[2 * pj + 1]
                if j >= 2:
                    e.wait_ge(s_sc.h, scF + 2 * (j - 1))
                if j >= 1:
                    e.wait_ge(s_v.h, vF + 1 + 2 * j)
                for i in range(NDC):
                    st_ = (i == 0)
                    e.matmul(ga[:], wbg[pj][:, i * 128:(i + 1) * 128], xn1To[:, i * TC:i * TC + 512],
                             start=st_, stop=False)
                    e.matmul(gb[:], wbg[pj][:, i * 128:(i + 1) * 128], xn1To[:, i * TC + 512:(i + 1) * TC],
                             start=st_, stop=False)
                    e.matmul(ppf[4][:], wbu[pj][:, i * 128:(i + 1) * 128], xn1To[:, i * TC:i * TC + 512],
                             start=st_, stop=False)
                    e.matmul(ppf[5][:], wbu[pj][:, i * 128:(i + 1) * 128], xn1To[:, i * TC + 512:(i + 1) * TC],
                             start=st_, stop=False)
                e.matmul(ga[:], negones[:], Rb[:, 0:512], start=False, stop=True)
                e.matmul(gb[:], negones[:], Rb[:, 512:1024], start=False, stop=True)
                e.matmul(ppf[4][:], negones[:], Rb[:, 0:512], start=False, stop=True)
                mi = e.matmul(ppf[5][:], negones[:], Rb[:, 512:1024], start=False, stop=True)
                mi.then_inc(s_mm.h, 1)
                s_mm.n += 1

        @b.vector
        def _(e):
            e.wait_ge(s_mm.h, mmF + 1)
            e.tensor_copy(Rb[:, 0:512], ppf[4][0:1, :])
            e.tensor_copy(Rb[:, 512:1024], ppf[5][0:1, :]).then_inc(s_ms.h, 1)
            s_ms.n += 1
            e.nop().then_inc(s_v.h, 1)
            s_v.n += 1
            for j in range(NJ1):
                e.wait_ge(s_sc.h, scF + 2 * (j + 1))
                e.scalar_tensor_tensor(hav[j][:, 0:512], ppf[4][:], uscol1[:, j:j + 1],
                                       sg4[j % 2][:, 0:512], ALU.mult, ALU.mult).then_inc(s_v.h, 1)
                e.scalar_tensor_tensor(hav[j][:, 512:1024], ppf[5][:], uscol1[:, j:j + 1],
                                       sg4[j % 2][:, 512:1024], ALU.mult, ALU.mult).then_inc(s_v.h, 1)
                s_v.n += 2

        @b.scalar
        def _(e):
            e.wait_ge(s_sb[0].h, sbQ[0])
            e.wait_ge(s_sb[1].h, sbQ[1])
            for j in range(NJ1):
                pj = j % 2
                e.wait_ge(s_mm.h, mmF + 2 + j)
                if j >= 2:
                    e.wait_ge(s_v.h, vF + 1 + 2 * (j - 1))
                e.activation(sg4[pj][:, 0:512], ppf[2 * pj][:], AF.Silu, scale=gscol1[:, j:j + 1]).then_inc(s_sc.h, 1)
                e.activation(sg4[pj][:, 512:1024], ppf[2 * pj + 1][:], AF.Silu, scale=gscol1[:, j:j + 1]).then_inc(s_sc.h, 1)
                s_sc.n += 2

    esB4.close()
    mmG = s_mm.n
    vG = s_v.n
    scG = s_sc.n
    msG = s_ms.n

    esC4 = ExitStack()
    dwq = esC4.enter_context(nc.sbuf_tensor("dwq", [128, NJ1 * 256], BF16))
    ds1b = esC4.enter_context(nc.sbuf_tensor("ds1b", [128, D], F32))
    with nc.Block() as b:
        @b.tensor
        def _(e):
            e.wait_ge(s_v.h, vG)
            e.wait_ge(s_sc.h, scG)
            for t in range(NT):
                if t >= 1:
                    e.wait_ge(s_v.h, vG + t)
                for j in range(NJ1):
                    mi = e.matmul(pRc[:], hav[j][:, t * 128:(t + 1) * 128], onescol[:],
                                  start=(j == 0), stop=(j == NJ1 - 1))
                mi.then_inc(s_mm.h, 1)
                s_mm.n += 1
            e.wait_ge(s_cg.h, s_cg.n)
            for q in range(4):
                mi = e.matmul(ppf[q][:], ones1r[:], ds1row_sb[:, q * 512:(q + 1) * 512], start=True, stop=True)
            mi.then_inc(s_mm.h, 1)
            s_mm.n += 1

        @b.vector
        def _(e):
            for t in range(NT):
                e.wait_ge(s_mm.h, mmG + t + 1)
                e.tensor_copy(R2c[:, t:t + 1], pRc[:]).then_inc(s_v.h, 1)
                s_v.n += 1
            e.wait_ge(s_v.h, vG + NT)
            e.tensor_scalar_mul(mR2c[:], R2c[:], -128.0).then_inc(s_ms.h, 1)
            s_ms.n += 1
            e.wait_ge(s_mm.h, mmG + NT + 1)
            for q in range(4):
                ci = e.tensor_copy(ds1b[:, q * 512:(q + 1) * 512], ppf[q][:])
            ci.then_inc(s_ms.h, 1)
            s_ms.n += 1

    mmR = s_mm.n
    vR = s_v.n
    fs_marks = {}
    with nc.Block() as b:
        @b.gpsimd
        def _(e):
            e.wait_ge(s_mm.h, mmG)
            for dq in range(8):
                if dq >= 1:
                    e.wait_ge(s_mm.h, mmR + NT * dq)
                e.dma_start(dwq[:], d1_t[dq]).then_inc(s_wb.h, 16)
                s_wb.n += 16
                wb_marks[("d1", dq)] = s_wb.n
                for t in range(NT):
                    it_ = dq * NT + t
                    p = it_ % 2
                    e.wait_ge(s_v.h, vR + it_ + 1)
                    e.dma_start(ffn1_scr[t * 128:(t + 1) * 128, dq * 256:(dq + 1) * 256],
                                st4[p][:, 0:256]).then_inc(s_go[p].h, 16)
                    s_go[p].n += 16
                    fs_marks[it_] = s_go[p].n

        @b.tensor
        def _(e):
            e.wait_ge(s_ms.h, msG + 2)
            for dq in range(8):
                e.wait_ge(s_wb.h, wb_marks[("d1", dq)])
                for t in range(NT):
                    it_ = dq * NT + t
                    pf = ppf[it_ % 2]
                    if it_ >= 2:
                        e.wait_ge(s_v.h, vR + (it_ - 1))
                    for j in range(NJ1):
                        mi = e.matmul(pf[:, 0:256], hav[j][:, t * 128:(t + 1) * 128],
                                      dwq[:, j * 256:(j + 1) * 256], start=(j == 0), stop=(j == NJ1 - 1))
                    mi.then_inc(s_mm.h, 1)
                    s_mm.n += 1

        @b.vector
        def _(e):
            e.wait_ge(s_ms.h, msG + 2)
            for dq in range(8):
                for t in range(NT):
                    it_ = dq * NT + t
                    p = it_ % 2
                    pf = ppf[it_ % 2]
                    e.wait_ge(s_mm.h, mmR + it_ + 1)
                    if it_ >= 2:
                        e.wait_ge(s_go[p].h, fs_marks[it_ - 2])
                    e.scalar_tensor_tensor(st4[p][:, 0:256], pf[:, 0:256], mR2c[:, t:t + 1],
                                           ds1b[:, dq * 256:(dq + 1) * 256], ALU.add, ALU.mult).then_inc(s_v.h, 1)
                    s_v.n += 1

    esC4.close()
    es4.close()
    mmD4 = s_mm.n
    vD4 = s_v.n
    scD4 = s_sc.n

    esD = ExitStack()
    o_sb = es.enter_context(nc.sbuf_tensor("o_sb", [128, NH * TC], BF16))
    qts = esD.enter_context(nc.sbuf_tensor("qts", [128, NH * TC], BF16))
    kxs = esD.enter_context(nc.sbuf_tensor("kxs", [128, NH * TH], BF16))
    vxs = esD.enter_context(nc.sbuf_tensor("vxs", [128, NKV * NH * 128], BF16))
    msks = esD.enter_context(nc.sbuf_tensor("msks", [128, 8 * 640], BF16))
    ssb = [esD.enter_context(nc.sbuf_tensor(f"ssb{k}", [128, 640], F32)) for k in range(2)]
    pbb = [esD.enter_context(nc.sbuf_tensor(f"pbb{k}", [128, 640], BF16)) for k in range(2)]
    pnb = [esD.enter_context(nc.sbuf_tensor(f"pnb{k}", [128, 640], BF16)) for k in range(2)]
    ptsb = [esD.enter_context(nc.sbuf_tensor(f"ptsb{k}", [128, 640], BF16)) for k in range(2)]
    mxc = esD.enter_context(nc.sbuf_tensor("mxc", [128, 2], F32))
    smc = esD.enter_context(nc.sbuf_tensor("smc", [128, 2], F32))
    rcc = esD.enter_context(nc.sbuf_tensor("rcc", [128, 2], F32))
    psc = [esD.enter_context(nc.psum_tensor(f"psc{k}", [128, 1024], F32)) for k in range(2)]
    ptt = [esD.enter_context(nc.psum_tensor(f"ptt{k}", [128, 1024], BF16)) for k in range(2)]
    pov = [esD.enter_context(nc.psum_tensor(f"pov{k}", [128, 512], F32)) for k in range(2)]

    NIT = NH * 8
    pma = {}
    _v, _sc, _mm = s_v.n, s_sc.n, s_mm.n
    for it in range(NIT + 1):
        if it < NIT:
            _mm += 1; pma[("sc", it)] = _mm
        if it >= 1:
            _mm += 1; pma[("tp", it - 1)] = _mm
            _mm += 1; pma[("av", it - 1)] = _mm
    for it in range(NIT + 1):
        if it < NIT:
            _v += 1; pma[("mask", it)] = _v
            _v += 1; pma[("max", it)] = _v
        if it >= 1:
            _v += 1; pma[("pnb", it - 1)] = _v
        if it < NIT:
            _v += 1; pma[("recip", it)] = _v
        if it >= 1:
            _v += 1; pma[("ptsb", it - 1)] = _v
    for it in range(NIT + 1):
        if it < NIT:
            _sc += 1; pma[("exp", it)] = _sc
        if it >= 1:
            _sc += 1; pma[("oc", it - 1)] = _sc

    with nc.Block() as b:
        @b.sync
        def _(e):
            e.wait_ge(s_v.h, vD4)
            e.wait_ge(s_sb[0].h, sbQ[0])
            e.wait_ge(s_sb[1].h, sbQ[1])
            e.wait_ge(s_go[0].h, s_go[0].n)
            e.wait_ge(s_go[1].h, s_go[1].n)
            e.dma_start(qts[:], q_scr[:, :]).then_inc(s_c.h, 16)
            e.dma_start(kxs[:], k_scr[:, :]).then_inc(s_c.h, 16)
            e.dma_start(vxs[:], v_scr[:, :]).then_inc(s_c.h, 16)
            e.dma_start(msks[:], msk_in).then_inc(s_c.h, 16)
            s_c.n += 64

        @b.tensor
        def _(e):
            e.wait_ge(s_c.h, s_c.n)
            e.wait_ge(s_v.h, vD4)
            for it in range(NIT + 1):
                if it < NIT:
                    h_, qb = divmod(it, 8)
                    p = it % 2
                    if it >= 2:
                        e.wait_ge(s_v.h, pma[("mask", it - 2)])
                    e.matmul(psc[p][:, 0:512], qts[:, h_ * TC + qb * 128:h_ * TC + (qb + 1) * 128],
                             kxs[:, h_ * TH + qb * 128:h_ * TH + qb * 128 + 512], start=True, stop=True)
                    mi = e.matmul(psc[p][:, 512:640], qts[:, h_ * TC + qb * 128:h_ * TC + (qb + 1) * 128],
                                  kxs[:, h_ * TH + qb * 128 + 512:h_ * TH + qb * 128 + 640],
                                  start=True, stop=True)
                    mi.then_inc(s_mm.h, 1)
                    s_mm.n += 1
                if it >= 1:
                    jt = it - 1
                    h_, qb = divmod(jt, 8)
                    p = jt % 2
                    e.wait_ge(s_v.h, pma[("pnb", jt)])
                    if jt >= 2:
                        e.wait_ge(s_v.h, pma[("ptsb", jt - 2)])
                    for kb in range(5):
                        mi = e.transpose(ptt[p][:, kb * 128:(kb + 1) * 128],
                                         pnb[p][:, kb * 128:(kb + 1) * 128], ident[:])
                    mi.then_inc(s_mm.h, 1)
                    s_mm.n += 1
                    e.wait_ge(s_v.h, pma[("ptsb", jt)])
                    if jt >= 2:
                        e.wait_ge(s_sc.h, pma[("oc", jt - 2)])
                    for kb in range(5):
                        mi = e.matmul(pov[p][:, 0:128],
                                      vxs[:, ((qb + kb) * NH + h_) * 128:((qb + kb) * NH + h_ + 1) * 128],
                                      ptsb[p][:, kb * 128:(kb + 1) * 128], start=(kb == 0), stop=(kb == 4))
                    mi.then_inc(s_mm.h, 1)
                    s_mm.n += 1

        @b.vector
        def _(e):
            for it in range(NIT + 1):
                if it < NIT:
                    h_, qb = divmod(it, 8)
                    p = it % 2
                    e.wait_ge(s_mm.h, pma[("sc", it)])
                    if it >= 2:
                        e.wait_ge(s_sc.h, pma[("exp", it - 2)])
                    e.tensor_tensor(ssb[p][:, 0:640], psc[p][:, 0:640], msks[:, qb * 640:(qb + 1) * 640],
                                    ALU.add).then_inc(s_v.h, 1)
                    e.wait_ge(s_v.h, pma[("mask", it)])
                    e.reduce_max(mxc[:, p:p + 1], ssb[p][:, 0:640], axis=AX.X, negate=True).then_inc(s_v.h, 1)
                    s_v.n += 2
                if it >= 1:
                    jt = it - 1
                    p = jt % 2
                    e.wait_ge(s_sc.h, pma[("exp", jt)])
                    e.wait_ge(s_v.h, pma[("recip", jt)])
                    if jt >= 2:
                        e.wait_ge(s_mm.h, pma[("tp", jt - 2)])
                    e.tensor_scalar_mul(pnb[p][:, 0:640], pbb[p][:, 0:640], rcc[:, p:p + 1]).then_inc(s_v.h, 1)
                    s_v.n += 1
                if it < NIT:
                    p = it % 2
                    e.wait_ge(s_sc.h, pma[("exp", it)])
                    e.reciprocal(rcc[:, p:p + 1], smc[:, p:p + 1]).then_inc(s_v.h, 1)
                    s_v.n += 1
                if it >= 1:
                    jt = it - 1
                    p = jt % 2
                    e.wait_ge(s_mm.h, pma[("tp", jt)])
                    if jt >= 2:
                        e.wait_ge(s_mm.h, pma[("av", jt - 2)])
                    e.tensor_copy(ptsb[p][:, 0:640], ptt[p][:, 0:640]).then_inc(s_v.h, 1)
                    s_v.n += 1

        @b.scalar
        def _(e):
            for it in range(NIT + 1):
                if it < NIT:
                    p = it % 2
                    e.wait_ge(s_v.h, pma[("max", it)])
                    if it >= 2:
                        e.wait_ge(s_v.h, pma[("pnb", it - 2)])
                        e.wait_ge(s_v.h, pma[("recip", it - 2)])
                    e.activation(pbb[p][:, 0:640], ssb[p][:, 0:640], AF.Exp, bias=mxc[:, p:p + 1],
                                 accum_out=smc[:, p:p + 1]).then_inc(s_sc.h, 1)
                    s_sc.n += 1
                if it >= 1:
                    jt = it - 1
                    h_, qb = divmod(jt, 8)
                    p = jt % 2
                    e.wait_ge(s_mm.h, pma[("av", jt)])
                    e.activation(o_sb[:, h_ * TC + qb * 128:h_ * TC + (qb + 1) * 128],
                                 pov[p][:, 0:128], AF.Copy).then_inc(s_sc.h, 1)
                    s_sc.n += 1

    assert s_v.n == _v and s_sc.n == _sc and s_mm.n == _mm, (s_v.n, _v, s_sc.n, _sc, s_mm.n, _mm)
    esD.close()
    mmA4 = s_mm.n
    vA4 = s_v.n
    scA4 = s_sc.n

    go_base4c = [s_go[0].n, s_go[1].n]
    esE = ExitStack()
    stgo = [esE.enter_context(nc.sbuf_tensor(f"stgo{k}", [128, TC], F32)) for k in range(2)]
    woh1 = esE.enter_context(nc.sbuf_tensor("woh1", [128, NDC * 1024], BF16))
    xto = [esE.enter_context(nc.sbuf_tensor(f"xto{k}", [128, 1024], F32)) for k in range(2)]
    fto = [esE.enter_context(nc.sbuf_tensor(f"fto{k}", [128, 1024], F32)) for k in range(2)]
    ppo = [esE.enter_context(nc.psum_tensor(f"ppo{k}", [128, 512], F32)) for k in range(4)]
    with nc.Block() as b:
        @b.sync
        def _(e):
            e.wait_ge(s_v.h, vA4)
            e.wait_ge(s_sc.h, scA4)
            e.wait_ge(s_go[0].h, s_go[0].n)
            e.wait_ge(s_go[1].h, s_go[1].n)
            for dh_ in range(2):
                for t in range(NT):
                    it_ = dh_ * NT + t
                    p = it_ % 2
                    if it_ >= 2:
                        e.wait_ge(s_v.h, vA4 + 2 * (it_ - 1))
                    e.dma_start(xto[p][:], x1_scr[HOFF + t * 128:HOFF + (t + 1) * 128,
                                                  dh_ * 1024:(dh_ + 1) * 1024]).then_inc(s_x[p].h, 16)
                    e.dma_start(fto[p][:], ffn1_scr[t * 128:(t + 1) * 128,
                                                    dh_ * 1024:(dh_ + 1) * 1024]).then_inc(s_x[p].h, 16)
                    s_x[p].n += 32
                    x_marks[("fin", it_)] = s_x[p].n

        @b.gpsimd
        def _(e):
            e.wait_ge(s_v.h, vA4)
            e.wait_ge(s_sc.h, scA4)
            for dh_ in range(2):
                if dh_ == 1:
                    e.wait_ge(s_mm.h, mmA4 + NT)
                e.dma_start(woh1[:], wo1_t[dh_]).then_inc(s_wb.h, 16)
                s_wb.n += 16
                wb_marks[("wo1", dh_)] = s_wb.n
                for t in range(NT):
                    it_ = dh_ * NT + t
                    p = it_ % 2
                    e.wait_ge(s_v.h, vA4 + 2 * (it_ + 1))
                    e.dma_start(out_f[t * 128:(t + 1) * 128, dh_ * 1024:(dh_ + 1) * 1024],
                                stgo[p][:, 0:1024]).then_inc(s_go[p].h, 16)
                    s_go[p].n += 16
            e.wait_ge(s_go[0].h, s_go[0].n)
            e.wait_ge(s_go[1].h, s_go[1].n)

        @b.tensor
        def _(e):
            e.wait_ge(s_v.h, vA4)
            e.wait_ge(s_sc.h, scA4)
            for dh_ in range(2):
                e.wait_ge(s_wb.h, wb_marks[("wo1", dh_)])
                for t in range(NT):
                    it_ = dh_ * NT + t
                    pa, pb = (ppo[0], ppo[1]) if it_ % 2 == 0 else (ppo[2], ppo[3])
                    if it_ >= 2:
                        e.wait_ge(s_v.h, vA4 + 2 * (it_ - 1))
                    for i in range(NDC):
                        st_, sp_ = (i == 0), (i == NDC - 1)
                        e.matmul(pa[:], o_sb[:, i * TC + t * 128:i * TC + (t + 1) * 128],
                                 woh1[:, i * 1024:i * 1024 + 512], start=st_, stop=sp_)
                        mi = e.matmul(pb[:], o_sb[:, i * TC + t * 128:i * TC + (t + 1) * 128],
                                      woh1[:, i * 1024 + 512:(i + 1) * 1024], start=st_, stop=sp_)
                        if sp_:
                            mi.then_inc(s_mm.h, 1)
                    s_mm.n += 1

        @b.vector
        def _(e):
            for dh_ in range(2):
                for t in range(NT):
                    it_ = dh_ * NT + t
                    p = it_ % 2
                    pa, pb = (ppo[0], ppo[1]) if p == 0 else (ppo[2], ppo[3])
                    e.wait_ge(s_x[p].h, x_marks[("fin", it_)])
                    e.wait_ge(s_mm.h, mmA4 + it_ + 1)
                    if it_ >= 2:
                        e.wait_ge(s_go[p].h, go_base4c[p] + 16 * (it_ // 2))
                    e.tensor_tensor(stgo[p][:, 0:512], xto[p][:, 0:512], pa[:], ALU.add)
                    e.tensor_tensor(stgo[p][:, 512:1024], xto[p][:, 512:1024], pb[:], ALU.add).then_inc(s_v.h, 1)
                    e.wait_ge(s_v.h, vA4 + 2 * it_ + 1)
                    e.tensor_tensor(stgo[p][:, 0:1024], stgo[p][:, 0:1024], fto[p][:], ALU.add).then_inc(s_v.h, 1)
                    s_v.n += 2

    esE.close()
    es.close()
    return nc


def host_inputs_fused(inputs):
    shared = dict(
        w_in_t=tile_wT(np.asarray(inputs["L0_attn_W_in"]).T, NJ_IN).astype(BF),
        w_dtT=f32(np.asarray(inputs["L0_attn_W_dt"]).T).astype(BF),
        bdt_pc=col_major_pc(inputs["L0_attn_b_dt"], NJ_DT),
        gw_t=tile_wT(np.asarray(inputs["L0_ffn_gate_w"]).T, NJ_F).astype(BF),
        uw_t=tile_wT(np.asarray(inputs["L0_ffn_up_w"]).T, NJ_F).astype(BF),
        dw_t=np.ascontiguousarray(
            f32(np.asarray(inputs["L0_ffn_down_w"]).T).reshape(NJ_F, 128, 2, 1024)
            .transpose(2, 1, 0, 3).reshape(2, 128, NJ_F * 1024)).astype(BF),
        gs_pc=col_major_pc(inputs["L0_ffn_gate_s"], NJ_F),
        us_pc=col_major_pc(inputs["L0_ffn_up_s"], NJ_F),
        ds_row=f32(np.asarray(inputs["L0_ffn_down_s"]).reshape(1, D)),
        ident=np.eye(128, dtype=BF),
        alog_pc=np.ascontiguousarray(
            np.asarray(inputs["L0_attn_A_log"], np.float32).reshape(NIY, 128, DST)
            .transpose(1, 0, 2).reshape(128, NIY * DST)),
        cw_pc=np.ascontiguousarray(
            np.asarray(inputs["L0_attn_conv_w"], np.float32).reshape(DI, DCONV)
            .reshape(NIY, 128, DCONV).transpose(1, 0, 2).reshape(128, NIY * DCONV)),
        cb_pc=col_major_pc(inputs["L0_attn_conv_b"], NIY),
        dp_pc=col_major_pc(inputs["L0_attn_D_param"], NIY),
        wo_t=np.ascontiguousarray(
            f32(np.asarray(inputs["L0_attn_W_out"]).T).reshape(NIY, 128, 2, 1024)
            .transpose(2, 1, 0, 3).reshape(2, 128, NIY * 1024)).astype(BF),
        wqk_t=tile_wT(f32(np.asarray(inputs["L1_attn_w_qkv"])[0:2 * D].T), 2 * NDC).astype(BF),
        wv_t=np.ascontiguousarray(
            f32(np.asarray(inputs["L1_attn_w_qkv"])[2 * D:3 * D].T)
            .reshape(NDC, 128, D).transpose(1, 0, 2).reshape(128, NDC * D)).astype(BF),
        g1_t=tile_wT_i32(np.asarray(inputs["L1_ffn_gate_w"]).T, NJ1),
        u1_t=tile_wT_i32(np.asarray(inputs["L1_ffn_up_w"]).T, NJ1),
        d1_t=np.ascontiguousarray(
            np.asarray(inputs["L1_ffn_down_w"], np.int32).T
            .reshape(NJ1, 128, 8, 256).transpose(2, 1, 0, 3).reshape(8, 128, NJ1 * 256)),
        gs1_pc=col_major_pc(inputs["L1_ffn_gate_scale"], NJ1),
        us1_pc=col_major_pc(inputs["L1_ffn_up_scale"], NJ1),
        ds1_row=f32(np.asarray(inputs["L1_ffn_down_scale"]).reshape(1, D)),
        wo1_t=np.ascontiguousarray(
            f32(np.asarray(inputs["L1_attn_w_out"]).T).reshape(NDC, 128, 2, 1024)
            .transpose(2, 1, 0, 3).reshape(2, 128, NDC * 1024)).astype(BF),
    )
    eexp = np.zeros((16, 128), dtype=BF)
    for p_ in range(128):
        eexp[p_ % 16, p_] = 1
    ebc = np.zeros((DST, 16, 128), dtype=BF)
    for n in range(DST):
        ebc[n, n, :] = 1
    scs = np.zeros((DST, 128, 128), dtype=BF)
    for n in range(DST):
        for p_ in range(128):
            scs[n, 16 * (p_ // 16) + n, p_] = 1
    shared["eexp"] = eexp
    shared["ebc"] = ebc
    shared["scs"] = scs

    x = np.asarray(inputs["L0_attn_hidden_states"], np.float32)
    jj = np.arange(640)
    pp_ = np.arange(128)
    per_core = []
    for c in range(NCORE):
        bq, hh = c // 2, c % 2
        m = dict(shared)
        if hh == 0:
            xw = np.concatenate([np.zeros((1024, D), np.float32), x[bq, 0:1024]], axis=0)
        else:
            xw = np.ascontiguousarray(x[bq])
        m["x"] = xw
        msk = np.zeros((8, 128, 640), np.float32)
        for qb in range(8):
            kpos = hh * 1024 + qb * 128 - 512 + jj
            band = (jj[None, :] >= pp_[:, None]) & (jj[None, :] <= pp_[:, None] + 512)
            ok = band & (kpos >= 0)[None, :]
            msk[qb] = np.where(ok, 0.0, -1e30).astype(np.float32)
        m["msk"] = np.ascontiguousarray(msk.transpose(1, 0, 2).reshape(128, 8 * 640)).astype(BF)
        per_core.append(m)
    return per_core


def kernel_fused(**inputs):
    nc = build_fused()
    res = run_bass_kernel_spmd(nc, host_inputs_fused(inputs), core_ids=list(range(NCORE))).results
    out = np.concatenate([res[c]["out"] for c in range(NCORE)], axis=0)
    return np.ascontiguousarray(out.reshape(NB, L, D), dtype=np.float32)


# version 24
# speedup vs baseline: 1.0422x; 1.0085x over previous
import math
import numpy as np
import ml_dtypes
from contextlib import ExitStack

import concourse.bass as bass
import concourse.mybir as mybir
from concourse.bass_utils import run_bass_kernel_spmd

F32 = mybir.dt.float32
BF16 = mybir.dt.bfloat16
I32 = mybir.dt.int32
AF = mybir.ActivationFunctionType
ALU = mybir.AluOpType
AX = mybir.AxisListType
BF = ml_dtypes.bfloat16

D = 2048
DI = 4096
DXB = 512
DST = 16
DCONV = 4
DTR = 128
NB = 4
L = 2048
ZXROWS = 2 * DI + 2 * DXB + DTR
FI0 = 5632
FI1 = 8192
NH = 16
DH = 128
WIN = 512
SCALE = 1.0 / math.sqrt(DH)
NCORE = 8
TC = 1024
CSL = 512
EPS = 1e-5

OFF_Z = 0
OFF_X = DI
OFF_B = DI + DXB
OFF_C = DI + 2 * DXB
OFF_DT = 2 * DI + 2 * DXB

NT = TC // 128
NDC = D // 128
NJ_IN = ZXROWS // 128
NJ_DT = DI // 128
NJ_F = FI0 // 128


class Sem:
    def __init__(self, es, nc, name):
        self.h = es.enter_context(nc.semaphore(name))
        self.n = 0


def bf16(a):
    return np.ascontiguousarray(np.asarray(a).astype(BF))


def f32(a):
    return np.ascontiguousarray(np.asarray(a), dtype=np.float32)


def col_major_pc(vec, nchunks):
    v = np.ascontiguousarray(np.asarray(vec, dtype=np.float32)).reshape(nchunks, 128)
    return np.ascontiguousarray(v.T)


def tile_wT(wT, nj, nin=NDC, block=128):
    a = np.asarray(wT, dtype=np.float32)
    ni = a.shape[0] // 128
    a = a.reshape(ni, 128, nj, block).transpose(2, 1, 0, 3).reshape(nj, 128, ni * block)
    return np.ascontiguousarray(a)



def build_launch1():
    nc = bass.Bass("TRN2", target_bir_lowering=False)
    x_in = nc.dram_tensor("x", [TC, D], F32, kind="ExternalInput").ap()
    w_in_t = nc.dram_tensor("w_in_t", [NJ_IN, 128, NDC * 128], F32, kind="ExternalInput").ap()
    w_dtT = nc.dram_tensor("w_dtT", [DTR, DI], F32, kind="ExternalInput").ap()
    bdt_pc = nc.dram_tensor("bdt_pc", [128, NJ_DT], F32, kind="ExternalInput").ap()
    gw_t = nc.dram_tensor("gw_t", [NJ_F, 128, NDC * 128], F32, kind="ExternalInput").ap()
    uw_t = nc.dram_tensor("uw_t", [NJ_F, 128, NDC * 128], F32, kind="ExternalInput").ap()
    dw_t = nc.dram_tensor("dw_t", [2, 128, NJ_F * 1024], F32, kind="ExternalInput").ap()
    gs_pc = nc.dram_tensor("gs_pc", [128, NJ_F], F32, kind="ExternalInput").ap()
    us_pc = nc.dram_tensor("us_pc", [128, NJ_F], F32, kind="ExternalInput").ap()
    ds_row = nc.dram_tensor("ds_row", [1, D], F32, kind="ExternalInput").ap()
    ident_in = nc.dram_tensor("ident", [128, 128], BF16, kind="ExternalInput").ap()

    zx_out = nc.dram_tensor("zx", [ZXROWS, TC], BF16, kind="ExternalOutput").ap()
    delta_out = nc.dram_tensor("delta", [DI, TC], F32, kind="ExternalOutput").ap()
    ffn0_out = nc.dram_tensor("ffn0", [TC, D], F32, kind="ExternalOutput").ap()
    h_scr = nc.dram_tensor("h_scr", [128, NJ_F, TC], BF16, kind="Internal").ap()

    es = ExitStack()
    ident = es.enter_context(nc.sbuf_tensor("identt", [128, 128], BF16))
    xnT = [es.enter_context(nc.sbuf_tensor(f"xnT{j}", [128, TC], BF16)) for j in range(NDC)]
    wbuf = [es.enter_context(nc.sbuf_tensor(f"wbuf{k}", [128, NDC * 128], BF16)) for k in range(2)]
    stg = [es.enter_context(nc.sbuf_tensor(f"stg{k}", [128, TC], F32)) for k in range(2)]
    stgb = [es.enter_context(nc.sbuf_tensor(f"stgb{k}", [128, TC], BF16)) for k in range(2)]
    sgst = [es.enter_context(nc.sbuf_tensor(f"sgst{k}", [128, TC], F32)) for k in range(2)]
    dtl_sb = es.enter_context(nc.sbuf_tensor("dtl_sb", [128, TC], BF16))
    bdt1 = es.enter_context(nc.sbuf_tensor("bdt1", [128, NJ_DT], F32))
    bdt2 = es.enter_context(nc.sbuf_tensor("bdt2", [128, NJ_DT], F32))
    gscol = es.enter_context(nc.sbuf_tensor("gscol", [128, NJ_F], F32))
    uscol = es.enter_context(nc.sbuf_tensor("uscol", [128, NJ_F], F32))
    dsb = es.enter_context(nc.sbuf_tensor("dsb", [128, D], F32))
    dsrow_sb = es.enter_context(nc.sbuf_tensor("dsrow_sb", [1, D], BF16))
    ones1 = es.enter_context(nc.sbuf_tensor("ones1", [1, 128], BF16))
    mcol = es.enter_context(nc.sbuf_tensor("mcol", [128, NT], F32))
    rcol = es.enter_context(nc.sbuf_tensor("rcol", [128, NT], F32))
    scr1 = es.enter_context(nc.sbuf_tensor("scr1", [128, NT], F32))
    scr2 = es.enter_context(nc.sbuf_tensor("scr2", [128, NT], F32))
    konst = es.enter_context(nc.sbuf_tensor("konst", [128, 2], F32))


    s_c = Sem(es, nc, "s_c")
    s_cg = Sem(es, nc, "s_cg")
    s_x = [Sem(es, nc, f"s_x{k}") for k in range(2)]
    s_wq = [Sem(es, nc, f"s_wq{k}") for k in range(2)]
    s_hb = [Sem(es, nc, f"s_hb{k}") for k in range(2)]
    s_sb = [Sem(es, nc, f"s_sb{k}") for k in range(2)]
    s_sg = [Sem(es, nc, f"s_sg{k}") for k in range(2)]
    s_v = Sem(es, nc, "s_v")
    s_sc = Sem(es, nc, "s_sc")
    s_mm = Sem(es, nc, "s_mm")
    s_ms = Sem(es, nc, "s_ms")

    wq_marks = {}
    x_marks = {}
    hb_marks = {}

    with nc.sbuf_tensor("xt0", [128, D], F32) as xt0, \
         nc.sbuf_tensor("xt1", [128, D], F32) as xt1, \
         nc.sbuf_tensor("sq0", [128, D], F32) as sq0, \
         nc.sbuf_tensor("sq1", [128, D], F32) as sq1, \
         nc.sbuf_tensor("xnb0", [128, D], BF16) as xnb0, \
         nc.sbuf_tensor("xnb1", [128, D], BF16) as xnb1:
        xt = [xt0, xt1]
        sq = [sq0, sq1]
        xnb = [xnb0, xnb1]

        with nc.Block() as b:
            @b.sync
            def _(e):
                e.dma_start(ident[:], ident_in).then_inc(s_c.h, 16)
                e.dma_start(bdt1[:], bdt_pc).then_inc(s_c.h, 16)
                e.dma_start(gscol[:], gs_pc).then_inc(s_c.h, 16)
                e.dma_start(uscol[:], us_pc).then_inc(s_c.h, 16)
                s_c.n += 64
                for i in range(NT):
                    p = i % 2
                    if i >= 2:
                        e.wait_ge(s_v.h, 22 * (i - 1))
                        e.wait_ge(s_sc.h, 3 * (i - 1))
                    e.dma_start(xt[p][:], x_in[i * 128:(i + 1) * 128, :]).then_inc(s_x[p].h, 16)
                    s_x[p].n += 16
                    x_marks[i] = s_x[p].n

            @b.gpsimd
            def _(e):
                e.dma_start(dsrow_sb[:], ds_row).then_inc(s_cg.h, 16)
                e.memset(ones1[:], 1.0)
                e.memset(konst[:, 0:1], EPS)
                e.memset(konst[:, 1:2], -0.5).then_inc(s_ms.h, 1)
                s_ms.n += 1
                s_cg.n += 16
                e.dma_start(wbuf[0][:], w_in_t[0]).then_inc(s_wq[0].h, 16)
                s_wq[0].n += 16
                wq_marks[("in", 0)] = (0, s_wq[0].n)

            @b.scalar
            def _(e):
                e.wait_ge(s_ms.h, 1)
                for i in range(NT):
                    p = i % 2
                    e.wait_ge(s_x[p].h, x_marks[i])
                    e.square(sq[p][:], xt[p][:]).then_inc(s_sc.h, 1)
                    e.wait_ge(s_v.h, 22 * i + 5 + 1)
                    e.activation(scr1[:, i:i + 1], scr2[:, i:i + 1], AF.Ln, bias=konst[:, 0:1]).then_inc(s_sc.h, 1)
                    e.wait_ge(s_sc.h, s_sc.n + 2)
                    e.activation(rcol[:, i:i + 1], scr1[:, i:i + 1], AF.Exp, scale=konst[:, 1:2]).then_inc(s_sc.h, 1)
                    s_sc.n += 3

            @b.vector
            def _(e):
                e.wait_ge(s_c.h, 64)
                e.wait_ge(s_cg.h, 16)
                e.tensor_add(bdt2[:], bdt1[:], bdt1[:]).then_inc(s_v.h, 1)
                s_v.n += 1
                for i in range(NT):
                    p = i % 2
                    e.wait_ge(s_x[p].h, x_marks[i])
                    e.reduce_sum(scr1[:, i:i + 1], xt[p][:], axis=AX.X).then_inc(s_v.h, 1)
                    e.wait_ge(s_v.h, s_v.n + 1)
                    e.tensor_scalar_mul(mcol[:, i:i + 1], scr1[:, i:i + 1], 1.0 / D).then_inc(s_v.h, 1)
                    e.wait_ge(s_sc.h, 3 * i + 1)
                    e.wait_ge(s_v.h, s_v.n + 2)
                    e.reduce_sum(scr2[:, i:i + 1], sq[p][:], axis=AX.X).then_inc(s_v.h, 1)
                    e.wait_ge(s_v.h, s_v.n + 3)
                    e.tensor_scalar_mul(scr2[:, i:i + 1], scr2[:, i:i + 1], 1.0 / D)
                    e.tensor_mul(scr1[:, i:i + 1], mcol[:, i:i + 1], mcol[:, i:i + 1]).then_inc(s_v.h, 1)
                    e.wait_ge(s_v.h, s_v.n + 4)
                    e.tensor_sub(scr2[:, i:i + 1], scr2[:, i:i + 1], scr1[:, i:i + 1]).then_inc(s_v.h, 1)
                    e.wait_ge(s_sc.h, 3 * i + 3)
                    if i >= 2:
                        e.wait_ge(s_mm.h, 16 * (i - 1))
                    e.tensor_scalar(xnb[p][:], xt[p][:], mcol[:, i:i + 1], rcol[:, i:i + 1],
                                    ALU.subtract, ALU.mult).then_inc(s_v.h, 1)
                    s_v.n += 6
                    for j in range(NDC):
                        e.wait_ge(s_mm.h, 16 * i + j + 1)
                        e.tensor_copy(xnT[j][:, i * 128:(i + 1) * 128],
                                      ptp[(16 * i + j) % 2][:, 0:128]).then_inc(s_v.h, 1)
                        s_v.n += 1

            @b.tensor
            def _(e):
                e.wait_ge(s_c.h, 64)
                for i in range(NT):
                    e.wait_ge(s_v.h, 22 * i + 6 + 1)
                    for j in range(NDC):
                        k = 16 * i + j
                        if k >= 2:
                            i2, j2 = divmod(k - 2, 16)
                            e.wait_ge(s_v.h, 22 * i2 + 6 + j2 + 1 + 1)
                        e.transpose(ptp[k % 2][:, 0:128], xnb[i % 2][:, j * 128:(j + 1) * 128],
                                    ident[:]).then_inc(s_mm.h, 1)
                        s_mm.n += 1

    mm0 = s_mm.n
    v0 = s_v.n

    with nc.Block() as b:
        @b.gpsimd
        def _(e):
            for j in range(1, NJ_IN):
                p = j % 2
                if j >= 2:
                    e.wait_ge(s_mm.h, mm0 + 32 * (j - 1))
                e.dma_start(wbuf[p][:], w_in_t[j]).then_inc(s_wq[p].h, 16)
                s_wq[p].n += 16
                wq_marks[("in", j)] = (p, s_wq[p].n)

        @b.tensor
        def _(e):
            e.wait_ge(s_v.h, v0)
            for j in range(NJ_IN):
                p, mark = wq_marks[("in", j)]
                e.wait_ge(s_wq[p].h, mark)
                pa, pb = (pp[0], pp[1]) if j % 2 == 0 else (pp[2], pp[3])
                if j >= 2:
                    e.wait_ge(s_v.h, v0 + 2 * (j - 1))
                for i in range(NDC):
                    st_, sp_ = (i == 0), (i == NDC - 1)
                    e.matmul(pa[:], wbuf[p][:, i * 128:(i + 1) * 128], xnT[i][:, 0:512],
                             start=st_, stop=sp_)
                    mi = e.matmul(pb[:], wbuf[p][:, i * 128:(i + 1) * 128], xnT[i][:, 512:1024],
                                  start=st_, stop=sp_)
                    if sp_:
                        mi.then_inc(s_mm.h, 32)
                s_mm.n += 32

        @b.vector
        def _(e):
            for j in range(NJ_IN):
                p = j % 2
                pa, pb = (pp[0], pp[1]) if p == 0 else (pp[2], pp[3])
                e.wait_ge(s_mm.h, mm0 + 32 * (j + 1))
                if j >= 2:
                    e.wait_ge(s_sb[p].h, 16 * (j // 2))
                e.tensor_copy(stgb[p][:, 0:512], pa[:])
                e.tensor_copy(stgb[p][:, 512:1024], pb[:]).then_inc(s_v.h, 2)
                s_v.n += 2
                if j == NJ_IN - 1:
                    e.tensor_copy(dtl_sb[:, 0:512], pa[:])
                    e.tensor_copy(dtl_sb[:, 512:1024], pb[:]).then_inc(s_ms.h, 1)
                    s_ms.n += 1

        @b.sync
        def _(e):
            for j in range(NJ_IN):
                p = j % 2
                e.wait_ge(s_v.h, v0 + 2 * (j + 1))
                e.dma_start(zx_out[j * 128:(j + 1) * 128, :], stgb[p][:]).then_inc(s_sb[p].h, 16)
                s_sb[p].n += 16

    mm1 = s_mm.n
    v1 = s_v.n
    sc1 = s_sc.n
    sb1 = [s_sb[0].n, s_sb[1].n]

    with nc.Block() as b:
        @b.gpsimd
        def _(e):
            for jj in range(NJ_DT):
                p = jj % 2
                if jj >= 2:
                    e.wait_ge(s_mm.h, mm1 + 2 * (jj - 1))
                e.dma_start(wbuf[p][:, 0:128],
                            w_dtT[:, jj * 128:(jj + 1) * 128]).then_inc(s_wq[p].h, 16)
                s_wq[p].n += 16
                wq_marks[("dt", jj)] = (p, s_wq[p].n)

        @b.tensor
        def _(e):
            e.wait_ge(s_ms.h, 2)
            e.wait_ge(s_v.h, v1)
            for jj in range(NJ_DT):
                p, mark = wq_marks[("dt", jj)]
                e.wait_ge(s_wq[p].h, mark)
                pa, pb = (pp[0], pp[1]) if jj % 2 == 0 else (pp[2], pp[3])
                if jj >= 2:
                    e.wait_ge(s_sc.h, sc1 + 4 * (jj - 1))
                e.matmul(pa[:], wbuf[p][:, 0:128], dtl_sb[:, 0:512], start=True, stop=True)
                e.matmul(pb[:], wbuf[p][:, 0:128], dtl_sb[:, 512:1024],
                         start=True, stop=True).then_inc(s_mm.h, 2)
                s_mm.n += 2

        @b.scalar
        def _(e):
            for jj in range(NJ_DT):
                p = jj % 2
                pa, pb = (pp[0], pp[1]) if p == 0 else (pp[2], pp[3])
                e.wait_ge(s_mm.h, mm1 + 2 * (jj + 1))
                if jj >= 2:
                    e.wait_ge(s_v.h, v1 + 2 * (jj - 1))
                    e.wait_ge(s_sg[p].h, 16 * (jj // 2))
                e.add(stg[p][:, 0:512], pa[:], bdt2[:, jj:jj + 1])
                e.add(stg[p][:, 512:1024], pb[:], bdt2[:, jj:jj + 1]).then_inc(s_sc.h, 1)
                e.wait_ge(s_sc.h, s_sc.n + 1)
                e.activation(sgst[p][:], stg[p][:], AF.Abs).then_inc(s_sc.h, 1)
                e.wait_ge(s_sc.h, s_sc.n + 2)
                e.activation(sgst[p][:], sgst[p][:], AF.Exp, scale=-1.0).then_inc(s_sc.h, 1)
                e.wait_ge(s_sc.h, s_sc.n + 3)
                e.activation(sgst[p][:], sgst[p][:], AF.Ln, bias=1.0).then_inc(s_sc.h, 1)
                s_sc.n += 4

        @b.vector
        def _(e):
            for jj in range(NJ_DT):
                p = jj % 2
                e.wait_ge(s_sc.h, sc1 + 4 * (jj + 1))
                e.tensor_scalar_max(stg[p][:], stg[p][:], 0.0).then_inc(s_v.h, 1)
                e.wait_ge(s_v.h, s_v.n + 1)
                e.tensor_add(stg[p][:], stg[p][:], sgst[p][:]).then_inc(s_v.h, 1)
                s_v.n += 2

        @b.sync
        def _(e):
            for jj in range(NJ_DT):
                p = jj % 2
                e.wait_ge(s_v.h, v1 + 2 * (jj + 1))
                e.dma_start(delta_out[jj * 128:(jj + 1) * 128, :], stg[p][:]).then_inc(s_sg[p].h, 16)
                s_sg[p].n += 16

    mm2 = s_mm.n
    v2 = s_v.n
    sc2 = s_sc.n
    sb2 = [s_sb[0].n, s_sb[1].n]

    with nc.Block() as b:
        @b.gpsimd
        def _(e):
            for j in range(NJ_F):
                if j >= 1:
                    e.wait_ge(s_mm.h, mm2 + 64 * j)
                e.dma_start(wbuf[0][:], gw_t[j]).then_inc(s_wq[0].h, 16)
                s_wq[0].n += 16
                e.dma_start(wbuf[1][:], uw_t[j]).then_inc(s_wq[1].h, 16)
                s_wq[1].n += 16
                wq_marks[("f", j)] = (s_wq[0].n, s_wq[1].n)

        @b.tensor
        def _(e):
            e.wait_ge(s_sc.h, sc2)
            for j in range(NJ_F):
                m0, m1 = wq_marks[("f", j)]
                e.wait_ge(s_wq[0].h, m0)
                e.wait_ge(s_wq[1].h, m1)
                if j >= 1:
                    e.wait_ge(s_v.h, v2 + 2 * j)
                for i in range(NDC):
                    st_, sp_ = (i == 0), (i == NDC - 1)
                    e.matmul(pp[0][:], wbuf[0][:, i * 128:(i + 1) * 128], xnT[i][:, 0:512], start=st_, stop=sp_)
                    e.matmul(pp[1][:], wbuf[0][:, i * 128:(i + 1) * 128], xnT[i][:, 512:1024], start=st_, stop=sp_)
                    e.matmul(pp[2][:], wbuf[1][:, i * 128:(i + 1) * 128], xnT[i][:, 0:512], start=st_, stop=sp_)
                    mi = e.matmul(pp[3][:], wbuf[1][:, i * 128:(i + 1) * 128], xnT[i][:, 512:1024],
                                  start=st_, stop=sp_)
                    if sp_:
                        mi.then_inc(s_mm.h, 64)
                s_mm.n += 64

        @b.scalar
        def _(e):
            for j in range(NJ_F):
                p = j % 2
                e.wait_ge(s_mm.h, mm2 + 64 * (j + 1))
                if j >= 2:
                    e.wait_ge(s_v.h, v2 + 2 * (j - 1))
                e.activation(sgst[p][:, 0:512], pp[0][:], AF.Silu, scale=gscol[:, j:j + 1])
                e.activation(sgst[p][:, 512:1024], pp[1][:], AF.Silu, scale=gscol[:, j:j + 1]).then_inc(s_sc.h, 2)
                s_sc.n += 2

        @b.vector
        def _(e):
            for j in range(NJ_F):
                p = j % 2
                e.wait_ge(s_sc.h, sc2 + 2 * (j + 1))
                e.wait_ge(s_sb[p].h, sb2[p] + 16 * (j // 2))
                e.scalar_tensor_tensor(stgb[p][:, 0:512], pp[2][:], uscol[:, j:j + 1],
                                       sgst[p][:, 0:512], ALU.mult, ALU.mult)
                e.scalar_tensor_tensor(stgb[p][:, 512:1024], pp[3][:], uscol[:, j:j + 1],
                                       sgst[p][:, 512:1024], ALU.mult, ALU.mult).then_inc(s_v.h, 2)
                s_v.n += 2

        @b.sync
        def _(e):
            for j in range(NJ_F):
                p = j % 2
                e.wait_ge(s_v.h, v2 + 2 * (j + 1))
                e.dma_start(h_scr[:, j, :], stgb[p][:]).then_inc(s_sb[p].h, 16)
                s_sb[p].n += 16

    mm3 = s_mm.n
    v3 = s_v.n
    sb3 = [s_sb[0].n, s_sb[1].n]
    sg3 = [s_sg[0].n, s_sg[1].n]

    with nc.Block() as b:
        @b.tensor
        def _(e):
            e.wait_ge(s_ms.h, 1)
            e.wait_ge(s_cg.h, 16)
            e.wait_ge(s_v.h, v3)
            for q in range(4):
                mi = e.matmul(pp[q][:], ones1[:], dsrow_sb[:, q * 512:(q + 1) * 512], start=True, stop=True)
            mi.then_inc(s_mm.h, 1)
            s_mm.n += 1

        @b.vector
        def _(e):
            e.wait_ge(s_mm.h, mm3 + 1)
            for q in range(4):
                ci = e.tensor_copy(dsb[:, q * 512:(q + 1) * 512], pp[q][:])
            ci.then_inc(s_ms.h, 1)
            s_ms.n += 1

    mm4 = s_mm.n
    ms4 = s_ms.n

    HBS = 2
    with nc.sbuf_tensor("dwh", [128, NJ_F * 1024], BF16) as dwh, \
         nc.sbuf_tensor("hb0", [128, NJ_F * 128], BF16) as hb0, \
         nc.sbuf_tensor("hb1", [128, NJ_F * 128], BF16) as hb1:
        hb = [hb0, hb1]
        s_dw = Sem(es, nc, "s_dw")
        dw_marks = {}
        with nc.Block() as b:
            @b.gpsimd
            def _(e):
                for dh_ in range(2):
                    if dh_ == 1:
                        e.wait_ge(s_mm.h, mm4 + 88 * NT)
                    e.dma_start(dwh[:], dw_t[dh_]).then_inc(s_dw.h, 16)
                    s_dw.n += 16
                    dw_marks[dh_] = s_dw.n
                    for t in range(NT):
                        it_ = dh_ * NT + t
                        p = t % 2
                        if dh_ == 0 and t == 0:
                            e.wait_ge(s_sb[0].h, sb3[0])
                            e.wait_ge(s_sb[1].h, sb3[1])
                        if it_ >= 2:
                            e.wait_ge(s_mm.h, mm4 + 88 * (it_ - 1))
                        half = NJ_F // HBS
                        for q in range(HBS):
                            e.dma_start(
                                hb[p][:, q * half * 128:(q + 1) * half * 128]
                                .rearrange("p (i o) -> p i o", i=half),
                                h_scr[:, q * half:(q + 1) * half, t * 128:(t + 1) * 128],
                            ).then_inc(s_hb[p].h, 16)
                            s_hb[p].n += 16
                        hb_marks[it_] = (p, s_hb[p].n)

            @b.tensor
            def _(e):
                e.wait_ge(s_ms.h, ms4)
                for dh_ in range(2):
                    for t in range(NT):
                        it_ = dh_ * NT + t
                        p, mark = hb_marks[it_]
                        e.wait_ge(s_dw.h, dw_marks[dh_])
                        e.wait_ge(s_hb[p].h, mark)
                        pa, pb = (pp[0], pp[1]) if t % 2 == 0 else (pp[2], pp[3])
                        if it_ >= 2:
                            e.wait_ge(s_v.h, v3 + 2 * (it_ - 1))
                        for j in range(NJ_F):
                            st_, sp_ = (j == 0), (j == NJ_F - 1)
                            e.matmul(pa[:], hb[p][:, j * 128:(j + 1) * 128],
                                     dwh[:, j * 1024:j * 1024 + 512], start=st_, stop=sp_)
                            mi = e.matmul(pb[:], hb[p][:, j * 128:(j + 1) * 128],
                                          dwh[:, j * 1024 + 512:(j + 1) * 1024], start=st_, stop=sp_)
                            if sp_:
                                mi.then_inc(s_mm.h, 88)
                        s_mm.n += 88

            @b.vector
            def _(e):
                e.wait_ge(s_ms.h, ms4)
                for dh_ in range(2):
                    for t in range(NT):
                        it_ = dh_ * NT + t
                        p = it_ % 2
                        pa, pb = (pp[0], pp[1]) if t % 2 == 0 else (pp[2], pp[3])
                        e.wait_ge(s_mm.h, mm4 + 88 * (it_ + 1))
                        e.wait_ge(s_sg[p].h, sg3[p] + 16 * (it_ // 2))
                        e.tensor_tensor(stg[p][:, 0:512], pa[:], dsb[:, dh_ * 1024:dh_ * 1024 + 512], ALU.mult)
                        e.tensor_tensor(stg[p][:, 512:1024], pb[:], dsb[:, dh_ * 1024 + 512:(dh_ + 1) * 1024],
                                        ALU.mult).then_inc(s_v.h, 2)
                        s_v.n += 2

            @b.sync
            def _(e):
                for dh_ in range(2):
                    for t in range(NT):
                        it_ = dh_ * NT + t
                        p = it_ % 2
                        e.wait_ge(s_v.h, v3 + 2 * (it_ + 1))
                        e.dma_start(ffn0_out[t * 128:(t + 1) * 128, dh_ * 1024:(dh_ + 1) * 1024],
                                    stg[p][:]).then_inc(s_sg[p].h, 16)
                        s_sg[p].n += 16
                e.wait_ge(s_sg[0].h, s_sg[0].n)
                e.wait_ge(s_sg[1].h, s_sg[1].n)
                e.wait_ge(s_sb[0].h, s_sb[0].n)
                e.wait_ge(s_sb[1].h, s_sb[1].n)

    es.close()
    return nc


def host_inputs_l1(inputs):
    shared = dict(
        w_in_t=tile_wT(np.asarray(inputs["L0_attn_W_in"]).T, NJ_IN),
        w_dtT=f32(np.asarray(inputs["L0_attn_W_dt"]).T),
        bdt_pc=col_major_pc(inputs["L0_attn_b_dt"], NJ_DT),
        gw_t=tile_wT(np.asarray(inputs["L0_ffn_gate_w"]).T, NJ_F),
        uw_t=tile_wT(np.asarray(inputs["L0_ffn_up_w"]).T, NJ_F),
        dw_t=np.ascontiguousarray(
            f32(np.asarray(inputs["L0_ffn_down_w"]).T).reshape(NJ_F, 128, 2, 1024)
            .transpose(2, 1, 0, 3).reshape(2, 128, NJ_F * 1024)),
        gs_pc=col_major_pc(inputs["L0_ffn_gate_s"], NJ_F),
        us_pc=col_major_pc(inputs["L0_ffn_up_s"], NJ_F),
        ds_row=f32(np.asarray(inputs["L0_ffn_down_s"]).reshape(1, D)),
        ident=np.eye(128, dtype=BF),
    )
    x = np.asarray(inputs["L0_attn_hidden_states"])
    xt = x.reshape(NB, 2, TC, D)
    per_core = []
    for c in range(NCORE):
        m = dict(shared)
        m["x"] = f32(xt[c // 2, c % 2])
        per_core.append(m)
    return per_core



NCH = CSL // 128
NSEQ = NB
NTile = NCH * NSEQ


def build_launch2():
    nc = bass.Bass("TRN2", target_bir_lowering=False)
    xsrc_in = nc.dram_tensor("xsrc", [64, NB * L], BF16, kind="ExternalInput").ap()
    bm_in = nc.dram_tensor("bm", [64, NB * L], BF16, kind="ExternalInput").ap()
    cm_in = nc.dram_tensor("cm", [CSL, NB * L], BF16, kind="ExternalInput").ap()
    z_in = nc.dram_tensor("z", [CSL, NB * L], BF16, kind="ExternalInput").ap()
    dl_in = nc.dram_tensor("dl", [CSL, NB * L], F32, kind="ExternalInput").ap()
    alog_pc = nc.dram_tensor("alog_pc", [128, NCH * DST], F32, kind="ExternalInput").ap()
    cw_pc = nc.dram_tensor("cw_pc", [128, NCH * DCONV], F32, kind="ExternalInput").ap()
    cb_pc = nc.dram_tensor("cb_pc", [128, NCH], F32, kind="ExternalInput").ap()
    dp_pc = nc.dram_tensor("dp_pc", [128, NCH], F32, kind="ExternalInput").ap()
    eexp_in = nc.dram_tensor("eexp", [NCH, 64, 128], BF16, kind="ExternalInput").ap()
    ebc_in = nc.dram_tensor("ebc", [64, 64, 128], BF16, kind="ExternalInput").ap()
    scs_in = nc.dram_tensor("scs", [DST, 128, 128], BF16, kind="ExternalInput").ap()
    ident_in = nc.dram_tensor("ident", [128, 128], BF16, kind="ExternalInput").ap()

    y2_out = nc.dram_tensor("y2", [CSL, NB * L], BF16, kind="ExternalOutput").ap()

    es = ExitStack()
    xsrc = es.enter_context(nc.sbuf_tensor("xsrc_s", [64, NB * L], BF16))
    bm = es.enter_context(nc.sbuf_tensor("bm_s", [64, NB * L], BF16))
    apc = es.enter_context(nc.sbuf_tensor("apc", [128, NCH * DST], F32))
    atmp = es.enter_context(nc.sbuf_tensor("atmp", [128, NCH * DST], F32))
    cw = es.enter_context(nc.sbuf_tensor("cw", [128, NCH * DCONV], F32))
    cb = es.enter_context(nc.sbuf_tensor("cb", [128, NCH], F32))
    dp = es.enter_context(nc.sbuf_tensor("dp", [128, NCH], F32))
    eexp = es.enter_context(nc.sbuf_tensor("eexp_s", [64, NCH * 128], BF16))
    ebc = es.enter_context(nc.sbuf_tensor("ebc_s", [64, 64 * 128], BF16))
    scs = es.enter_context(nc.sbuf_tensor("scs_s", [128, DST * 128], BF16))
    identb = es.enter_context(nc.sbuf_tensor("identb", [128, 128], BF16))
    xe = es.enter_context(nc.sbuf_tensor("xe", [128, L + 3], F32))
    u_sb = es.enter_context(nc.sbuf_tensor("u_sb", [128, L], F32))
    du = es.enter_context(nc.sbuf_tensor("du", [128, L], F32))
    dls = [es.enter_context(nc.sbuf_tensor(f"dls{k}", [128, L], F32)) for k in range(2)]
    cms = [es.enter_context(nc.sbuf_tensor(f"cms{k}", [128, L], BF16)) for k in range(2)]
    zs = [es.enter_context(nc.sbuf_tensor(f"zs{k}", [128, L], BF16)) for k in range(2)]
    da = [es.enter_context(nc.sbuf_tensor(f"da{k}", [128, L], F32)) for k in range(2)]
    db = [es.enter_context(nc.sbuf_tensor(f"db{k}", [128, L], F32)) for k in range(2)]
    scn = [es.enter_context(nc.sbuf_tensor(f"scn{k}", [128, L], F32)) for k in range(2)]
    mb = [es.enter_context(nc.sbuf_tensor(f"mb{k}", [128, L], BF16)) for k in range(2)]
    szb = es.enter_context(nc.sbuf_tensor("szb", [128, L], F32))
    t1b = es.enter_context(nc.sbuf_tensor("t1b", [128, L], F32))
    y2st = [es.enter_context(nc.sbuf_tensor(f"y2st{k}", [128, L], BF16)) for k in range(2)]

    psA = es.enter_context(nc.psum_tensor("psA", [128, L], F32))
    py = es.enter_context(nc.psum_tensor("py", [128, L], F32))

    s_cst = Sem(es, nc, "s_cst")
    s_dl = [Sem(es, nc, f"s_dl{k}") for k in range(2)]
    s_cz = [Sem(es, nc, f"s_cz{k}") for k in range(2)]
    s_y2 = [Sem(es, nc, f"s_y2{k}") for k in range(2)]
    s_v = Sem(es, nc, "s_v")
    s_sc = Sem(es, nc, "s_sc")
    s_mm = Sem(es, nc, "s_mm")
    s_pad = Sem(es, nc, "s_pad")
    s_ax1 = Sem(es, nc, "s_ax1")
    s_ax2 = Sem(es, nc, "s_ax2")

    dl_marks = {}
    cz_marks = {}

    VT = 56
    MT = 49
    ST = 18

    with nc.Block() as b:
        @b.sync
        def _(e):
            e.dma_start(xsrc[:], xsrc_in).then_inc(s_cst.h, 16)
            e.dma_start(bm[:], bm_in).then_inc(s_cst.h, 16)
            e.dma_start(atmp[:], alog_pc).then_inc(s_cst.h, 16)
            e.dma_start(cw[:], cw_pc).then_inc(s_cst.h, 16)
            e.dma_start(cb[:], cb_pc).then_inc(s_cst.h, 16)
            e.dma_start(dp[:], dp_pc).then_inc(s_cst.h, 16)
            e.dma_start(eexp[:].rearrange("p (j o) -> p j o", j=NCH), eexp_in.rearrange("j p o -> p j o")).then_inc(s_cst.h, 16)
            e.dma_start(ebc[:].rearrange("p (j o) -> p j o", j=64), ebc_in.rearrange("j p o -> p j o")).then_inc(s_cst.h, 16)
            e.dma_start(scs[:].rearrange("p (j o) -> p j o", j=DST), scs_in.rearrange("j p o -> p j o")).then_inc(s_cst.h, 16)
            e.dma_start(identb[:], ident_in).then_inc(s_cst.h, 16)
            s_cst.n += 160
            for k in range(NTile):
                jj, bq = divmod(k, NSEQ)
                p = k % 2
                if k >= 2:
                    e.wait_ge(s_sc.h, ST * (k - 1))
                    e.wait_ge(s_mm.h, MT * (k - 1))
                e.dma_start(dls[p][:], dl_in[jj * 128:(jj + 1) * 128, bq * L:(bq + 1) * L]).then_inc(s_dl[p].h, 16)
                s_dl[p].n += 16
                dl_marks[k] = s_dl[p].n
                e.dma_start(cms[p][:], cm_in[jj * 128:(jj + 1) * 128, bq * L:(bq + 1) * L]).then_inc(s_cz[p].h, 16)
                e.dma_start(zs[p][:], z_in[jj * 128:(jj + 1) * 128, bq * L:(bq + 1) * L]).then_inc(s_cz[p].h, 16)
                s_cz[p].n += 32
                cz_marks[k] = s_cz[p].n

        @b.gpsimd
        def _(e):
            e.memset(xe[:, 0:3], 0.0).then_inc(s_pad.h, 1)
            s_pad.n += 1
            for k in range(NTile):
                jj, bq = divmod(k, NSEQ)
                p = k % 2
                e.wait_ge(s_v.h, VT * k + 56)
                e.dma_start(y2_out[jj * 128:(jj + 1) * 128, bq * L:(bq + 1) * L],
                            y2st[p][:]).then_inc(s_y2[p].h, 16)
                s_y2[p].n += 16
            e.wait_ge(s_y2[0].h, s_y2[0].n)
            e.wait_ge(s_y2[1].h, s_y2[1].n)

        @b.scalar
        def _(e):
            e.wait_ge(s_cst.h, 160)
            e.activation(apc[:], atmp[:], AF.Exp).then_inc(s_ax1.h, 1)
            s_ax1.n += 1
            for k in range(NTile):
                jj = k // NSEQ
                S0, V0 = ST * k, VT * k
                e.wait_ge(s_v.h, V0 + 5)
                e.activation(u_sb[:], u_sb[:], AF.Silu, bias=cb[:, jj:jj + 1]).then_inc(s_sc.h, 1)
                s_sc.n += 1
                e.wait_ge(s_ax2.h, 1)
                for n in range(DST):
                    if n >= 2:
                        e.wait_ge(s_v.h, V0 + 8 + 3 * (n - 2))
                    e.activation(da[n % 2][:], dls[k % 2][:], AF.Exp,
                                 scale=apc[:, jj * DST + n:jj * DST + n + 1]).then_inc(s_sc.h, 1)
                    s_sc.n += 1
                e.wait_ge(s_cz[k % 2].h, cz_marks[k])
                e.activation(szb[:], zs[k % 2][:], AF.Silu).then_inc(s_sc.h, 1)
                s_sc.n += 1

        @b.vector
        def _(e):
            e.wait_ge(s_ax1.h, 1)
            e.tensor_scalar_mul(apc[:], apc[:], -1.0).then_inc(s_ax2.h, 1)
            s_ax2.n += 1
            for k in range(NTile):
                jj = k // NSEQ
                p = k % 2
                S0, V0, M0 = ST * k, VT * k, MT * k
                e.wait_ge(s_mm.h, M0 + 1)
                for q in range(4):
                    ci = e.tensor_copy(xe[:, 3 + q * 512:3 + (q + 1) * 512], psA[:, q * 512:(q + 1) * 512])
                ci.then_inc(s_v.h, 1)
                e.wait_ge(s_v.h, V0 + 1)
                e.wait_ge(s_pad.h, 1)
                e.tensor_scalar(u_sb[:], xe[:, 3:3 + L], cw[:, jj * 4 + 3:jj * 4 + 4], None,
                                ALU.mult).then_inc(s_v.h, 1)
                for q in range(3):
                    e.wait_ge(s_v.h, V0 + 2 + q)
                    e.scalar_tensor_tensor(u_sb[:], xe[:, q:q + L], cw[:, jj * 4 + q:jj * 4 + q + 1],
                                           u_sb[:], ALU.mult, ALU.add).then_inc(s_v.h, 1)
                s_v.n += 5
                e.wait_ge(s_sc.h, S0 + 1)
                e.wait_ge(s_dl[p].h, dl_marks[k])
                e.tensor_tensor(du[:], dls[p][:], u_sb[:], ALU.mult).then_inc(s_v.h, 1)
                s_v.n += 1
                for n in range(DST):
                    e.wait_ge(s_mm.h, M0 + 2 + 3 * n)
                    if n == 0:
                        e.wait_ge(s_v.h, V0 + 6)
                    e.tensor_tensor(db[n % 2][:], psA[:], du[:], ALU.mult).then_inc(s_v.h, 1)
                    e.wait_ge(s_v.h, V0 + 7 + 3 * n)
                    e.wait_ge(s_sc.h, S0 + 2 + n)
                    e.tensor_tensor_scan(scn[n % 2][:], da[n % 2][:], db[n % 2][:], 0.0,
                                         ALU.mult, ALU.add).then_inc(s_v.h, 1)
                    e.wait_ge(s_v.h, V0 + 8 + 3 * n)
                    e.wait_ge(s_mm.h, M0 + 3 + 3 * n)
                    e.tensor_tensor(mb[n % 2][:], psA[:], scn[n % 2][:], ALU.mult).then_inc(s_v.h, 1)
                    s_v.n += 3
                e.wait_ge(s_mm.h, M0 + 4 + 3 * 15)
                e.scalar_tensor_tensor(t1b[:], u_sb[:], dp[:, jj:jj + 1], py[:], ALU.mult, ALU.add).then_inc(s_v.h, 1)
                e.wait_ge(s_v.h, V0 + 55)
                e.wait_ge(s_sc.h, S0 + 18)
                e.wait_ge(s_y2[p].h, 16 * (k // 2))
                e.tensor_tensor(y2st[p][:], t1b[:], szb[:], ALU.mult).then_inc(s_v.h, 1)
                s_v.n += 2

        @b.tensor
        def _(e):
            e.wait_ge(s_cst.h, 160)
            for k in range(NTile):
                jj, bq = divmod(k, NSEQ)
                S0, V0, M0 = ST * k, VT * k, MT * k
                if k >= 1:
                    e.wait_ge(s_v.h, V0)
                for q in range(4):
                    mi = e.matmul(psA[:, q * 512:(q + 1) * 512], eexp[:, jj * 128:(jj + 1) * 128],
                                  xsrc[:, bq * L + q * 512:bq * L + (q + 1) * 512], start=True, stop=True)
                mi.then_inc(s_mm.h, 1)
                s_mm.n += 1
                for n in range(DST):
                    if n == 0:
                        e.wait_ge(s_v.h, V0 + 1)
                    else:
                        e.wait_ge(s_v.h, V0 + 9 + 3 * (n - 1))
                    idx = jj * DST + n
                    for q in range(4):
                        mi = e.matmul(psA[:, q * 512:(q + 1) * 512], ebc[:, idx * 128:(idx + 1) * 128],
                                      bm[:, bq * L + q * 512:bq * L + (q + 1) * 512], start=True, stop=True)
                    mi.then_inc(s_mm.h, 1)
                    e.wait_ge(s_v.h, V0 + 7 + 3 * n)
                    e.wait_ge(s_cz[k % 2].h, cz_marks[k])
                    for q in range(4):
                        mi = e.matmul(psA[:, q * 512:(q + 1) * 512], scs[:, n * 128:(n + 1) * 128],
                                      cms[k % 2][:, q * 512:(q + 1) * 512], start=True, stop=True)
                    mi.then_inc(s_mm.h, 1)
                    e.wait_ge(s_v.h, V0 + 9 + 3 * n)
                    if n == 0 and k >= 1:
                        e.wait_ge(s_v.h, V0 + 1 - VT + 54)
                    for q in range(4):
                        mi = e.matmul(py[:, q * 512:(q + 1) * 512], identb[:],
                                      mb[n % 2][:, q * 512:(q + 1) * 512],
                                      start=(n == 0), stop=(n == DST - 1))
                    mi.then_inc(s_mm.h, 1)
                    s_mm.n += 3

    es.close()
    return nc


def host_inputs_l2(inputs, zx_all, delta_all):
    x_orig = zx_all[OFF_X:OFF_X + DXB]
    bm_g = zx_all[OFF_B:OFF_B + DXB]
    cm_g = zx_all[OFF_C:OFF_C + DI]
    z_g = zx_all[OFF_Z:DI]
    alog = np.asarray(inputs["L0_attn_A_log"], np.float32)
    cwf = np.asarray(inputs["L0_attn_conv_w"], np.float32).reshape(DI, DCONV)
    cbf = np.asarray(inputs["L0_attn_conv_b"], np.float32)
    dpf = np.asarray(inputs["L0_attn_D_param"], np.float32)

    eexp = np.zeros((NCH, 64, 128), dtype=BF)
    for j in range(NCH):
        for p_ in range(128):
            eexp[j, 16 * j + p_ % 16, p_] = 1
    ebc = np.zeros((64, 64, 128), dtype=BF)
    for i in range(64):
        ebc[i, i, :] = 1
    scs = np.zeros((DST, 128, 128), dtype=BF)
    for n in range(DST):
        for p_ in range(128):
            scs[n, 16 * (p_ // 16) + n, p_] = 1
    ident = np.eye(128, dtype=BF)

    per_core = []
    for c in range(NCORE):
        cs = slice(CSL * c, CSL * (c + 1))
        m = dict(
            xsrc=np.ascontiguousarray(x_orig[64 * c:64 * (c + 1)]),
            bm=np.ascontiguousarray(bm_g[64 * c:64 * (c + 1)]),
            cm=np.ascontiguousarray(cm_g[cs]),
            z=np.ascontiguousarray(z_g[cs]),
            dl=np.ascontiguousarray(delta_all[cs]),
            alog_pc=np.ascontiguousarray(alog[cs].reshape(NCH, 128, DST).transpose(1, 0, 2).reshape(128, NCH * DST)),
            cw_pc=np.ascontiguousarray(cwf[cs].reshape(NCH, 128, DCONV).transpose(1, 0, 2).reshape(128, NCH * DCONV)),
            cb_pc=col_major_pc(cbf[cs], NCH),
            dp_pc=col_major_pc(dpf[cs], NCH),
            eexp=eexp, ebc=ebc, scs=scs, ident=ident,
        )
        per_core.append(m)
    return per_core



def build_launch3():
    nc = bass.Bass("TRN2", target_bir_lowering=False)
    x_in = nc.dram_tensor("x", [TC, D], F32, kind="ExternalInput").ap()
    ffn0_in = nc.dram_tensor("ffn0", [TC, D], F32, kind="ExternalInput").ap()
    y2t_in = nc.dram_tensor("y2t", [128, (DI // 128) * TC], BF16, kind="ExternalInput").ap()
    wo_t = nc.dram_tensor("wo_t", [2, 128, (DI // 128) * 1024], F32, kind="ExternalInput").ap()
    wqk_t = nc.dram_tensor("wqk_t", [2 * NDC, 128, NDC * 128], F32, kind="ExternalInput").ap()
    wv_t = nc.dram_tensor("wv_t", [128, NDC * D], F32, kind="ExternalInput").ap()
    ident_in = nc.dram_tensor("ident", [128, 128], BF16, kind="ExternalInput").ap()

    x1_out = nc.dram_tensor("x1", [TC, D], F32, kind="ExternalOutput").ap()
    xn1_out = nc.dram_tensor("xn1", [D, TC], BF16, kind="ExternalOutput").ap()
    q_out = nc.dram_tensor("qf", [D, TC], BF16, kind="ExternalOutput").ap()
    k_out = nc.dram_tensor("kf", [D, TC], BF16, kind="ExternalOutput").ap()
    v_out = nc.dram_tensor("vt", [TC, D], BF16, kind="ExternalOutput").ap()

    NIY = DI // 128

    es = ExitStack()
    wbuf = [es.enter_context(nc.sbuf_tensor(f"wbuf{k}", [128, NDC * 128], BF16)) for k in range(2)]
    xts = [es.enter_context(nc.sbuf_tensor(f"xts{k}", [128, D], F32)) for k in range(2)]
    fts = [es.enter_context(nc.sbuf_tensor(f"fts{k}", [128, D], F32)) for k in range(2)]
    stg = [es.enter_context(nc.sbuf_tensor(f"stg{k}", [128, TC], F32)) for k in range(2)]
    stgb = [es.enter_context(nc.sbuf_tensor(f"stgb{k}", [128, TC], BF16)) for k in range(2)]
    x1ts = [es.enter_context(nc.sbuf_tensor(f"x1ts{k}", [128, D], F32)) for k in range(2)]
    ident = es.enter_context(nc.sbuf_tensor("identt", [128, 128], BF16))
    mcol = es.enter_context(nc.sbuf_tensor("mcol", [128, NT], F32))
    rcol = es.enter_context(nc.sbuf_tensor("rcol", [128, NT], F32))
    scr1 = es.enter_context(nc.sbuf_tensor("scr1", [128, NT], F32))
    scr2 = es.enter_context(nc.sbuf_tensor("scr2", [128, NT], F32))
    konst = es.enter_context(nc.sbuf_tensor("konst", [128, 2], F32))


    s_c = Sem(es, nc, "s_c")
    s_xl = [Sem(es, nc, f"s_xl{k}") for k in range(2)]
    s_wq = [Sem(es, nc, f"s_wq{k}") for k in range(2)]
    s_wb = Sem(es, nc, "s_wb")
    s_x1s = [Sem(es, nc, f"s_x1s{k}") for k in range(2)]
    s_x1l = [Sem(es, nc, f"s_x1l{k}") for k in range(2)]
    s_so = [Sem(es, nc, f"s_so{k}") for k in range(2)]
    s_sv = [Sem(es, nc, f"s_sv{k}") for k in range(2)]
    s_v = Sem(es, nc, "s_v")
    s_sc = Sem(es, nc, "s_sc")
    s_mm = Sem(es, nc, "s_mm")
    s_ms = Sem(es, nc, "s_ms")

    xl_marks = {}
    wq_marks = {}
    wb_marks = {}
    x1l_marks = {}

    es1 = ExitStack()
    y2s = es1.enter_context(nc.sbuf_tensor("y2s", [128, NIY * TC], BF16))
    woh = es1.enter_context(nc.sbuf_tensor("woh", [128, NIY * 1024], BF16))
    x1s_marks = {}
    with nc.Block() as b:
        @b.sync
        def _(e):
            e.dma_start(ident[:], ident_in).then_inc(s_c.h, 16)
            e.dma_start(y2s[:], y2t_in).then_inc(s_c.h, 16)
            s_c.n += 32
            for dh_ in range(2):
                for t in range(NT):
                    it_ = dh_ * NT + t
                    p = it_ % 2
                    if it_ >= 2:
                        e.wait_ge(s_v.h, 2 * (it_ - 1))
                    e.dma_start(xts[p][:, 0:1024],
                                x_in[t * 128:(t + 1) * 128, dh_ * 1024:(dh_ + 1) * 1024]).then_inc(s_xl[p].h, 16)
                    e.dma_start(fts[p][:, 0:1024],
                                ffn0_in[t * 128:(t + 1) * 128, dh_ * 1024:(dh_ + 1) * 1024]).then_inc(s_xl[p].h, 16)
                    s_xl[p].n += 32
                    xl_marks[it_] = s_xl[p].n

        @b.gpsimd
        def _(e):
            for dh_ in range(2):
                if dh_ == 1:
                    e.wait_ge(s_mm.h, NT)
                e.dma_start(woh[:], wo_t[dh_]).then_inc(s_wb.h, 16)
                s_wb.n += 16
                wb_marks[("wo", dh_)] = s_wb.n
                for t in range(NT):
                    it_ = dh_ * NT + t
                    p = it_ % 2
                    e.wait_ge(s_v.h, 2 * (it_ + 1))
                    e.dma_start(x1_out[t * 128:(t + 1) * 128, dh_ * 1024:(dh_ + 1) * 1024],
                                stg[p][:]).then_inc(s_x1s[p].h, 16)
                    s_x1s[p].n += 16
                    x1s_marks[it_] = s_x1s[p].n

        @b.tensor
        def _(e):
            e.wait_ge(s_c.h, 32)
            for dh_ in range(2):
                e.wait_ge(s_wb.h, wb_marks[("wo", dh_)])
                for t in range(NT):
                    it_ = dh_ * NT + t
                    pa, pb = (pp[0], pp[1]) if it_ % 2 == 0 else (pp[2], pp[3])
                    if it_ >= 2:
                        e.wait_ge(s_v.h, 2 * (it_ - 1))
                    for i in range(NIY):
                        st_, sp_ = (i == 0), (i == NIY - 1)
                        e.matmul(pa[:], y2s[:, i * TC + t * 128:i * TC + (t + 1) * 128],
                                 woh[:, i * 1024:i * 1024 + 512], start=st_, stop=sp_)
                        mi = e.matmul(pb[:], y2s[:, i * TC + t * 128:i * TC + (t + 1) * 128],
                                      woh[:, i * 1024 + 512:(i + 1) * 1024], start=st_, stop=sp_)
                        if sp_:
                            mi.then_inc(s_mm.h, 1)
                    s_mm.n += 1

        @b.vector
        def _(e):
            for dh_ in range(2):
                for t in range(NT):
                    it_ = dh_ * NT + t
                    p = it_ % 2
                    pa, pb = (pp[0], pp[1]) if p == 0 else (pp[2], pp[3])
                    e.wait_ge(s_xl[p].h, xl_marks[it_])
                    e.wait_ge(s_mm.h, it_ + 1)
                    if it_ >= 2:
                        e.wait_ge(s_x1s[p].h, 16 * (it_ // 2))
                    e.tensor_tensor(stg[p][:, 0:512], xts[p][:, 0:512], pa[:], ALU.add)
                    e.tensor_tensor(stg[p][:, 512:1024], xts[p][:, 512:1024], pb[:], ALU.add).then_inc(s_v.h, 1)
                    e.wait_ge(s_v.h, 2 * it_ + 1)
                    e.tensor_tensor(stg[p][:], stg[p][:], fts[p][:, 0:1024], ALU.add).then_inc(s_v.h, 1)
                    s_v.n += 2

    es1.close()
    v1 = s_v.n
    mm1 = s_mm.n

    es2 = ExitStack()
    wvs = es2.enter_context(nc.sbuf_tensor("wvs", [128, NDC * D], BF16))
    xnT = [es2.enter_context(nc.sbuf_tensor(f"xnT{j}", [128, TC], BF16)) for j in range(NDC)]
    sq = [es2.enter_context(nc.sbuf_tensor(f"sq{k}", [128, D], F32)) for k in range(2)]
    xnb = [es2.enter_context(nc.sbuf_tensor(f"xnb{k}", [128, D], BF16)) for k in range(2)]
    with nc.Block() as b:
        @b.sync
        def _(e):
            for i in range(NT):
                p = i % 2
                e.wait_ge(s_x1s[i % 2].h, x1s_marks[i])
                e.wait_ge(s_x1s[(NT + i) % 2].h, x1s_marks[NT + i])
                if i < 2:
                    e.wait_ge(s_v.h, v1)
                if i >= 2:
                    e.wait_ge(s_v.h, v1 + 22 * (i - 1))
                    e.wait_ge(s_sc.h, 3 * (i - 1))
                e.dma_start(xts[p][:], x1_out[i * 128:(i + 1) * 128, :]).then_inc(s_x1l[p].h, 16)
                s_x1l[p].n += 16
                x1l_marks[i] = s_x1l[p].n

        @b.gpsimd
        def _(e):
            e.memset(konst[:, 0:1], EPS)
            e.memset(konst[:, 1:2], -0.5).then_inc(s_ms.h, 1)
            s_ms.n += 1
            e.dma_start(wbuf[0][:], wqk_t[0]).then_inc(s_wq[0].h, 16)
            s_wq[0].n += 16
            wq_marks[("qk", 0)] = (0, s_wq[0].n)
            e.dma_start(wvs[:], wv_t).then_inc(s_wb.h, 16)
            s_wb.n += 16
            wb_marks["wv"] = s_wb.n

        @b.scalar
        def _(e):
            e.wait_ge(s_ms.h, 1)
            for i in range(NT):
                p = i % 2
                e.wait_ge(s_x1l[p].h, x1l_marks[i])
                e.square(sq[p][:], xts[p][:]).then_inc(s_sc.h, 1)
                e.wait_ge(s_v.h, v1 + 22 * i + 5)
                e.activation(scr1[:, i:i + 1], scr2[:, i:i + 1], AF.Ln, bias=konst[:, 0:1]).then_inc(s_sc.h, 1)
                e.wait_ge(s_sc.h, 3 * i + 2)
                e.activation(rcol[:, i:i + 1], scr1[:, i:i + 1], AF.Exp, scale=konst[:, 1:2]).then_inc(s_sc.h, 1)
                s_sc.n += 3

        @b.vector
        def _(e):
            for i in range(NT):
                p = i % 2
                e.wait_ge(s_x1l[p].h, x1l_marks[i])
                e.reduce_sum(scr1[:, i:i + 1], xts[p][:], axis=AX.X).then_inc(s_v.h, 1)
                e.wait_ge(s_v.h, v1 + 22 * i + 1)
                e.tensor_scalar_mul(mcol[:, i:i + 1], scr1[:, i:i + 1], 1.0 / D).then_inc(s_v.h, 1)
                e.wait_ge(s_sc.h, 3 * i + 1)
                e.wait_ge(s_v.h, v1 + 22 * i + 2)
                e.reduce_sum(scr2[:, i:i + 1], sq[p][:], axis=AX.X).then_inc(s_v.h, 1)
                e.wait_ge(s_v.h, v1 + 22 * i + 3)
                e.tensor_scalar_mul(scr2[:, i:i + 1], scr2[:, i:i + 1], 1.0 / D)
                e.tensor_mul(scr1[:, i:i + 1], mcol[:, i:i + 1], mcol[:, i:i + 1]).then_inc(s_v.h, 1)
                e.wait_ge(s_v.h, v1 + 22 * i + 4)
                e.tensor_sub(scr2[:, i:i + 1], scr2[:, i:i + 1], scr1[:, i:i + 1]).then_inc(s_v.h, 1)
                e.wait_ge(s_sc.h, 3 * i + 3)
                if i >= 2:
                    e.wait_ge(s_mm.h, mm1 + 16 * (i - 1))
                e.tensor_scalar(xnb[p][:], xts[p][:], mcol[:, i:i + 1], rcol[:, i:i + 1],
                                ALU.subtract, ALU.mult).then_inc(s_v.h, 1)
                s_v.n += 6
                for j in range(NDC):
                    e.wait_ge(s_mm.h, mm1 + 16 * i + j + 1)
                    e.tensor_copy(xnT[j][:, i * 128:(i + 1) * 128],
                                  ptp[(16 * i + j) % 2][:, 0:128]).then_inc(s_v.h, 1)
                    s_v.n += 1

        @b.tensor
        def _(e):
            for i in range(NT):
                e.wait_ge(s_v.h, v1 + 22 * i + 6)
                for j in range(NDC):
                    k = 16 * i + j
                    if k >= 2:
                        i2, j2 = divmod(k - 2, 16)
                        e.wait_ge(s_v.h, v1 + 22 * i2 + 6 + j2 + 1)
                    e.transpose(ptp[k % 2][:, 0:128], xnb[i % 2][:, j * 128:(j + 1) * 128],
                                ident[:]).then_inc(s_mm.h, 1)
                    s_mm.n += 1

    v2 = s_v.n
    mm2 = s_mm.n
    sc2 = s_sc.n

    NQK = 2 * NDC
    with nc.Block() as b:
        @b.gpsimd
        def _(e):
            for j in range(1, NQK):
                p = j % 2
                if j >= 2:
                    e.wait_ge(s_mm.h, mm2 + 32 * (j - 1))
                e.dma_start(wbuf[p][:], wqk_t[j]).then_inc(s_wq[p].h, 16)
                s_wq[p].n += 16
                wq_marks[("qk", j)] = (p, s_wq[p].n)

        @b.tensor
        def _(e):
            e.wait_ge(s_v.h, v2)
            for j in range(NQK):
                p, mark = wq_marks[("qk", j)]
                e.wait_ge(s_wq[p].h, mark)
                pa, pb = (pp[0], pp[1]) if j % 2 == 0 else (pp[2], pp[3])
                if j >= 2:
                    e.wait_ge(s_v.h, v2 + 2 * (j - 1))
                for i in range(NDC):
                    st_, sp_ = (i == 0), (i == NDC - 1)
                    e.matmul(pa[:], wbuf[p][:, i * 128:(i + 1) * 128], xnT[i][:, 0:512],
                             start=st_, stop=sp_)
                    mi = e.matmul(pb[:], wbuf[p][:, i * 128:(i + 1) * 128], xnT[i][:, 512:1024],
                                  start=st_, stop=sp_)
                    if sp_:
                        mi.then_inc(s_mm.h, 32)
                s_mm.n += 32
            e.wait_ge(s_wb.h, wb_marks["wv"])
            for t in range(NT):
                it_ = NQK + t
                pa, pb = (pp[0], pp[1]) if t % 2 == 0 else (pp[2], pp[3])
                e.wait_ge(s_v.h, v2 + 2 * (it_ - 1))
                for i in range(NDC):
                    st_, sp_ = (i == 0), (i == NDC - 1)
                    e.matmul(pa[:], xnT[i][:, t * 128:(t + 1) * 128],
                             wvs[:, i * D:i * D + 512], start=st_, stop=sp_)
                    mi = e.matmul(pb[:], xnT[i][:, t * 128:(t + 1) * 128],
                                  wvs[:, i * D + 512:i * D + 1024], start=st_, stop=sp_)
                    if sp_:
                        mi.then_inc(s_mm.h, 32)
                s_mm.n += 32
            for t in range(NT):
                it_ = NQK + NT + t
                pa, pb = (pp[0], pp[1]) if t % 2 == 0 else (pp[2], pp[3])
                e.wait_ge(s_v.h, v2 + 2 * (it_ - 1))
                for i in range(NDC):
                    st_, sp_ = (i == 0), (i == NDC - 1)
                    e.matmul(pa[:], xnT[i][:, t * 128:(t + 1) * 128],
                             wvs[:, i * D + 1024:i * D + 1536], start=st_, stop=sp_)
                    mi = e.matmul(pb[:], xnT[i][:, t * 128:(t + 1) * 128],
                                  wvs[:, i * D + 1536:(i + 1) * D], start=st_, stop=sp_)
                    if sp_:
                        mi.then_inc(s_mm.h, 32)
                s_mm.n += 32

        @b.vector
        def _(e):
            NTOT = NQK + 2 * NT
            for k in range(NTOT):
                p = k % 2
                pa, pb = (pp[0], pp[1]) if p == 0 else (pp[2], pp[3])
                e.wait_ge(s_mm.h, mm2 + 32 * (k + 1))
                e.wait_ge(s_so[p].h, 16 * (k // 2))
                if k < NDC:
                    e.tensor_scalar_mul(stgb[p][:, 0:512], pa[:], SCALE)
                    e.tensor_scalar_mul(stgb[p][:, 512:1024], pb[:], SCALE).then_inc(s_v.h, 2)
                else:
                    e.tensor_copy(stgb[p][:, 0:512], pa[:])
                    e.tensor_copy(stgb[p][:, 512:1024], pb[:]).then_inc(s_v.h, 2)
                s_v.n += 2

        @b.scalar
        def _(e):
            NTOT = NQK + 2 * NT
            for k in range(NTOT):
                p = k % 2
                e.wait_ge(s_v.h, v2 + 2 * (k + 1))
                if k < NDC:
                    dst = q_out[k * 128:(k + 1) * 128, :]
                elif k < NQK:
                    dst = k_out[(k - NDC) * 128:(k - NDC + 1) * 128, :]
                elif k < NQK + NT:
                    t = k - NQK
                    dst = v_out[t * 128:(t + 1) * 128, 0:1024]
                else:
                    t = k - NQK - NT
                    dst = v_out[t * 128:(t + 1) * 128, 1024:2048]
                e.dma_start(dst, stgb[p][:]).then_inc(s_so[p].h, 16)
                s_so[p].n += 16
            for j in range(NDC):
                pq = j % 2
                e.wait_ge(s_v.h, v2)
                e.dma_start(xn1_out[j * 128:(j + 1) * 128, :], xnT[j][:]).then_inc(s_sv[pq].h, 16)
                s_sv[pq].n += 16
            e.wait_ge(s_so[0].h, s_so[0].n)
            e.wait_ge(s_so[1].h, s_so[1].n)
            e.wait_ge(s_sv[0].h, s_sv[0].n)
            e.wait_ge(s_sv[1].h, s_sv[1].n)
            e.wait_ge(s_x1s[0].h, s_x1s[0].n)
            e.wait_ge(s_x1s[1].h, s_x1s[1].n)

    es2.close()
    es.close()
    return nc


def host_inputs_l3(inputs, x_full, ffn0_all, y2_all):
    woT = f32(np.asarray(inputs["L0_attn_W_out"]).T)
    wo_tiled = np.ascontiguousarray(
        woT.reshape(32, 128, 2, 1024).transpose(2, 1, 0, 3).reshape(2, 128, 32 * 1024))
    wqkv = np.asarray(inputs["L1_attn_w_qkv"])
    wqkT = f32(wqkv[0:2 * D].T)
    wqk_tiled = tile_wT(wqkT, 2 * NDC)
    wvT = f32(wqkv[2 * D:3 * D].T)
    wv_tiled = np.ascontiguousarray(
        wvT.reshape(NDC, 128, D).transpose(1, 0, 2).reshape(128, NDC * D))
    ident = np.eye(128, dtype=BF)
    per_core = []
    for c in range(NCORE):
        ts = slice(TC * c, TC * (c + 1))
        y2c = np.ascontiguousarray(y2_all[:, ts])
        y2tile = np.ascontiguousarray(
            y2c.reshape(32, 128, TC).transpose(1, 0, 2).reshape(128, 32 * TC))
        m = dict(
            x=np.ascontiguousarray(x_full[ts]),
            ffn0=np.ascontiguousarray(ffn0_all[ts]),
            y2t=y2tile,
            wo_t=wo_tiled, wqk_t=wqk_tiled, wv_t=wv_tiled, ident=ident,
        )
        per_core.append(m)
    return per_core



NJ1 = FI1 // 128
NKV = 12


def tile_wT_i32(wT, nj):
    a = np.ascontiguousarray(np.asarray(wT, dtype=np.int32))
    ni = a.shape[0] // 128
    a = a.reshape(ni, 128, nj, 128).transpose(2, 1, 0, 3).reshape(nj, 128, ni * 128)
    return np.ascontiguousarray(a)


def build_launch4():
    nc = bass.Bass("TRN2", target_bir_lowering=False)
    qt_in = nc.dram_tensor("qt", [128, NH * TC], BF16, kind="ExternalInput").ap()
    kxt_in = nc.dram_tensor("kxt", [128, NH * 1536], BF16, kind="ExternalInput").ap()
    vxt_in = nc.dram_tensor("vxt", [128, NKV * NH * 128], BF16, kind="ExternalInput").ap()
    msk_in = nc.dram_tensor("msk", [128, 8 * 640], BF16, kind="ExternalInput").ap()
    x1_in = nc.dram_tensor("x1", [TC, D], F32, kind="ExternalInput").ap()
    xn1t_in = nc.dram_tensor("xn1t", [128, NDC * TC], BF16, kind="ExternalInput").ap()
    g1_t = nc.dram_tensor("g1_t", [NJ1, 128, NDC * 128], I32, kind="ExternalInput").ap()
    u1_t = nc.dram_tensor("u1_t", [NJ1, 128, NDC * 128], I32, kind="ExternalInput").ap()
    d1_t = nc.dram_tensor("d1_t", [8, 128, NJ1 * 256], I32, kind="ExternalInput").ap()
    gs1_pc = nc.dram_tensor("gs1_pc", [128, NJ1], F32, kind="ExternalInput").ap()
    us1_pc = nc.dram_tensor("us1_pc", [128, NJ1], F32, kind="ExternalInput").ap()
    ds1_row = nc.dram_tensor("ds1_row", [1, D], F32, kind="ExternalInput").ap()
    wo1_t = nc.dram_tensor("wo1_t", [2, 128, NDC * 1024], F32, kind="ExternalInput").ap()
    ident_in = nc.dram_tensor("ident", [128, 128], BF16, kind="ExternalInput").ap()

    out_f = nc.dram_tensor("out", [TC, D], F32, kind="ExternalOutput").ap()
    ffn1_scr = nc.dram_tensor("ffn1_scr", [TC, D], F32, kind="Internal").ap()

    es = ExitStack()
    stgb = [es.enter_context(nc.sbuf_tensor(f"stgb{k}", [128, TC], BF16)) for k in range(2)]
    sgst = [es.enter_context(nc.sbuf_tensor(f"sgst{k}", [128, TC], BF16)) for k in range(2)]
    onescol = es.enter_context(nc.sbuf_tensor("onescol", [128, 1], BF16))
    negones = es.enter_context(nc.sbuf_tensor("negones", [1, 128], BF16))
    ones1r = es.enter_context(nc.sbuf_tensor("ones1r", [1, 128], BF16))
    gscol = es.enter_context(nc.sbuf_tensor("gscol", [128, NJ1], F32))
    uscol = es.enter_context(nc.sbuf_tensor("uscol", [128, NJ1], F32))
    ds1row_sb = es.enter_context(nc.sbuf_tensor("ds1row_sb", [1, D], BF16))
    Rb = es.enter_context(nc.sbuf_tensor("Rb", [1, TC], BF16))
    identt = es.enter_context(nc.sbuf_tensor("identt", [128, 128], BF16))
    R2c = es.enter_context(nc.sbuf_tensor("R2c", [128, NT], F32))
    mR2c = es.enter_context(nc.sbuf_tensor("mR2c", [128, NT], F32))

    s_c = Sem(es, nc, "s_c")
    s_cg = Sem(es, nc, "s_cg")
    s_wq = [Sem(es, nc, f"s_wq{k}") for k in range(2)]
    s_uq = [Sem(es, nc, f"s_uq{k}") for k in range(2)]
    s_wb = Sem(es, nc, "s_wb")
    s_fs = [Sem(es, nc, f"s_fs{k}") for k in range(2)]
    s_xl = [Sem(es, nc, f"s_xl{k}") for k in range(2)]
    s_os = [Sem(es, nc, f"s_os{k}") for k in range(2)]
    s_v = Sem(es, nc, "s_v")
    s_sc = Sem(es, nc, "s_sc")
    s_mm = Sem(es, nc, "s_mm")
    s_ms = Sem(es, nc, "s_ms")

    wq_marks = {}
    uq_marks = {}
    wb_marks = {}
    xl_marks = {}
    fs_marks = {}

    esA = ExitStack()
    h_all = [esA.enter_context(nc.sbuf_tensor(f"h{j}", [128, TC], BF16)) for j in range(NJ1)]
    ppf = [esA.enter_context(nc.psum_tensor(f"ppf{k}", [128, 512], F32)) for k in range(4)]
    pR = esA.enter_context(nc.psum_tensor("pR", [1, TC], F32))
    pRc = esA.enter_context(nc.psum_tensor("pRc", [128, 1], F32))

    esB = ExitStack()
    xn1T = esB.enter_context(nc.sbuf_tensor("xn1T", [128, NDC * TC], BF16))
    wbg = [esB.enter_context(nc.sbuf_tensor(f"wbg{k}", [128, NDC * 128], BF16)) for k in range(2)]
    wbu = [esB.enter_context(nc.sbuf_tensor(f"wbu{k}", [128, NDC * 128], BF16)) for k in range(2)]

    with nc.Block() as b:
        @b.sync
        def _(e):
            e.dma_start(xn1T[:], xn1t_in).then_inc(s_c.h, 16)
            e.dma_start(gscol[:], gs1_pc).then_inc(s_c.h, 16)
            e.dma_start(uscol[:], us1_pc).then_inc(s_c.h, 16)
            e.dma_start(identt[:], ident_in).then_inc(s_c.h, 16)
            s_c.n += 64

        @b.gpsimd
        def _(e):
            e.memset(onescol[:], 1.0)
            e.memset(negones[:], -128.0)
            e.memset(ones1r[:], 1.0)
            e.dma_start(ds1row_sb[:], ds1_row).then_inc(s_cg.h, 16)
            s_cg.n += 16
            for j in range(NJ1):
                pj = j % 2
                if j >= 2:
                    e.wait_ge(s_mm.h, 1 + (j - 1))
                e.dma_start(wbg[pj][:], g1_t[j]).then_inc(s_wq[pj].h, 16)
                s_wq[pj].n += 16
                wq_marks[j] = s_wq[pj].n
                e.dma_start(wbu[pj][:], u1_t[j]).then_inc(s_uq[pj].h, 16)
                s_uq[pj].n += 16
                uq_marks[j] = s_uq[pj].n

        @b.tensor
        def _(e):
            e.wait_ge(s_c.h, 64)
            e.wait_ge(s_cg.h, 16)
            for i in range(NDC):
                st_, sp_ = (i == 0), (i == NDC - 1)
                e.matmul(pR[:, 0:512], onescol[:], xn1T[:, i * TC:i * TC + 512], start=st_, stop=sp_)
                mi = e.matmul(pR[:, 512:1024], onescol[:], xn1T[:, i * TC + 512:(i + 1) * TC], start=st_, stop=sp_)
                if sp_:
                    mi.then_inc(s_mm.h, 1)
            s_mm.n += 1
            for j in range(NJ1):
                pj = j % 2
                e.wait_ge(s_wq[pj].h, wq_marks[j])
                e.wait_ge(s_uq[pj].h, uq_marks[j])
                e.wait_ge(s_ms.h, 1)
                if j >= 1:
                    e.wait_ge(s_sc.h, 2 * j)
                    e.wait_ge(s_v.h, 2 * j + 1)
                for i in range(NDC):
                    st_ = (i == 0)
                    e.matmul(ppf[0][:], wbg[pj][:, i * 128:(i + 1) * 128], xn1T[:, i * TC:i * TC + 512],
                             start=st_, stop=False)
                    e.matmul(ppf[1][:], wbg[pj][:, i * 128:(i + 1) * 128], xn1T[:, i * TC + 512:(i + 1) * TC],
                             start=st_, stop=False)
                    e.matmul(ppf[2][:], wbu[pj][:, i * 128:(i + 1) * 128], xn1T[:, i * TC:i * TC + 512],
                             start=st_, stop=False)
                    e.matmul(ppf[3][:], wbu[pj][:, i * 128:(i + 1) * 128], xn1T[:, i * TC + 512:(i + 1) * TC],
                             start=st_, stop=False)
                e.matmul(ppf[0][:], negones[:], Rb[:, 0:512], start=False, stop=True)
                e.matmul(ppf[1][:], negones[:], Rb[:, 512:1024], start=False, stop=True)
                e.matmul(ppf[2][:], negones[:], Rb[:, 0:512], start=False, stop=True)
                mi = e.matmul(ppf[3][:], negones[:], Rb[:, 512:1024], start=False, stop=True)
                mi.then_inc(s_mm.h, 1)
                s_mm.n += 1

        @b.vector
        def _(e):
            e.wait_ge(s_mm.h, 1)
            e.tensor_copy(Rb[:, :], pR[:, :]).then_inc(s_ms.h, 1)
            s_ms.n += 1
            s_v.n += 1
            e.nop().then_inc(s_v.h, 1)
            for j in range(NJ1):
                e.wait_ge(s_sc.h, 2 * (j + 1))
                e.scalar_tensor_tensor(h_all[j][:, 0:512], ppf[2][:], uscol[:, j:j + 1],
                                       sgst[j % 2][:, 0:512], ALU.mult, ALU.mult).then_inc(s_v.h, 1)
                e.scalar_tensor_tensor(h_all[j][:, 512:1024], ppf[3][:], uscol[:, j:j + 1],
                                       sgst[j % 2][:, 512:1024], ALU.mult, ALU.mult).then_inc(s_v.h, 1)
                s_v.n += 2

        @b.scalar
        def _(e):
            for j in range(NJ1):
                e.wait_ge(s_mm.h, 2 + j)
                if j >= 2:
                    e.wait_ge(s_v.h, 1 + 2 * (j - 1))
                e.activation(sgst[j % 2][:, 0:512], ppf[0][:], AF.Silu, scale=gscol[:, j:j + 1]).then_inc(s_sc.h, 1)
                e.activation(sgst[j % 2][:, 512:1024], ppf[1][:], AF.Silu, scale=gscol[:, j:j + 1]).then_inc(s_sc.h, 1)
                s_sc.n += 2

    esB.close()
    mmF = s_mm.n
    vF = s_v.n
    scF = s_sc.n
    msF = s_ms.n

    esC = ExitStack()
    dwq = esC.enter_context(nc.sbuf_tensor("dwq", [128, NJ1 * 256], BF16))
    ds1b = esC.enter_context(nc.sbuf_tensor("ds1b", [128, D], F32))
    with nc.Block() as b:
        @b.tensor
        def _(e):
            e.wait_ge(s_v.h, vF)
            e.wait_ge(s_sc.h, scF)
            for t in range(NT):
                if t >= 1:
                    e.wait_ge(s_v.h, vF + t)
                for j in range(NJ1):
                    mi = e.matmul(pRc[:], h_all[j][:, t * 128:(t + 1) * 128], onescol[:],
                                  start=(j == 0), stop=(j == NJ1 - 1))
                mi.then_inc(s_mm.h, 1)
                s_mm.n += 1
            for q in range(4):
                mi = e.matmul(ppf[q][:], ones1r[:], ds1row_sb[:, q * 512:(q + 1) * 512], start=True, stop=True)
            mi.then_inc(s_mm.h, 1)
            s_mm.n += 1

        @b.vector
        def _(e):
            for t in range(NT):
                e.wait_ge(s_mm.h, mmF + t + 1)
                e.tensor_copy(R2c[:, t:t + 1], pRc[:]).then_inc(s_v.h, 1)
                s_v.n += 1
            e.wait_ge(s_v.h, vF + NT)
            e.tensor_scalar_mul(mR2c[:], R2c[:], -128.0).then_inc(s_ms.h, 1)
            s_ms.n += 1
            e.wait_ge(s_mm.h, mmF + NT + 1)
            for q in range(4):
                ci = e.tensor_copy(ds1b[:, q * 512:(q + 1) * 512], ppf[q][:])
            ci.then_inc(s_ms.h, 1)
            s_ms.n += 1

    mmR = s_mm.n
    vR = s_v.n

    with nc.Block() as b:
        @b.gpsimd
        def _(e):
            e.wait_ge(s_mm.h, mmF)
            for dq in range(8):
                if dq >= 1:
                    e.wait_ge(s_mm.h, mmR + NT * dq)
                e.dma_start(dwq[:], d1_t[dq]).then_inc(s_wb.h, 16)
                s_wb.n += 16
                wb_marks[("d1", dq)] = s_wb.n
                for t in range(NT):
                    it_ = dq * NT + t
                    p = it_ % 2
                    e.wait_ge(s_v.h, vR + it_ + 1)
                    e.dma_start(ffn1_scr[t * 128:(t + 1) * 128, dq * 256:(dq + 1) * 256],
                                stgb[p][:, 0:256]).then_inc(s_fs[p].h, 16)
                    s_fs[p].n += 16
                    fs_marks[it_] = s_fs[p].n

        @b.tensor
        def _(e):
            e.wait_ge(s_ms.h, msF + 2)
            for dq in range(8):
                e.wait_ge(s_wb.h, wb_marks[("d1", dq)])
                for t in range(NT):
                    it_ = dq * NT + t
                    pf = ppf[it_ % 2]
                    if it_ >= 2:
                        e.wait_ge(s_v.h, vR + (it_ - 1))
                    for j in range(NJ1):
                        mi = e.matmul(pf[:, 0:256], h_all[j][:, t * 128:(t + 1) * 128],
                                      dwq[:, j * 256:(j + 1) * 256], start=(j == 0), stop=(j == NJ1 - 1))
                    mi.then_inc(s_mm.h, 1)
                    s_mm.n += 1

        @b.vector
        def _(e):
            for dq in range(8):
                for t in range(NT):
                    it_ = dq * NT + t
                    p = it_ % 2
                    pf = ppf[it_ % 2]
                    e.wait_ge(s_mm.h, mmR + it_ + 1)
                    if it_ >= 2:
                        e.wait_ge(s_fs[p].h, fs_marks[it_ - 2])
                    e.scalar_tensor_tensor(stgb[p][:, 0:256], pf[:, 0:256], mR2c[:, t:t + 1],
                                           ds1b[:, dq * 256:(dq + 1) * 256], ALU.add, ALU.mult).then_inc(s_v.h, 1)
                    s_v.n += 1

    esC.close()
    esA.close()
    mmD = s_mm.n
    vD = s_v.n

    o_sb = es.enter_context(nc.sbuf_tensor("o_sb", [128, NH * TC], BF16))
    esD = ExitStack()
    qts = esD.enter_context(nc.sbuf_tensor("qts", [128, NH * TC], BF16))
    kxs = esD.enter_context(nc.sbuf_tensor("kxs", [128, NH * 1536], BF16))
    vxs = esD.enter_context(nc.sbuf_tensor("vxs", [128, NKV * NH * 128], BF16))
    msks = esD.enter_context(nc.sbuf_tensor("msks", [128, 8 * 640], BF16))
    ssb = [esD.enter_context(nc.sbuf_tensor(f"ssb{k}", [128, 640], F32)) for k in range(2)]
    pbb = [esD.enter_context(nc.sbuf_tensor(f"pbb{k}", [128, 640], BF16)) for k in range(2)]
    pnb = [esD.enter_context(nc.sbuf_tensor(f"pnb{k}", [128, 640], BF16)) for k in range(2)]
    ptsb = [esD.enter_context(nc.sbuf_tensor(f"ptsb{k}", [128, 640], BF16)) for k in range(2)]
    mxc = esD.enter_context(nc.sbuf_tensor("mxc", [128, 2], F32))
    smc = esD.enter_context(nc.sbuf_tensor("smc", [128, 2], F32))
    rcc = esD.enter_context(nc.sbuf_tensor("rcc", [128, 2], F32))
    psc = [esD.enter_context(nc.psum_tensor(f"psc{k}", [128, 1024], F32)) for k in range(2)]
    ptt = [esD.enter_context(nc.psum_tensor(f"ptt{k}", [128, 1024], BF16)) for k in range(2)]
    pov = [esD.enter_context(nc.psum_tensor(f"pov{k}", [128, 512], F32)) for k in range(2)]

    NIT = NH * 8
    with nc.Block() as b:
        @b.sync
        def _(e):
            e.wait_ge(s_v.h, vD)
            e.dma_start(qts[:], qt_in).then_inc(s_c.h, 16)
            e.dma_start(kxs[:], kxt_in).then_inc(s_c.h, 16)
            e.dma_start(vxs[:], vxt_in).then_inc(s_c.h, 16)
            e.dma_start(msks[:], msk_in).then_inc(s_c.h, 16)
            s_c.n += 64

        @b.tensor
        def _(e):
            e.wait_ge(s_c.h, 64 + 64)
            e.wait_ge(s_v.h, vD)
            for it in range(NIT):
                h_, qb = divmod(it, 8)
                p = it % 2
                if it >= 2:
                    e.wait_ge(s_v.h, vD + 6 * (it - 2) + 1)
                e.matmul(psc[p][:, 0:512], qts[:, h_ * TC + qb * 128:h_ * TC + (qb + 1) * 128],
                         kxs[:, h_ * 1536 + qb * 128:h_ * 1536 + qb * 128 + 512], start=True, stop=True)
                mi = e.matmul(psc[p][:, 512:640], qts[:, h_ * TC + qb * 128:h_ * TC + (qb + 1) * 128],
                              kxs[:, h_ * 1536 + qb * 128 + 512:h_ * 1536 + qb * 128 + 640],
                              start=True, stop=True)
                mi.then_inc(s_mm.h, 1)
                e.wait_ge(s_v.h, vD + 6 * it + 4)
                if it >= 2:
                    e.wait_ge(s_v.h, vD + 6 * (it - 2) + 5)
                for kb in range(5):
                    mi = e.transpose(ptt[p][:, kb * 128:(kb + 1) * 128],
                                     pnb[p][:, kb * 128:(kb + 1) * 128], identt[:])
                mi.then_inc(s_mm.h, 1)
                e.wait_ge(s_v.h, vD + 6 * it + 5)
                if it >= 2:
                    e.wait_ge(s_v.h, vD + 6 * (it - 2) + 6)
                for kb in range(5):
                    mi = e.matmul(pov[p][:, 0:128],
                                  vxs[:, ((qb + kb) * NH + h_) * 128:((qb + kb) * NH + h_ + 1) * 128],
                                  ptsb[p][:, kb * 128:(kb + 1) * 128], start=(kb == 0), stop=(kb == 4))
                mi.then_inc(s_mm.h, 1)
                s_mm.n += 3

        @b.vector
        def _(e):
            for it in range(NIT):
                h_, qb = divmod(it, 8)
                p = it % 2
                e.wait_ge(s_mm.h, mmD + 3 * it + 1)
                e.tensor_tensor(ssb[p][:, 0:640], psc[p][:, 0:640], msks[:, qb * 640:(qb + 1) * 640],
                                ALU.add).then_inc(s_v.h, 1)
                e.wait_ge(s_v.h, vD + 6 * it + 1)
                e.reduce_max(mxc[:, p:p + 1], ssb[p][:, 0:640], axis=AX.X, negate=True).then_inc(s_v.h, 1)
                e.wait_ge(s_sc.h, scF + it + 1)
                e.reciprocal(rcc[:, p:p + 1], smc[:, p:p + 1]).then_inc(s_v.h, 1)
                e.wait_ge(s_v.h, vD + 6 * it + 3)
                e.tensor_scalar_mul(pnb[p][:, 0:640], pbb[p][:, 0:640], rcc[:, p:p + 1]).then_inc(s_v.h, 1)
                e.wait_ge(s_mm.h, mmD + 3 * it + 2)
                e.tensor_copy(ptsb[p][:, 0:640], ptt[p][:, 0:640]).then_inc(s_v.h, 1)
                e.wait_ge(s_mm.h, mmD + 3 * it + 3)
                e.tensor_copy(o_sb[:, h_ * TC + qb * 128:h_ * TC + (qb + 1) * 128],
                              pov[p][:, 0:128]).then_inc(s_v.h, 1)
                s_v.n += 6

        @b.scalar
        def _(e):
            for it in range(NIT):
                p = it % 2
                e.wait_ge(s_v.h, vD + 6 * it + 2)
                e.activation(pbb[p][:, 0:640], ssb[p][:, 0:640], AF.Exp, bias=mxc[:, p:p + 1],
                             accum_out=smc[:, p:p + 1]).then_inc(s_sc.h, 1)
                s_sc.n += 1

    esD.close()
    mmA = s_mm.n
    vA = s_v.n

    esE = ExitStack()
    stg = [esE.enter_context(nc.sbuf_tensor(f"stg{k}", [128, TC], F32)) for k in range(2)]
    woh = esE.enter_context(nc.sbuf_tensor("woh", [128, NDC * 1024], BF16))
    xts = [esE.enter_context(nc.sbuf_tensor(f"xts{k}", [128, 1024], F32)) for k in range(2)]
    fts = [esE.enter_context(nc.sbuf_tensor(f"fts{k}", [128, 1024], F32)) for k in range(2)]
    ppo = [esE.enter_context(nc.psum_tensor(f"ppo{k}", [128, 512], F32)) for k in range(4)]
    with nc.Block() as b:
        @b.sync
        def _(e):
            e.wait_ge(s_v.h, vA)
            e.wait_ge(s_fs[0].h, s_fs[0].n)
            e.wait_ge(s_fs[1].h, s_fs[1].n)
            for dh_ in range(2):
                for t in range(NT):
                    it_ = dh_ * NT + t
                    p = it_ % 2
                    if it_ >= 2:
                        e.wait_ge(s_v.h, vA + 2 * (it_ - 1))
                    e.dma_start(xts[p][:], x1_in[t * 128:(t + 1) * 128, dh_ * 1024:(dh_ + 1) * 1024]).then_inc(s_xl[p].h, 16)
                    e.dma_start(fts[p][:], ffn1_scr[t * 128:(t + 1) * 128, dh_ * 1024:(dh_ + 1) * 1024]).then_inc(s_xl[p].h, 16)
                    s_xl[p].n += 32
                    xl_marks[it_] = s_xl[p].n

        @b.gpsimd
        def _(e):
            e.wait_ge(s_v.h, vA)
            for dh_ in range(2):
                if dh_ == 1:
                    e.wait_ge(s_mm.h, mmA + NT)
                e.dma_start(woh[:], wo1_t[dh_]).then_inc(s_wb.h, 16)
                s_wb.n += 16
                wb_marks[("wo1", dh_)] = s_wb.n
                for t in range(NT):
                    it_ = dh_ * NT + t
                    p = it_ % 2
                    e.wait_ge(s_v.h, vA + 2 * (it_ + 1))
                    e.dma_start(out_f[t * 128:(t + 1) * 128, dh_ * 1024:(dh_ + 1) * 1024],
                                stg[p][:]).then_inc(s_os[p].h, 16)
                    s_os[p].n += 16
            e.wait_ge(s_os[0].h, s_os[0].n)
            e.wait_ge(s_os[1].h, s_os[1].n)

        @b.tensor
        def _(e):
            for dh_ in range(2):
                e.wait_ge(s_wb.h, wb_marks[("wo1", dh_)])
                for t in range(NT):
                    it_ = dh_ * NT + t
                    pa, pb = (ppo[0], ppo[1]) if it_ % 2 == 0 else (ppo[2], ppo[3])
                    if it_ >= 2:
                        e.wait_ge(s_v.h, vA + 2 * (it_ - 1))
                    for i in range(NDC):
                        st_, sp_ = (i == 0), (i == NDC - 1)
                        e.matmul(pa[:], o_sb[:, i * TC + t * 128:i * TC + (t + 1) * 128],
                                 woh[:, i * 1024:i * 1024 + 512], start=st_, stop=sp_)
                        mi = e.matmul(pb[:], o_sb[:, i * TC + t * 128:i * TC + (t + 1) * 128],
                                      woh[:, i * 1024 + 512:(i + 1) * 1024], start=st_, stop=sp_)
                        if sp_:
                            mi.then_inc(s_mm.h, 1)
                    s_mm.n += 1

        @b.vector
        def _(e):
            for dh_ in range(2):
                for t in range(NT):
                    it_ = dh_ * NT + t
                    p = it_ % 2
                    pa, pb = (ppo[0], ppo[1]) if p == 0 else (ppo[2], ppo[3])
                    e.wait_ge(s_xl[p].h, xl_marks[it_])
                    e.wait_ge(s_mm.h, mmA + it_ + 1)
                    if it_ >= 2:
                        e.wait_ge(s_os[p].h, 16 * (it_ // 2))
                    e.tensor_tensor(stg[p][:, 0:512], xts[p][:, 0:512], pa[:], ALU.add)
                    e.tensor_tensor(stg[p][:, 512:1024], xts[p][:, 512:1024], pb[:], ALU.add).then_inc(s_v.h, 1)
                    e.wait_ge(s_v.h, vA + 2 * it_ + 1)
                    e.tensor_tensor(stg[p][:], stg[p][:], fts[p][:], ALU.add).then_inc(s_v.h, 1)
                    s_v.n += 2

    esE.close()
    es.close()
    return nc


def host_inputs_l4(inputs, x1_all, xn1_all, q_all, k_all, v_all):
    g1 = tile_wT_i32(np.asarray(inputs["L1_ffn_gate_w"]).T, NJ1)
    u1 = tile_wT_i32(np.asarray(inputs["L1_ffn_up_w"]).T, NJ1)
    d1T = np.ascontiguousarray(np.asarray(inputs["L1_ffn_down_w"], np.int32).T)
    d1 = np.ascontiguousarray(d1T.reshape(NJ1, 128, 8, 256).transpose(2, 1, 0, 3).reshape(8, 128, NJ1 * 256))
    wo1T = f32(np.asarray(inputs["L1_attn_w_out"]).T)
    wo1 = np.ascontiguousarray(wo1T.reshape(NDC, 128, 2, 1024).transpose(2, 1, 0, 3).reshape(2, 128, NDC * 1024))
    gs1 = col_major_pc(inputs["L1_ffn_gate_scale"], NJ1)
    us1 = col_major_pc(inputs["L1_ffn_up_scale"], NJ1)
    ds1 = f32(np.asarray(inputs["L1_ffn_down_scale"]).reshape(1, D))

    per_core = []
    jj = np.arange(640)
    pp_ = np.arange(128)
    for c in range(NCORE):
        bq, hh = c // 2, c % 2
        base = bq * 2048 + hh * 1024
        if hh == 0:
            kx = np.concatenate([np.zeros((D, 512), dtype=BF), k_all[:, base:base + 1024]], axis=1)
            vx = np.concatenate([np.zeros((512, D), dtype=BF), v_all[base:base + 1024]], axis=0)
        else:
            kx = np.ascontiguousarray(k_all[:, base - 512:base + 1024])
            vx = np.ascontiguousarray(v_all[base - 512:base + 1024])
        qf = np.ascontiguousarray(q_all[:, base:base + 1024])
        msk = np.zeros((8, 128, 640), np.float32)
        for qb in range(8):
            kpos = hh * 1024 + qb * 128 - 512 + jj
            band = (jj[None, :] >= pp_[:, None]) & (jj[None, :] <= pp_[:, None] + 512)
            ok = band & (kpos >= 0)[None, :]
            msk[qb] = np.where(ok, 0.0, -1e30).astype(np.float32)
        m = dict(
            qt=np.ascontiguousarray(qf.reshape(NH, 128, TC).transpose(1, 0, 2).reshape(128, NH * TC)),
            kxt=np.ascontiguousarray(kx.reshape(NH, 128, 1536).transpose(1, 0, 2).reshape(128, NH * 1536)),
            vxt=np.ascontiguousarray(vx.reshape(NKV, 128, NH, 128).transpose(1, 0, 2, 3).reshape(128, NKV * NH * 128)),
            msk=np.ascontiguousarray(msk.transpose(1, 0, 2).reshape(128, 8 * 640)).astype(BF),
            x1=np.ascontiguousarray(x1_all[base:base + 1024]),
            xn1t=np.ascontiguousarray(xn1_all[:, base:base + 1024].reshape(NDC, 128, TC).transpose(1, 0, 2).reshape(128, NDC * TC)),
            g1_t=g1, u1_t=u1, d1_t=d1,
            gs1_pc=gs1, us1_pc=us1, ds1_row=ds1, wo1_t=wo1,
            ident=np.eye(128, dtype=BF),
        )
        per_core.append(m)
    return per_core



def kernel(**inputs):
    return kernel_fused(**inputs)


def kernel_unfused(**inputs):
    cores = list(range(NCORE))

    nc1 = build_launch1()
    r1 = run_bass_kernel_spmd(nc1, host_inputs_l1(inputs), core_ids=cores).results
    zx_all = np.concatenate([r1[c]["zx"] for c in range(NCORE)], axis=1)
    delta_all = np.concatenate([r1[c]["delta"] for c in range(NCORE)], axis=1)
    ffn0_all = np.concatenate([r1[c]["ffn0"] for c in range(NCORE)], axis=0)

    nc2 = build_launch2()
    r2 = run_bass_kernel_spmd(nc2, host_inputs_l2(inputs, zx_all, delta_all), core_ids=cores).results
    y2_all = np.concatenate([r2[c]["y2"] for c in range(NCORE)], axis=0)

    x_full = f32(np.asarray(inputs["L0_attn_hidden_states"])).reshape(NB * L, D)
    nc3 = build_launch3()
    r3 = run_bass_kernel_spmd(nc3, host_inputs_l3(inputs, x_full, ffn0_all, y2_all), core_ids=cores).results
    x1_all = np.concatenate([r3[c]["x1"] for c in range(NCORE)], axis=0)
    xn1_all = np.concatenate([r3[c]["xn1"] for c in range(NCORE)], axis=1)
    q_all = np.concatenate([r3[c]["qf"] for c in range(NCORE)], axis=1)
    k_all = np.concatenate([r3[c]["kf"] for c in range(NCORE)], axis=1)
    v_all = np.concatenate([r3[c]["vt"] for c in range(NCORE)], axis=0)

    nc4 = build_launch4()
    r4 = run_bass_kernel_spmd(nc4, host_inputs_l4(inputs, x1_all, xn1_all, q_all, k_all, v_all),
                              core_ids=cores).results
    out = np.concatenate([r4[c]["out"] for c in range(NCORE)], axis=0)
    return np.ascontiguousarray(out.reshape(NB, L, D), dtype=np.float32)





TT = 2048
TH = 1536
HOFF = 512
NTT = TT // 128
NTH = TH // 128
NIY = DI // 128


def build_fused():
    nc = bass.Bass("TRN2", target_bir_lowering=False)
    x_in = nc.dram_tensor("x", [TT, D], F32, kind="ExternalInput").ap()
    w_in_t = nc.dram_tensor("w_in_t", [NJ_IN, 128, NDC * 128], BF16, kind="ExternalInput").ap()
    w_dtT = nc.dram_tensor("w_dtT", [DTR, DI], BF16, kind="ExternalInput").ap()
    bdt_pc = nc.dram_tensor("bdt_pc", [128, NJ_DT], F32, kind="ExternalInput").ap()
    gw_t = nc.dram_tensor("gw_t", [NJ_F, 128, NDC * 128], BF16, kind="ExternalInput").ap()
    uw_t = nc.dram_tensor("uw_t", [NJ_F, 128, NDC * 128], BF16, kind="ExternalInput").ap()
    dw_t = nc.dram_tensor("dw_t", [2, 128, NJ_F * 1024], BF16, kind="ExternalInput").ap()
    gs_pc = nc.dram_tensor("gs_pc", [128, NJ_F], F32, kind="ExternalInput").ap()
    us_pc = nc.dram_tensor("us_pc", [128, NJ_F], F32, kind="ExternalInput").ap()
    ds_row = nc.dram_tensor("ds_row", [1, D], F32, kind="ExternalInput").ap()
    ident_in = nc.dram_tensor("ident", [128, 128], BF16, kind="ExternalInput").ap()
    alog_pc = nc.dram_tensor("alog_pc", [128, NIY * DST], F32, kind="ExternalInput").ap()
    cw_pc = nc.dram_tensor("cw_pc", [128, NIY * DCONV], F32, kind="ExternalInput").ap()
    cb_pc = nc.dram_tensor("cb_pc", [128, NIY], F32, kind="ExternalInput").ap()
    dp_pc = nc.dram_tensor("dp_pc", [128, NIY], F32, kind="ExternalInput").ap()
    eexp_in = nc.dram_tensor("eexp", [16, 128], BF16, kind="ExternalInput").ap()
    ebc_in = nc.dram_tensor("ebc", [DST, 16, 128], BF16, kind="ExternalInput").ap()
    scs_in = nc.dram_tensor("scs", [DST, 128, 128], BF16, kind="ExternalInput").ap()
    wo_t = nc.dram_tensor("wo_t", [2, 128, NIY * 1024], BF16, kind="ExternalInput").ap()
    wqk_t = nc.dram_tensor("wqk_t", [2 * NDC, 128, NDC * 128], BF16, kind="ExternalInput").ap()
    wv_t = nc.dram_tensor("wv_t", [128, NDC * D], BF16, kind="ExternalInput").ap()
    msk_in = nc.dram_tensor("msk", [128, 8 * 640], BF16, kind="ExternalInput").ap()
    g1_t = nc.dram_tensor("g1_t", [NJ1, 128, NDC * 128], BF16, kind="ExternalInput").ap()
    u1_t = nc.dram_tensor("u1_t", [NJ1, 128, NDC * 128], BF16, kind="ExternalInput").ap()
    d1_t = nc.dram_tensor("d1_t", [8, 128, NJ1 * 256], BF16, kind="ExternalInput").ap()
    gs1_pc = nc.dram_tensor("gs1_pc", [128, NJ1], F32, kind="ExternalInput").ap()
    us1_pc = nc.dram_tensor("us1_pc", [128, NJ1], F32, kind="ExternalInput").ap()
    ds1_row = nc.dram_tensor("ds1_row", [1, D], F32, kind="ExternalInput").ap()
    wo1_t = nc.dram_tensor("wo1_t", [2, 128, NDC * 1024], BF16, kind="ExternalInput").ap()

    out_f = nc.dram_tensor("out", [TC, D], F32, kind="ExternalOutput").ap()

    zx_scr = nc.dram_tensor("zx_scr", [ZXROWS, TT], BF16, kind="Internal").ap()
    dl_scr = nc.dram_tensor("dl_scr", [DI, TT], F32, kind="Internal").ap()
    h_scr = nc.dram_tensor("h_scr", [128, NJ_F, TH], BF16, kind="Internal").ap()
    ffn0_scr = nc.dram_tensor("ffn0_scr", [TH, D], F32, kind="Internal").ap()
    y2_scr = nc.dram_tensor("y2_scr", [128, NIY, TH], BF16, kind="Internal").ap()
    x1_scr = nc.dram_tensor("x1_scr", [TH, D], F32, kind="Internal").ap()
    xn1_scr = nc.dram_tensor("xn1_scr", [128, NDC * TH], BF16, kind="Internal").ap()
    q_scr = nc.dram_tensor("q_scr", [128, NH * TC], BF16, kind="Internal").ap()
    k_scr = nc.dram_tensor("k_scr", [128, NH * TH], BF16, kind="Internal").ap()
    v_scr = nc.dram_tensor("v_scr", [128, NKV * NH * 128], BF16, kind="Internal").ap()
    ffn1_scr = nc.dram_tensor("ffn1_scr", [TC, D], F32, kind="Internal").ap()

    es = ExitStack()
    ident = es.enter_context(nc.sbuf_tensor("identt", [128, 128], BF16))
    wbuf = [es.enter_context(nc.sbuf_tensor(f"wbuf{k}", [128, NDC * 128], BF16)) for k in range(2)]
    stgb = [es.enter_context(nc.sbuf_tensor(f"stgb{k}", [128, TT], BF16)) for k in range(2)]
    konst = es.enter_context(nc.sbuf_tensor("konst", [128, 2], F32))
    onescol = es.enter_context(nc.sbuf_tensor("onescol", [128, 1], BF16))
    negones = es.enter_context(nc.sbuf_tensor("negones", [1, 128], BF16))
    ones1r = es.enter_context(nc.sbuf_tensor("ones1r", [1, 128], BF16))
    mcol = es.enter_context(nc.sbuf_tensor("mcol", [128, NTT], F32))
    rcol = es.enter_context(nc.sbuf_tensor("rcol", [128, NTT], F32))
    scr1 = es.enter_context(nc.sbuf_tensor("scr1", [128, NTT], F32))
    scr2 = es.enter_context(nc.sbuf_tensor("scr2", [128, NTT], F32))


    s_c = Sem(es, nc, "s_c")
    s_cg = Sem(es, nc, "s_cg")
    s_x = [Sem(es, nc, f"s_x{k}") for k in range(2)]
    s_wq = [Sem(es, nc, f"s_wq{k}") for k in range(2)]
    s_uq = [Sem(es, nc, f"s_uq{k}") for k in range(2)]
    s_wb = Sem(es, nc, "s_wb")
    s_sb = [Sem(es, nc, f"s_sb{k}") for k in range(2)]
    s_sg = [Sem(es, nc, f"s_sg{k}") for k in range(2)]
    s_go = [Sem(es, nc, f"s_go{k}") for k in range(2)]
    s_hb = [Sem(es, nc, f"s_hb{k}") for k in range(2)]
    s_dl = [Sem(es, nc, f"s_dl{k}") for k in range(2)]
    s_cz = [Sem(es, nc, f"s_cz{k}") for k in range(2)]
    s_v = Sem(es, nc, "s_v")
    s_sc = Sem(es, nc, "s_sc")
    s_mm = Sem(es, nc, "s_mm")
    s_ms = Sem(es, nc, "s_ms")
    s_pad = Sem(es, nc, "s_pad")
    s_ax1 = Sem(es, nc, "s_ax1")
    s_ax2 = Sem(es, nc, "s_ax2")
    s_gm = Sem(es, nc, "s_gm")
    s_xe = [Sem(es, nc, f"s_xe{k}") for k in range(2)]

    x_marks = {}
    wq_marks = {}
    uq_marks = {}
    wb_marks = {}
    sb_marks = {}
    sg_marks = {}
    go_marks = {}
    hb_marks = {}
    dl_marks = {}
    cz_marks = {}

    es1b = ExitStack()
    dsb = es1b.enter_context(nc.sbuf_tensor("dsb", [128, D], F32))
    dsrow_sb = es1b.enter_context(nc.sbuf_tensor("dsrow_sb", [1, D], BF16))
    ptp = [es1b.enter_context(nc.psum_tensor(f"ptp{k}", [128, 1024], BF16)) for k in range(2)]
    pp = [es1b.enter_context(nc.psum_tensor(f"pp{k}", [128, 512], F32)) for k in range(4)]
    es1 = ExitStack()
    stg = [es1.enter_context(nc.sbuf_tensor(f"stg{k}", [128, TT], F32)) for k in range(2)]
    es1x = ExitStack()
    sgst = [es1x.enter_context(nc.sbuf_tensor(f"sgst{k}", [128, TT], BF16)) for k in range(2)]
    xnTall = es1x.enter_context(nc.sbuf_tensor("xnTall", [128, NDC * TT], BF16))
    dtl_sb = es1x.enter_context(nc.sbuf_tensor("dtl_sb", [128, TT], BF16))
    bdt1 = es1x.enter_context(nc.sbuf_tensor("bdt1", [128, NJ_DT], F32))
    bdt2 = es1x.enter_context(nc.sbuf_tensor("bdt2", [128, NJ_DT], F32))
    gscol = es1x.enter_context(nc.sbuf_tensor("gscol", [128, NJ_F], F32))
    uscol = es1x.enter_context(nc.sbuf_tensor("uscol", [128, NJ_F], F32))
    xnT = [xnTall[:, j * TT:(j + 1) * TT] for j in range(NDC)]
    xnTH = [xnTall[:, j * TT + HOFF:(j + 1) * TT] for j in range(NDC)]

    with nc.sbuf_tensor("xt0", [128, D], F32) as xt0, \
         nc.sbuf_tensor("xt1", [128, D], F32) as xt1, \
         nc.sbuf_tensor("sq0", [128, D], F32) as sq0, \
         nc.sbuf_tensor("sq1", [128, D], F32) as sq1, \
         nc.sbuf_tensor("xnb0", [128, D], BF16) as xnb0, \
         nc.sbuf_tensor("xnb1", [128, D], BF16) as xnb1:
        xt = [xt0, xt1]
        sq = [sq0, sq1]
        xnb = [xnb0, xnb1]

        with nc.Block() as b:
            @b.sync
            def _(e):
                e.dma_start(ident[:], ident_in).then_inc(s_c.h, 16)
                e.dma_start(bdt1[:], bdt_pc).then_inc(s_c.h, 16)
                e.dma_start(gscol[:], gs_pc).then_inc(s_c.h, 16)
                e.dma_start(uscol[:], us_pc).then_inc(s_c.h, 16)
                s_c.n += 64
                for i in range(NTT):
                    p = i % 2
                    if i >= 2:
                        e.wait_ge(s_v.h, 22 * (i - 1))
                        e.wait_ge(s_sc.h, 3 * (i - 1))
                    e.dma_start(xt[p][:], x_in[i * 128:(i + 1) * 128, :]).then_inc(s_x[p].h, 16)
                    s_x[p].n += 16
                    x_marks[("ln0", i)] = s_x[p].n

            @b.gpsimd
            def _(e):
                e.dma_start(dsrow_sb[:], ds_row).then_inc(s_cg.h, 16)
                e.memset(ones1r[:], 1.0)
                e.memset(onescol[:], 1.0)
                e.memset(negones[:], -128.0)
                e.memset(konst[:, 0:1], EPS)
                e.memset(konst[:, 1:2], -0.5).then_inc(s_pad.h, 1)
                s_pad.n += 1
                s_cg.n += 16
                e.dma_start(wbuf[0][:], w_in_t[NJ_IN - 1]).then_inc(s_wq[0].h, 16)
                s_wq[0].n += 16
                wq_marks[("in", NJ_IN - 1)] = (0, s_wq[0].n)

            @b.scalar
            def _(e):
                e.wait_ge(s_pad.h, 1)
                for i in range(NTT):
                    p = i % 2
                    e.wait_ge(s_x[p].h, x_marks[("ln0", i)])
                    e.square(sq[p][:], xt[p][:]).then_inc(s_sc.h, 1)
                    e.wait_ge(s_v.h, 22 * i + 5)
                    e.activation(scr1[:, i:i + 1], scr2[:, i:i + 1], AF.Ln, bias=konst[:, 0:1]).then_inc(s_sc.h, 1)
                    e.wait_ge(s_sc.h, 3 * i + 2)
                    e.activation(rcol[:, i:i + 1], scr1[:, i:i + 1], AF.Exp, scale=konst[:, 1:2]).then_inc(s_sc.h, 1)
                    s_sc.n += 3

            @b.vector
            def _(e):
                e.wait_ge(s_c.h, 64)
                e.tensor_add(bdt2[:], bdt1[:], bdt1[:]).then_inc(s_ms.h, 1)
                s_ms.n += 1
                for i in range(NTT):
                    p = i % 2
                    e.wait_ge(s_x[p].h, x_marks[("ln0", i)])
                    e.reduce_sum(scr1[:, i:i + 1], xt[p][:], axis=AX.X).then_inc(s_v.h, 1)
                    e.wait_ge(s_v.h, 22 * i + 1)
                    e.tensor_scalar_mul(mcol[:, i:i + 1], scr1[:, i:i + 1], 1.0 / D).then_inc(s_v.h, 1)
                    e.wait_ge(s_sc.h, 3 * i + 1)
                    e.wait_ge(s_v.h, 22 * i + 2)
                    e.reduce_sum(scr2[:, i:i + 1], sq[p][:], axis=AX.X).then_inc(s_v.h, 1)
                    e.wait_ge(s_v.h, 22 * i + 3)
                    e.tensor_scalar_mul(scr2[:, i:i + 1], scr2[:, i:i + 1], 1.0 / D)
                    e.tensor_mul(scr1[:, i:i + 1], mcol[:, i:i + 1], mcol[:, i:i + 1]).then_inc(s_v.h, 1)
                    e.wait_ge(s_v.h, 22 * i + 4)
                    e.tensor_sub(scr2[:, i:i + 1], scr2[:, i:i + 1], scr1[:, i:i + 1]).then_inc(s_v.h, 1)
                    e.wait_ge(s_sc.h, 3 * i + 3)
                    if i >= 2:
                        e.wait_ge(s_mm.h, 16 * (i - 1))
                    e.tensor_scalar(xnb[p][:], xt[p][:], mcol[:, i:i + 1], rcol[:, i:i + 1],
                                    ALU.subtract, ALU.mult).then_inc(s_v.h, 1)
                    s_v.n += 6
                    for j in range(NDC):
                        e.wait_ge(s_mm.h, 16 * i + j + 1)
                        e.tensor_copy(xnT[j][:, i * 128:(i + 1) * 128],
                                      ptp[(16 * i + j) % 2][:, 0:128]).then_inc(s_v.h, 1)
                        s_v.n += 1

            @b.tensor
            def _(e):
                e.wait_ge(s_c.h, 64)
                for i in range(NTT):
                    e.wait_ge(s_v.h, 22 * i + 6)
                    for j in range(NDC):
                        k = 16 * i + j
                        if k >= 2:
                            i2, j2 = divmod(k - 2, 16)
                            e.wait_ge(s_v.h, 22 * i2 + 6 + j2 + 1)
                        e.transpose(ptp[k % 2][:, 0:128], xnb[i % 2][:, j * 128:(j + 1) * 128],
                                    ident[:]).then_inc(s_mm.h, 1)
                        s_mm.n += 1

    mm0 = s_mm.n
    v0 = s_v.n

    jorder = [NJ_IN - 1] + list(range(NJ_IN - 1))
    with nc.Block() as b:
        @b.gpsimd
        def _(e):
            for idx in range(1, NJ_IN):
                j = jorder[idx]
                p = idx % 2
                if idx >= 2:
                    e.wait_ge(s_mm.h, mm0 + 64 * (idx - 1))
                e.dma_start(wbuf[p][:], w_in_t[j]).then_inc(s_wq[p].h, 16)
                s_wq[p].n += 16
                wq_marks[("in", j)] = (p, s_wq[p].n)

        @b.tensor
        def _(e):
            e.wait_ge(s_v.h, v0)
            for idx in range(NJ_IN):
                j = jorder[idx]
                p, mark = wq_marks[("in", j)]
                e.wait_ge(s_wq[p].h, mark)
                pa, pb = (pp[0], pp[1]) if idx % 2 == 0 else (pp[2], pp[3])
                for half in range(2):
                    if half == 1:
                        e.wait_ge(s_v.h, v0 + 2 * idx + 1)
                    elif idx >= 2:
                        e.wait_ge(s_v.h, v0 + 2 * (idx - 1))
                    for i in range(NDC):
                        st_, sp_ = (i == 0), (i == NDC - 1)
                        e.matmul(pa[:], wbuf[p][:, i * 128:(i + 1) * 128],
                                 xnT[i][:, half * 1024:half * 1024 + 512], start=st_, stop=sp_)
                        mi = e.matmul(pb[:], wbuf[p][:, i * 128:(i + 1) * 128],
                                      xnT[i][:, half * 1024 + 512:(half + 1) * 1024], start=st_, stop=sp_)
                        if sp_:
                            mi.then_inc(s_mm.h, 32)
                    s_mm.n += 32

        @b.vector
        def _(e):
            for idx in range(NJ_IN):
                p = idx % 2
                pa, pb = (pp[0], pp[1]) if p == 0 else (pp[2], pp[3])
                for half in range(2):
                    e.wait_ge(s_mm.h, mm0 + 64 * idx + 32 * (half + 1))
                    if idx >= 2 and half == 0:
                        e.wait_ge(s_sb[p].h, 16 * (idx // 2))
                    if idx == 0:
                        e.tensor_copy(dtl_sb[:, half * 1024:half * 1024 + 512], pa[:])
                        ci = e.tensor_copy(dtl_sb[:, half * 1024 + 512:(half + 1) * 1024], pb[:])
                        if half == 1:
                            ci.then_inc(s_ms.h, 1)
                            s_ms.n += 1
                    e.tensor_copy(stgb[p][:, half * 1024:half * 1024 + 512], pa[:])
                    e.tensor_copy(stgb[p][:, half * 1024 + 512:(half + 1) * 1024], pb[:]).then_inc(s_v.h, 1)
                    s_v.n += 1

        @b.sync
        def _(e):
            for idx in range(NJ_IN):
                j = jorder[idx]
                p = idx % 2
                e.wait_ge(s_v.h, v0 + 2 * (idx + 1))
                e.dma_start(zx_scr[j * 128:(j + 1) * 128, :], stgb[p][:]).then_inc(s_sb[p].h, 16)
                s_sb[p].n += 16
                sb_marks[("zx", j)] = s_sb[p].n

    c_dtl = s_c.n

    mm1 = s_mm.n
    v1 = s_v.n
    sc1 = s_sc.n

    wdtall = es1x.enter_context(nc.sbuf_tensor("wdtall", [128, NJ_DT * 128], BF16))
    spA = es1x.enter_context(nc.sbuf_tensor("spA", [128, TT], F32))
    spB = es1x.enter_context(nc.sbuf_tensor("spB", [128, TT], F32))
    spx = [spA, spB]
    pp4 = es1b.enter_context(nc.psum_tensor("pp4", [128, 512], F32))
    pp5 = es1b.enter_context(nc.psum_tensor("pp5", [128, 512], F32))
    with nc.Block() as b:
        @b.gpsimd
        def _(e):
            e.dma_start(wdtall[:], w_dtT).then_inc(s_uq[0].h, 16)
            s_uq[0].n += 16
            uq_marks["wdt"] = s_uq[0].n

        @b.tensor
        def _(e):
            e.wait_ge(s_ms.h, 2)
            e.wait_ge(s_c.h, c_dtl)
            e.wait_ge(s_v.h, v1)
            e.wait_ge(s_uq[0].h, uq_marks["wdt"])
            for jj in range(NJ_DT):
                for half in range(2):
                    if half == 1:
                        e.wait_ge(s_sc.h, sc1 + 8 * jj + 2)
                        e.wait_ge(s_v.h, v1 + 4 * jj + 1)
                    elif jj >= 1:
                        e.wait_ge(s_sc.h, sc1 + 8 * (jj - 1) + 6)
                        e.wait_ge(s_v.h, v1 + 4 * (jj - 1) + 3)
                    e.matmul(pp4[:], wdtall[:, jj * 128:(jj + 1) * 128], dtl_sb[:, half * 1024:half * 1024 + 512],
                             start=True, stop=True)
                    e.matmul(pp5[:], wdtall[:, jj * 128:(jj + 1) * 128], dtl_sb[:, half * 1024 + 512:(half + 1) * 1024],
                             start=True, stop=True).then_inc(s_mm.h, 2)
                    s_mm.n += 2

        @b.scalar
        def _(e):
            for jj in range(NJ_DT):
                p = jj % 2
                for half in range(2):
                    e.wait_ge(s_mm.h, mm1 + 4 * jj + 2 * (half + 1))
                    if jj >= 2 and half == 0:
                        e.wait_ge(s_v.h, v1 + 4 * (jj - 1))
                    e.activation(spx[p][:, half * 1024:half * 1024 + 512], pp4[:],
                                 AF.Abs, bias=bdt2[:, jj:jj + 1]).then_inc(s_sc.h, 1)
                    e.activation(spx[p][:, half * 1024 + 512:(half + 1) * 1024], pp5[:],
                                 AF.Abs, bias=bdt2[:, jj:jj + 1]).then_inc(s_sc.h, 1)
                    e.wait_ge(s_sc.h, sc1 + 8 * jj + 4 * half + 2)
                    e.activation(spx[p][:, half * 1024:(half + 1) * 1024],
                                 spx[p][:, half * 1024:(half + 1) * 1024], AF.Exp, scale=-1.0).then_inc(s_sc.h, 1)
                    e.wait_ge(s_sc.h, sc1 + 8 * jj + 4 * half + 3)
                    e.activation(spx[p][:, half * 1024:(half + 1) * 1024],
                                 spx[p][:, half * 1024:(half + 1) * 1024], AF.Ln, bias=1.0).then_inc(s_sc.h, 1)
                    s_sc.n += 4

        @b.vector
        def _(e):
            for jj in range(NJ_DT):
                p = jj % 2
                for half in range(2):
                    e.wait_ge(s_mm.h, mm1 + 4 * jj + 2 * (half + 1))
                    if jj >= 2 and half == 0:
                        e.wait_ge(s_sg[p].h, 16 * (jj // 2))
                    e.tensor_scalar(stg[p][:, half * 1024:half * 1024 + 512], pp4[:],
                                    bdt2[:, jj:jj + 1], 0.0, ALU.add, ALU.max)
                    e.tensor_scalar(stg[p][:, half * 1024 + 512:(half + 1) * 1024], pp5[:],
                                    bdt2[:, jj:jj + 1], 0.0, ALU.add, ALU.max).then_inc(s_v.h, 1)
                    e.wait_ge(s_sc.h, sc1 + 8 * jj + 4 * half + 4)
                    e.wait_ge(s_v.h, v1 + 4 * jj + 2 * half + 1)
                    e.tensor_add(stg[p][:, half * 1024:(half + 1) * 1024],
                                 stg[p][:, half * 1024:(half + 1) * 1024],
                                 spx[p][:, half * 1024:(half + 1) * 1024]).then_inc(s_v.h, 1)
                    s_v.n += 2

        @b.sync
        def _(e):
            for jj in range(NJ_DT):
                p = jj % 2
                e.wait_ge(s_v.h, v1 + 4 * (jj + 1))
                e.dma_start(dl_scr[jj * 128:(jj + 1) * 128, :], stg[p][:]).then_inc(s_sg[p].h, 16)
                s_sg[p].n += 16
                sg_marks[("dl", jj)] = s_sg[p].n

    mm2 = s_mm.n
    v2 = s_v.n
    sc2 = s_sc.n
    sb2 = [s_sb[0].n, s_sb[1].n]
    sg2 = [s_sg[0].n, s_sg[1].n]

    with nc.Block() as b:
        @b.gpsimd
        def _(e):
            for j in range(NJ_F):
                pj = j % 2
                if j >= 1:
                    e.wait_ge(s_mm.h, mm2 + 6 * j)
                e.dma_start(wbuf[pj][:], gw_t[j]).then_inc(s_wq[pj].h, 16)
                s_wq[pj].n += 16
                wq_marks[("g0", j)] = s_wq[pj].n
                e.dma_start(sgst[pj][:, 0:NDC * 128].rearrange("p (i o) -> p i o", i=NDC), uw_t[j]) \
                    .then_inc(s_uq[pj].h, 16)
                s_uq[pj].n += 16
                uq_marks[("u0", j)] = s_uq[pj].n

        @b.tensor
        def _(e):
            e.wait_ge(s_sc.h, sc2)
            for j in range(NJ_F):
                pj = j % 2
                e.wait_ge(s_wq[pj].h, wq_marks[("g0", j)])
                e.wait_ge(s_uq[pj].h, uq_marks[("u0", j)])
                if j >= 1:
                    e.wait_ge(s_v.h, v2 + 4 * j)
                for q in range(3):
                    if q == 2:
                        e.wait_ge(s_sc.h, sc2 + 3 * j + 1)
                        e.wait_ge(s_v.h, v2 + 4 * j + 1)
                    for i in range(NDC):
                        st_, sp_ = (i == 0), (i == NDC - 1)
                        e.matmul(pp[q % 2][:], wbuf[pj][:, i * 128:(i + 1) * 128],
                                 xnTH[i][:, q * 512:(q + 1) * 512], start=st_, stop=sp_)
                        mi = e.matmul(pp[2 + q % 2][:], sgst[pj][:, i * 128:(i + 1) * 128],
                                      xnTH[i][:, q * 512:(q + 1) * 512], start=st_, stop=sp_)
                        if sp_:
                            mi.then_inc(s_mm.h, 2)
                    s_mm.n += 2

        @b.scalar
        def _(e):
            for j in range(NJ_F):
                pj = j % 2
                if j < 2:
                    e.wait_ge(s_sg[pj].h, sg2[pj])
                for q in range(3):
                    e.wait_ge(s_mm.h, mm2 + 6 * j + 2 * (q + 1))
                    if j >= 1 or q >= 2:
                        e.wait_ge(s_v.h, v2 + 4 * (j - 1) + q + 2 if j >= 1 else v2)
                    e.activation(stg[pj][:, q * 512:(q + 1) * 512], pp[q % 2][:], AF.Silu,
                                 scale=gscol[:, j:j + 1]).then_inc(s_sc.h, 1)
                    s_sc.n += 1

        @b.vector
        def _(e):
            for j in range(NJ_F):
                pj = j % 2
                for q in range(3):
                    e.wait_ge(s_sc.h, sc2 + 3 * j + q + 1)
                    if q == 0:
                        e.wait_ge(s_sb[pj].h, sb2[pj] + 16 * (j // 2))
                    e.scalar_tensor_tensor(stgb[pj][:, q * 512:(q + 1) * 512], pp[2 + q % 2][:],
                                           uscol[:, j:j + 1], stg[pj][:, q * 512:(q + 1) * 512],
                                           ALU.mult, ALU.mult).then_inc(s_v.h, 2 if q == 2 else 1)
                    s_v.n += 2 if q == 2 else 1

        @b.sync
        def _(e):
            for j in range(NJ_F):
                pj = j % 2
                e.wait_ge(s_v.h, v2 + 4 * (j + 1))
                e.dma_start(h_scr[:, j, :], stgb[pj][:, 0:TH]).then_inc(s_sb[pj].h, 16)
                s_sb[pj].n += 16
                sb_marks[("h", j)] = s_sb[pj].n

    mm3 = s_mm.n
    v3 = s_v.n
    sb3 = [s_sb[0].n, s_sb[1].n]
    sg3 = [s_sg[0].n, s_sg[1].n]

    with nc.Block() as b:
        @b.tensor
        def _(e):
            e.wait_ge(s_cg.h, 16)
            e.wait_ge(s_v.h, v3)
            for q in range(4):
                mi = e.matmul(pp[q][:], ones1r[:], dsrow_sb[:, q * 512:(q + 1) * 512], start=True, stop=True)
            mi.then_inc(s_mm.h, 1)
            s_mm.n += 1

        @b.vector
        def _(e):
            e.wait_ge(s_mm.h, mm3 + 1)
            for q in range(4):
                ci = e.tensor_copy(dsb[:, q * 512:(q + 1) * 512], pp[q][:])
            ci.then_inc(s_ms.h, 1)
            s_ms.n += 1

    mm4 = s_mm.n
    ms4 = s_ms.n
    es1x.close()

    with nc.sbuf_tensor("dwh", [128, NJ_F * 1024], BF16) as dwh, \
         nc.sbuf_tensor("hb0", [128, NJ_F * 128], BF16) as hb0, \
         nc.sbuf_tensor("hb1", [128, NJ_F * 128], BF16) as hb1:
        hb = [hb0, hb1]
        with nc.Block() as b:
            @b.gpsimd
            def _(e):
                for dh_ in range(2):
                    if dh_ == 1:
                        e.wait_ge(s_mm.h, mm4 + 88 * NTH)
                    e.dma_start(dwh[:], dw_t[dh_]).then_inc(s_wb.h, 16)
                    s_wb.n += 16
                    wb_marks[("dw0", dh_)] = s_wb.n
                    for t in range(NTH):
                        it_ = dh_ * NTH + t
                        p = t % 2
                        if dh_ == 0 and t == 0:
                            e.wait_ge(s_sb[0].h, sb3[0])
                            e.wait_ge(s_sb[1].h, sb3[1])
                        if it_ >= 2:
                            e.wait_ge(s_mm.h, mm4 + 88 * (it_ - 1))
                        for q in range(2):
                            e.dma_start(
                                hb[p][:, q * 22 * 128:(q + 1) * 22 * 128].rearrange("p (i o) -> p i o", i=22),
                                h_scr[:, q * 22:(q + 1) * 22, t * 128:(t + 1) * 128],
                            ).then_inc(s_hb[p].h, 16)
                            s_hb[p].n += 16
                        hb_marks[("h0", it_)] = (p, s_hb[p].n)

            @b.tensor
            def _(e):
                e.wait_ge(s_ms.h, ms4)
                for dh_ in range(2):
                    for t in range(NTH):
                        it_ = dh_ * NTH + t
                        p, mark = hb_marks[("h0", it_)]
                        e.wait_ge(s_wb.h, wb_marks[("dw0", dh_)])
                        e.wait_ge(s_hb[p].h, mark)
                        pa, pb = (pp[0], pp[1]) if t % 2 == 0 else (pp[2], pp[3])
                        if it_ >= 2:
                            e.wait_ge(s_v.h, v3 + 2 * (it_ - 1))
                        for j in range(NJ_F):
                            st_, sp_ = (j == 0), (j == NJ_F - 1)
                            e.matmul(pa[:], hb[p][:, j * 128:(j + 1) * 128],
                                     dwh[:, j * 1024:j * 1024 + 512], start=st_, stop=sp_)
                            mi = e.matmul(pb[:], hb[p][:, j * 128:(j + 1) * 128],
                                          dwh[:, j * 1024 + 512:(j + 1) * 1024], start=st_, stop=sp_)
                            if sp_:
                                mi.then_inc(s_mm.h, 88)
                        s_mm.n += 88

            @b.vector
            def _(e):
                e.wait_ge(s_ms.h, ms4)
                for dh_ in range(2):
                    for t in range(NTH):
                        it_ = dh_ * NTH + t
                        p = it_ % 2
                        pa, pb = (pp[0], pp[1]) if t % 2 == 0 else (pp[2], pp[3])
                        e.wait_ge(s_mm.h, mm4 + 88 * (it_ + 1))
                        e.wait_ge(s_sg[p].h, sg3[p] + 16 * (it_ // 2))
                        e.tensor_tensor(stg[p][:, 0:512], pa[:], dsb[:, dh_ * 1024:dh_ * 1024 + 512], ALU.mult)
                        e.tensor_tensor(stg[p][:, 512:1024], pb[:], dsb[:, dh_ * 1024 + 512:(dh_ + 1) * 1024],
                                        ALU.mult).then_inc(s_v.h, 2)
                        s_v.n += 2

            @b.sync
            def _(e):
                for dh_ in range(2):
                    for t in range(NTH):
                        it_ = dh_ * NTH + t
                        p = it_ % 2
                        e.wait_ge(s_v.h, v3 + 2 * (it_ + 1))
                        e.dma_start(ffn0_scr[t * 128:(t + 1) * 128, dh_ * 1024:(dh_ + 1) * 1024],
                                    stg[p][:, 0:1024]).then_inc(s_sg[p].h, 16)
                        s_sg[p].n += 16
                        sg_marks[("f0", it_)] = s_sg[p].n

    es1.close()
    es1b.close()
    mm5 = s_mm.n
    v5 = s_v.n
    sc5 = s_sc.n

    es2 = ExitStack()
    apc = es2.enter_context(nc.sbuf_tensor("apc", [128, NIY * DST], F32))
    atmp = es2.enter_context(nc.sbuf_tensor("atmp", [128, NIY * DST], F32))
    cw = es2.enter_context(nc.sbuf_tensor("cw", [128, NIY * DCONV], F32))
    cbk = es2.enter_context(nc.sbuf_tensor("cbk", [128, NIY], F32))
    dpk = es2.enter_context(nc.sbuf_tensor("dpk", [128, NIY], F32))
    ebc = es2.enter_context(nc.sbuf_tensor("ebc_s", [16, DST * 128], BF16))
    scs = es2.enter_context(nc.sbuf_tensor("scs_s", [128, DST * 128], BF16))
    bs16 = [es2.enter_context(nc.sbuf_tensor(f"bs16_{k}", [16, TT], BF16)) for k in range(2)]
    xe = [es2.enter_context(nc.sbuf_tensor(f"xe{k}", [128, TT + 4], BF16)) for k in range(2)]
    u_sb2 = [es2.enter_context(nc.sbuf_tensor(f"u_sb{kk}", [128, TT], F32)) for kk in range(2)]
    du = es2.enter_context(nc.sbuf_tensor("du", [128, TT], F32))
    dls = [es2.enter_context(nc.sbuf_tensor(f"dls{k}", [128, TT], F32)) for k in range(2)]
    cms = [es2.enter_context(nc.sbuf_tensor(f"cms{k}", [128, TH], BF16)) for k in range(2)]
    zs = [es2.enter_context(nc.sbuf_tensor(f"zs{k}", [128, TH], BF16)) for k in range(2)]
    da = [es2.enter_context(nc.sbuf_tensor(f"da{k}", [128, TT], BF16)) for k in range(2)]
    db = [es2.enter_context(nc.sbuf_tensor(f"db{k}", [128, TT], BF16)) for k in range(2)]
    scn = [es2.enter_context(nc.sbuf_tensor(f"scn{k}", [128, TT], BF16)) for k in range(2)]
    csb = [es2.enter_context(nc.sbuf_tensor(f"csb{k}", [128, TH], BF16)) for k in range(2)]
    mb = [es2.enter_context(nc.sbuf_tensor(f"mb{k}", [128, TH], BF16)) for k in range(2)]
    szb = es2.enter_context(nc.sbuf_tensor("szb", [128, TH], BF16))
    t1b = es2.enter_context(nc.sbuf_tensor("t1b", [128, TH], BF16))
    y2st = [es2.enter_context(nc.sbuf_tensor(f"y2st{k}", [128, TH], BF16)) for k in range(2)]
    psA = es2.enter_context(nc.psum_tensor("psA", [128, TT], F32))
    py = es2.enter_context(nc.psum_tensor("py", [128, TH], F32))
    psC = es2.enter_context(nc.psum_tensor("psC", [128, 512], F32))

    vS = s_v.n
    mmS = s_mm.n
    scS = s_sc.n
    gmS = s_gm.n
    sgS = [s_sg[0].n, s_sg[1].n]
    NT2 = NIY
    pm = {}

    _v, _sc, _mm, _gm = vS, scS, mmS, gmS
    for k in range(NT2):
        _v += 4; pm[("conv", k)] = _v
        if k >= 1:
            _v += 1; pm[("t1b", k - 1)] = _v
            _v += 1; pm[("y2st", k - 1)] = _v
        _v += 1; pm[("du", k)] = _v
        for n in range(DST):
            _v += 4; pm[("db", k, n)] = _v
            _v += 1; pm[("scan", k, n)] = _v
        if k == NT2 - 1:
            _v += 1; pm[("t1b", k)] = _v
            _v += 1; pm[("y2st", k)] = _v
        _sc += 1; pm[("da", k, 0)] = _sc
        _sc += 1; pm[("da", k, 1)] = _sc
        _sc += 1; pm[("usilu", k)] = _sc
        for n in range(DST):
            for c in range(3):
                _sc += 1; pm[("ev", k, n, c)] = _sc
            if n == 7:
                _sc += 1; pm[("zsilu", k)] = _sc
            if n + 2 < DST:
                _sc += 1; pm[("da", k, n + 2)] = _sc
        for n in range(DST):
            for q in range(4):
                _mm += 1; pm[("Bq", k, n, q)] = _mm
            pm[("B", k, n)] = _mm
            for c in range(3):
                _mm += 1; pm[("C", k, n, c)] = _mm
            if n >= 1:
                _mm += 1; pm[("py", k, n - 1)] = _mm
        _mm += 1; pm[("py", k, DST - 1)] = _mm
        for n in range(DST):
            _gm += 1; pm[("mb", k, n)] = _gm

    with nc.Block() as b:
        @b.sync
        def _(e):
            e.wait_ge(s_v.h, vS)
            e.wait_ge(s_sg[0].h, sgS[0])
            e.wait_ge(s_sg[1].h, sgS[1])
            e.dma_start(atmp[:], alog_pc).then_inc(s_c.h, 16)
            e.dma_start(cw[:], cw_pc).then_inc(s_c.h, 16)
            e.dma_start(cbk[:], cb_pc).then_inc(s_c.h, 16)
            e.dma_start(dpk[:], dp_pc).then_inc(s_c.h, 16)
            e.dma_start(ebc[:].rearrange("p (j o) -> p j o", j=DST), ebc_in.rearrange("j p o -> p j o")).then_inc(s_c.h, 16)
            e.dma_start(scs[:].rearrange("p (j o) -> p j o", j=DST), scs_in.rearrange("j p o -> p j o")).then_inc(s_c.h, 16)
            s_c.n += 96
            for k in range(NT2):
                p = k % 2
                if k >= 2:
                    e.wait_ge(s_sc.h, pm[("da", k - 2, DST - 1)])
                    e.wait_ge(s_v.h, pm[("du", k - 2)])
                    e.wait_ge(s_mm.h, pm[("C", k - 2, DST - 1, 2)])
                e.dma_start(dls[p][:], dl_scr[k * 128:(k + 1) * 128, :]).then_inc(s_dl[p].h, 16)
                s_dl[p].n += 16
                dl_marks[k] = s_dl[p].n
                e.dma_start(cms[p][:], zx_scr[OFF_C + k * 128:OFF_C + (k + 1) * 128, HOFF:TT]).then_inc(s_cz[p].h, 16)
                e.dma_start(zs[p][:], zx_scr[OFF_Z + k * 128:OFF_Z + (k + 1) * 128, HOFF:TT]).then_inc(s_cz[p].h, 16)
                s_cz[p].n += 32
                cz_marks[k] = s_cz[p].n
                for g in range(8):
                    e.dma_start(xe[p][16 * g:16 * (g + 1), 3:3 + TT],
                                zx_scr[OFF_X + k * 16:OFF_X + (k + 1) * 16, :]).then_inc(s_xe[p].h, 16)
                s_xe[p].n += 128
                cz_marks[("xe", k)] = s_xe[p].n

        @b.gpsimd
        def _(e):
            e.wait_ge(s_v.h, vS)
            e.wait_ge(s_sg[0].h, sgS[0])
            e.wait_ge(s_sg[1].h, sgS[1])
            e.memset(xe[0][:, 0:3], 0.0).then_inc(s_pad.h, 1)
            e.memset(xe[1][:, 0:3], 0.0).then_inc(s_pad.h, 1)
            s_pad.n += 2
            for k in range(NT2):
                p = k % 2
                if k >= 2:
                    e.wait_ge(s_mm.h, pm[("B", k - 2, DST - 1)])
                e.dma_start(bs16[p][:], zx_scr[OFF_B + k * 16:OFF_B + (k + 1) * 16, :]).then_inc(s_wq[p].h, 16)
                s_wq[p].n += 16
                wq_marks[("bs", k)] = s_wq[p].n
                if k >= 1:
                    kk = k - 1
                    e.wait_ge(s_v.h, pm[("y2st", kk)])
                    e.dma_start(y2_scr[:, kk, :], y2st[kk % 2][:]).then_inc(s_go[kk % 2].h, 16)
                    s_go[kk % 2].n += 16
                    go_marks[("y2", kk)] = s_go[kk % 2].n
                for n in range(DST):
                    e.wait_ge(s_sc.h, pm[("ev", k, n, 2)])
                    e.wait_ge(s_v.h, pm[("scan", k, n)])
                    if (k, n) != (0, 0) and (k, n) != (0, 1):
                        kk2, nn2 = (k, n - 2) if n >= 2 else (k - 1, DST - 2 + n)
                        e.wait_ge(s_mm.h, pm[("py", kk2, nn2)])
                    e.tensor_tensor(mb[n % 2][:], csb[n % 2][:], scn[n % 2][:, HOFF:TT],
                                    ALU.mult).then_inc(s_gm.h, 1)
                    s_gm.n += 1
            kk = NT2 - 1
            e.wait_ge(s_v.h, pm[("y2st", kk)])
            e.dma_start(y2_scr[:, kk, :], y2st[kk % 2][:]).then_inc(s_go[kk % 2].h, 16)
            s_go[kk % 2].n += 16
            go_marks[("y2", kk)] = s_go[kk % 2].n

        @b.scalar
        def _(e):
            e.wait_ge(s_c.h, s_c.n)
            e.activation(apc[:], atmp[:], AF.Exp).then_inc(s_ax1.h, 1)
            s_ax1.n += 1
            e.wait_ge(s_ax2.h, 1)
            for k in range(NT2):
                p = k % 2
                for n in range(2):
                    e.wait_ge(s_dl[p].h, dl_marks[k])
                    if k >= 1:
                        e.wait_ge(s_v.h, pm[("scan", k - 1, DST - 2 + n)])
                    e.activation(da[n % 2][:], dls[p][:], AF.Exp,
                                 scale=apc[:, k * DST + n:k * DST + n + 1]).then_inc(s_sc.h, 1)
                    s_sc.n += 1
                e.wait_ge(s_v.h, pm[("conv", k)])
                e.activation(u_sb2[k % 2][:], u_sb2[k % 2][:], AF.Silu, bias=cbk[:, k:k + 1]).then_inc(s_sc.h, 1)
                s_sc.n += 1
                for n in range(DST):
                    for c in range(3):
                        e.wait_ge(s_mm.h, pm[("C", k, n, c)])
                        if n >= 2 and c == 0:
                            e.wait_ge(s_gm.h, pm[("mb", k, n - 2)])
                        elif k >= 1 and n < 2 and c == 0:
                            e.wait_ge(s_gm.h, pm[("mb", k - 1, DST - 2 + n)])
                        e.activation(csb[n % 2][:, c * 512:(c + 1) * 512], psC[:],
                                     AF.Copy).then_inc(s_sc.h, 1)
                        s_sc.n += 1
                    if n == 7:
                        e.wait_ge(s_cz[p].h, cz_marks[k])
                        if k >= 1:
                            e.wait_ge(s_v.h, pm[("y2st", k - 1)])
                        e.activation(szb[:], zs[p][:], AF.Silu).then_inc(s_sc.h, 1)
                        s_sc.n += 1
                    if n + 2 < DST:
                        e.wait_ge(s_v.h, pm[("scan", k, n)])
                        e.activation(da[n % 2][:], dls[p][:], AF.Exp,
                                     scale=apc[:, k * DST + n + 2:k * DST + n + 3]).then_inc(s_sc.h, 1)
                        s_sc.n += 1

        @b.vector
        def _(e):
            e.wait_ge(s_ax1.h, 1)
            e.tensor_scalar_mul(apc[:], apc[:], -1.0).then_inc(s_ax2.h, 1)
            s_ax2.n += 1
            def _tail(kk):
                pk = kk % 2
                e.wait_ge(s_mm.h, pm[("py", kk, DST - 1)])
                e.scalar_tensor_tensor(t1b[:], u_sb2[kk % 2][:, HOFF:TT], dpk[:, kk:kk + 1], py[:],
                                       ALU.mult, ALU.add).then_inc(s_v.h, 1)
                e.wait_ge(s_sc.h, pm[("zsilu", kk)])
                if kk >= 2:
                    e.wait_ge(s_go[pk].h, go_marks[("y2", kk - 2)])
                e.tensor_tensor(y2st[pk][:], t1b[:], szb[:], ALU.mult).then_inc(s_v.h, 1)
                s_v.n += 2

            for k in range(NT2):
                p = k % 2
                u_sb = u_sb2[k % 2]
                e.wait_ge(s_xe[p].h, cz_marks[("xe", k)])
                e.wait_ge(s_pad.h, 3)
                e.tensor_scalar(u_sb[:], xe[p][:, 3:3 + TT], cw[:, k * 4 + 3:k * 4 + 4], None,
                                ALU.mult)
                for q in range(3):
                    ci = e.scalar_tensor_tensor(u_sb[:], xe[p][:, q:q + TT], cw[:, k * 4 + q:k * 4 + q + 1],
                                                u_sb[:], ALU.mult, ALU.add)
                    ci.then_inc(s_v.h, 2 if q == 2 else 1)
                s_v.n += 4
                if k >= 1:
                    _tail(k - 1)
                e.wait_ge(s_sc.h, pm[("usilu", k)])
                e.wait_ge(s_dl[p].h, dl_marks[k])
                e.tensor_tensor(du[:], dls[p][:], u_sb[:], ALU.mult).then_inc(s_v.h, 1)
                s_v.n += 1
                for n in range(DST):
                    for q in range(4):
                        e.wait_ge(s_mm.h, pm[("Bq", k, n, q)])
                        e.tensor_tensor(db[n % 2][:, q * 512:(q + 1) * 512],
                                        psA[:, q * 512:(q + 1) * 512],
                                        du[:, q * 512:(q + 1) * 512], ALU.mult).then_inc(s_v.h, 1)
                        s_v.n += 1
                    e.wait_ge(s_sc.h, pm[("da", k, n)])
                    if n >= 2 or k >= 1:
                        kk2, nn2 = (k, n - 2) if n >= 2 else (k - 1, DST - 2 + n)
                        e.wait_ge(s_gm.h, pm[("mb", kk2, nn2)])
                    e.tensor_tensor_scan(scn[n % 2][:], da[n % 2][:], db[n % 2][:], 0.0,
                                         ALU.mult, ALU.add).then_inc(s_v.h, 1)
                    s_v.n += 1
            _tail(NT2 - 1)

        @b.tensor
        def _(e):
            e.wait_ge(s_c.h, s_c.n)
            e.wait_ge(s_v.h, vS)
            for k in range(NT2):
                p = k % 2
                for n in range(DST):
                    if n == 0:
                        e.wait_ge(s_wq[p].h, wq_marks[("bs", k)])
                        if k >= 1:
                            e.wait_ge(s_v.h, pm[("db", k - 1, DST - 1)])
                    else:
                        e.wait_ge(s_v.h, pm[("db", k, n - 1)])
                    for q in range(4):
                        mi = e.matmul(psA[:, q * 512:(q + 1) * 512], ebc[:, n * 128:(n + 1) * 128],
                                      bs16[p][:, q * 512:(q + 1) * 512], start=True, stop=True)
                        mi.then_inc(s_mm.h, 1)
                        s_mm.n += 1
                    if n == 0:
                        e.wait_ge(s_cz[p].h, cz_marks[k])
                    for c in range(3):
                        if (k, n, c) != (0, 0, 0):
                            kk2, nn2, cc2 = (k, n, c - 1) if c >= 1 else \
                                ((k, n - 1, 2) if n >= 1 else (k - 1, DST - 1, 2))
                            e.wait_ge(s_sc.h, pm[("ev", kk2, nn2, cc2)])
                        mi = e.matmul(psC[:], scs[:, n * 128:(n + 1) * 128],
                                      cms[p][:, c * 512:(c + 1) * 512], start=True, stop=True)
                        mi.then_inc(s_mm.h, 1)
                        s_mm.n += 1
                    if n >= 1:
                        nn = n - 1
                        e.wait_ge(s_gm.h, pm[("mb", k, nn)])
                        if nn == 0 and k >= 1:
                            e.wait_ge(s_v.h, pm[("t1b", k - 1)])
                        for q in range(3):
                            mi = e.matmul(py[:, q * 512:(q + 1) * 512], ident[:],
                                          mb[nn % 2][:, q * 512:(q + 1) * 512],
                                          start=(nn == 0), stop=(nn == DST - 1))
                        mi.then_inc(s_mm.h, 1)
                        s_mm.n += 1
                nn = DST - 1
                e.wait_ge(s_gm.h, pm[("mb", k, nn)])
                for q in range(3):
                    mi = e.matmul(py[:, q * 512:(q + 1) * 512], ident[:],
                                  mb[nn % 2][:, q * 512:(q + 1) * 512],
                                  start=(nn == 0), stop=(nn == DST - 1))
                mi.then_inc(s_mm.h, 1)
                s_mm.n += 1

    assert s_v.n == _v and s_sc.n == _sc and s_mm.n == _mm and s_gm.n == _gm, \
        (s_v.n, _v, s_sc.n, _sc, s_mm.n, _mm, s_gm.n, _gm)
    es2.close()
    vP2 = s_v.n
    mmP2 = s_mm.n

    es3 = ExitStack()
    stg3 = [es3.enter_context(nc.sbuf_tensor(f"stg3{k}", [128, 1024], F32)) for k in range(2)]
    stq3 = [es3.enter_context(nc.sbuf_tensor(f"stq3{k}", [128, TH], BF16)) for k in range(2)]
    xts = [es3.enter_context(nc.sbuf_tensor(f"xts3{k}", [128, 1024], F32)) for k in range(2)]
    fts = [es3.enter_context(nc.sbuf_tensor(f"fts3{k}", [128, 1024], F32)) for k in range(2)]
    ptp3 = [es3.enter_context(nc.psum_tensor(f"ptp3{k}", [128, 1024], BF16)) for k in range(2)]
    pp3 = [es3.enter_context(nc.psum_tensor(f"pp3{k}", [128, 512], F32)) for k in range(4)]

    vW = s_v.n
    mmW = s_mm.n

    es3a = ExitStack()
    woh = es3a.enter_context(nc.sbuf_tensor("woh", [128, NIY * 1024], BF16))
    y2b = [es3a.enter_context(nc.sbuf_tensor(f"y2b{k}", [128, NIY * 128], BF16)) for k in range(2)]
    with nc.Block() as b:
        @b.gpsimd
        def _(e):
            e.wait_ge(s_v.h, vW)
            for dh_ in range(2):
                if dh_ == 1:
                    e.wait_ge(s_mm.h, mmW + NTH)
                e.dma_start(woh[:], wo_t[dh_]).then_inc(s_wb.h, 16)
                s_wb.n += 16
                wb_marks[("wo", dh_)] = s_wb.n
                for t in range(NTH):
                    it_ = dh_ * NTH + t
                    p = t % 2
                    if dh_ == 0 and t == 0:
                        e.wait_ge(s_go[0].h, s_go[0].n)
                        e.wait_ge(s_go[1].h, s_go[1].n)
                    if it_ >= 2:
                        e.wait_ge(s_mm.h, mmW + (it_ - 1))
                    for q in range(2):
                        e.dma_start(
                            y2b[p][:, q * 16 * 128:(q + 1) * 16 * 128].rearrange("p (i o) -> p i o", i=16),
                            y2_scr[:, q * 16:(q + 1) * 16, t * 128:(t + 1) * 128],
                        ).then_inc(s_hb[p].h, 16)
                        s_hb[p].n += 16
                    hb_marks[("y2", it_)] = (p, s_hb[p].n)

        @b.sync
        def _(e):
            e.wait_ge(s_v.h, vW)
            for dh_ in range(2):
                for t in range(NTH):
                    it_ = dh_ * NTH + t
                    p = it_ % 2
                    if it_ >= 2:
                        e.wait_ge(s_v.h, vW + 2 * (it_ - 1))
                    e.dma_start(xts[p][:], x_in[HOFF + t * 128:HOFF + (t + 1) * 128,
                                                dh_ * 1024:(dh_ + 1) * 1024]).then_inc(s_x[p].h, 16)
                    e.dma_start(fts[p][:], ffn0_scr[t * 128:(t + 1) * 128,
                                                    dh_ * 1024:(dh_ + 1) * 1024]).then_inc(s_x[p].h, 16)
                    s_x[p].n += 32
                    x_marks[("x1", it_)] = s_x[p].n

        @b.tensor
        def _(e):
            e.wait_ge(s_v.h, vW)
            for dh_ in range(2):
                e.wait_ge(s_wb.h, wb_marks[("wo", dh_)])
                for t in range(NTH):
                    it_ = dh_ * NTH + t
                    pq, mark = hb_marks[("y2", it_)]
                    e.wait_ge(s_hb[pq].h, mark)
                    pa, pb = (pp3[0], pp3[1]) if it_ % 2 == 0 else (pp3[2], pp3[3])
                    if it_ >= 2:
                        e.wait_ge(s_v.h, vW + 2 * (it_ - 1))
                    for i in range(NIY):
                        st_, sp_ = (i == 0), (i == NIY - 1)
                        e.matmul(pa[:], y2b[pq][:, i * 128:(i + 1) * 128],
                                 woh[:, i * 1024:i * 1024 + 512], start=st_, stop=sp_)
                        mi = e.matmul(pb[:], y2b[pq][:, i * 128:(i + 1) * 128],
                                      woh[:, i * 1024 + 512:(i + 1) * 1024], start=st_, stop=sp_)
                        if sp_:
                            mi.then_inc(s_mm.h, 1)
                    s_mm.n += 1

        @b.vector
        def _(e):
            for dh_ in range(2):
                for t in range(NTH):
                    it_ = dh_ * NTH + t
                    p = it_ % 2
                    pa, pb = (pp3[0], pp3[1]) if p == 0 else (pp3[2], pp3[3])
                    e.wait_ge(s_x[p].h, x_marks[("x1", it_)])
                    e.wait_ge(s_mm.h, mmW + it_ + 1)
                    if it_ >= 2:
                        e.wait_ge(s_sg[p].h, sgS[p] + 16 * (it_ // 2))
                    e.tensor_tensor(stg3[p][:, 0:512], xts[p][:, 0:512], pa[:], ALU.add)
                    e.tensor_tensor(stg3[p][:, 512:1024], xts[p][:, 512:1024], pb[:], ALU.add).then_inc(s_v.h, 1)
                    e.wait_ge(s_v.h, vW + 2 * it_ + 1)
                    e.tensor_tensor(stg3[p][:], stg3[p][:], fts[p][:], ALU.add).then_inc(s_v.h, 1)
                    s_v.n += 2

        @b.scalar
        def _(e):
            for dh_ in range(2):
                for t in range(NTH):
                    it_ = dh_ * NTH + t
                    p = it_ % 2
                    e.wait_ge(s_v.h, vW + 2 * (it_ + 1))
                    e.dma_start(x1_scr[t * 128:(t + 1) * 128, dh_ * 1024:(dh_ + 1) * 1024],
                                stg3[p][:]).then_inc(s_sg[p].h, 16)
                    s_sg[p].n += 16
                    sg_marks[("x1", it_)] = s_sg[p].n

    es3a.close()
    vL = s_v.n
    mmL = s_mm.n
    scL = s_sc.n

    es3b = ExitStack()
    xn1Tall = es3b.enter_context(nc.sbuf_tensor("xn1Tall", [128, NDC * TH], BF16))
    xn1T = [xn1Tall[:, j * TH:(j + 1) * TH] for j in range(NDC)]
    x1t = [es3b.enter_context(nc.sbuf_tensor(f"x1t{k}", [128, D], F32)) for k in range(2)]
    sq1 = [es3b.enter_context(nc.sbuf_tensor(f"sq1_{k}", [128, D], F32)) for k in range(2)]
    xnb1 = [es3b.enter_context(nc.sbuf_tensor(f"xnb1_{k}", [128, D], BF16)) for k in range(2)]
    with nc.Block() as b:
        @b.sync
        def _(e):
            for i in range(NTH):
                p = i % 2
                e.wait_ge(s_sg[i % 2].h, sg_marks[("x1", i)])
                e.wait_ge(s_sg[(NTH + i) % 2].h, sg_marks[("x1", NTH + i)])
                if i < 2:
                    e.wait_ge(s_v.h, vL)
                if i >= 2:
                    e.wait_ge(s_v.h, vL + 22 * (i - 1))
                    e.wait_ge(s_sc.h, scL + 3 * (i - 1))
                e.dma_start(x1t[p][:], x1_scr[i * 128:(i + 1) * 128, :]).then_inc(s_x[p].h, 16)
                s_x[p].n += 16
                x_marks[("ln1", i)] = s_x[p].n

        @b.scalar
        def _(e):
            for i in range(NTH):
                p = i % 2
                e.wait_ge(s_x[p].h, x_marks[("ln1", i)])
                e.square(sq1[p][:], x1t[p][:]).then_inc(s_sc.h, 1)
                e.wait_ge(s_v.h, vL + 22 * i + 5)
                e.activation(scr1[:, i:i + 1], scr2[:, i:i + 1], AF.Ln, bias=konst[:, 0:1]).then_inc(s_sc.h, 1)
                e.wait_ge(s_sc.h, scL + 3 * i + 2)
                e.activation(rcol[:, i:i + 1], scr1[:, i:i + 1], AF.Exp, scale=konst[:, 1:2]).then_inc(s_sc.h, 1)
                s_sc.n += 3

        @b.vector
        def _(e):
            for i in range(NTH):
                p = i % 2
                e.wait_ge(s_x[p].h, x_marks[("ln1", i)])
                e.reduce_sum(scr1[:, i:i + 1], x1t[p][:], axis=AX.X).then_inc(s_v.h, 1)
                e.wait_ge(s_v.h, vL + 22 * i + 1)
                e.tensor_scalar_mul(mcol[:, i:i + 1], scr1[:, i:i + 1], 1.0 / D).then_inc(s_v.h, 1)
                e.wait_ge(s_sc.h, scL + 3 * i + 1)
                e.wait_ge(s_v.h, vL + 22 * i + 2)
                e.reduce_sum(scr2[:, i:i + 1], sq1[p][:], axis=AX.X).then_inc(s_v.h, 1)
                e.wait_ge(s_v.h, vL + 22 * i + 3)
                e.tensor_scalar_mul(scr2[:, i:i + 1], scr2[:, i:i + 1], 1.0 / D)
                e.tensor_mul(scr1[:, i:i + 1], mcol[:, i:i + 1], mcol[:, i:i + 1]).then_inc(s_v.h, 1)
                e.wait_ge(s_v.h, vL + 22 * i + 4)
                e.tensor_sub(scr2[:, i:i + 1], scr2[:, i:i + 1], scr1[:, i:i + 1]).then_inc(s_v.h, 1)
                e.wait_ge(s_sc.h, scL + 3 * i + 3)
                if i >= 2:
                    e.wait_ge(s_mm.h, mmL + 16 * (i - 1))
                e.tensor_scalar(xnb1[p][:], x1t[p][:], mcol[:, i:i + 1], rcol[:, i:i + 1],
                                ALU.subtract, ALU.mult).then_inc(s_v.h, 1)
                s_v.n += 6
                for j in range(NDC):
                    e.wait_ge(s_mm.h, mmL + 16 * i + j + 1)
                    e.tensor_copy(xn1T[j][:, i * 128:(i + 1) * 128],
                                  ptp3[(16 * i + j) % 2][:, 0:128]).then_inc(s_v.h, 1)
                    s_v.n += 1

        @b.tensor
        def _(e):
            for i in range(NTH):
                e.wait_ge(s_v.h, vL + 22 * i + 6)
                for j in range(NDC):
                    k = 16 * i + j
                    if k >= 2:
                        i2, j2 = divmod(k - 2, 16)
                        e.wait_ge(s_v.h, vL + 22 * i2 + 6 + j2 + 1)
                    e.transpose(ptp3[k % 2][:, 0:128], xnb1[i % 2][:, j * 128:(j + 1) * 128],
                                ident[:]).then_inc(s_mm.h, 1)
                    s_mm.n += 1

        @b.gpsimd
        def _(e):
            for j in range(NDC):
                e.wait_ge(s_v.h, vL + 22 * (NTH - 1) + 7 + (j + 1) * 0 + 22 * 0)
                e.wait_ge(s_v.h, vL + 22 * NTH)
                e.dma_start(xn1_scr[:, j * TH:(j + 1) * TH], xn1T[j][:]).then_inc(s_go[j % 2].h, 16)
                s_go[j % 2].n += 16
                go_marks[("xn1", j)] = s_go[j % 2].n

    es3b.close()

    es3c = ExitStack()
    xn1Tb = es3c.enter_context(nc.sbuf_tensor("xn1Tb", [128, NDC * TH], BF16))
    wvs = es3c.enter_context(nc.sbuf_tensor("wvs", [128, NDC * D], BF16))
    xq1 = [xn1Tb[:, j * TH:(j + 1) * TH] for j in range(NDC)]
    vQ = s_v.n
    mmQ = s_mm.n
    with nc.Block() as b:
        @b.sync
        def _(e):
            e.wait_ge(s_go[0].h, s_go[0].n)
            e.wait_ge(s_go[1].h, s_go[1].n)
            e.dma_start(xn1Tb[:], xn1_scr[:, :]).then_inc(s_c.h, 16)
            s_c.n += 16
            cQ = s_c.n

        @b.gpsimd
        def _(e):
            e.wait_ge(s_v.h, vQ)
            e.dma_start(wvs[:], wv_t).then_inc(s_wb.h, 16)
            s_wb.n += 16
            wb_marks["wv"] = s_wb.n
            for j in range(2 * NDC):
                p = j % 2
                if j >= 2:
                    e.wait_ge(s_mm.h, mmQ + 3 * (j - 1))
                e.dma_start(wbuf[p][:], wqk_t[j]).then_inc(s_wq[p].h, 16)
                s_wq[p].n += 16
                wq_marks[("qk", j)] = (p, s_wq[p].n)

        @b.tensor
        def _(e):
            e.wait_ge(s_c.h, s_c.n)
            e.wait_ge(s_v.h, vQ)
            for j in range(2 * NDC):
                p, mark = wq_marks[("qk", j)]
                e.wait_ge(s_wq[p].h, mark)
                pa, pb = (pp3[0], pp3[1]) if j % 2 == 0 else (pp3[2], pp3[3])
                if j >= 2:
                    e.wait_ge(s_v.h, vQ + 3 * (j - 1))
                nsl_ = 3
                for q in range(nsl_):
                    dst = pa if q % 2 == 0 else pb
                    if q == 2:
                        e.wait_ge(s_v.h, vQ + 3 * j + 1)
                    for i in range(NDC):
                        st_, sp_ = (i == 0), (i == NDC - 1)
                        mi = e.matmul(dst[:], wbuf[p][:, i * 128:(i + 1) * 128],
                                      xq1[i][:, q * 512:(q + 1) * 512], start=st_, stop=sp_)
                        if sp_:
                            mi.then_inc(s_mm.h, 1)
                    s_mm.n += 1
            e.wait_ge(s_wb.h, wb_marks["wv"])
            for t in range(NTH):
                for vh in range(2):
                    it_ = 2 * NDC * 3 + t * 2 + vh
                    pa, pb = (pp3[0], pp3[1]) if it_ % 2 == 0 else (pp3[2], pp3[3])
                    e.wait_ge(s_v.h, vQ + it_ - 1)
                    for i in range(NDC):
                        st_, sp_ = (i == 0), (i == NDC - 1)
                        e.matmul(pa[:], xq1[i][:, t * 128:(t + 1) * 128],
                                 wvs[:, i * D + vh * 1024:i * D + vh * 1024 + 512], start=st_, stop=sp_)
                        mi = e.matmul(pb[:], xq1[i][:, t * 128:(t + 1) * 128],
                                      wvs[:, i * D + vh * 1024 + 512:i * D + (vh + 1) * 1024],
                                      start=st_, stop=sp_)
                        if sp_:
                            mi.then_inc(s_mm.h, 2)
                    s_mm.n += 2

        @b.vector
        def _(e):
            for j in range(2 * NDC):
                p = j % 2
                pa, pb = (pp3[0], pp3[1]) if p == 0 else (pp3[2], pp3[3])
                for q in range(3):
                    src = pa if q % 2 == 0 else pb
                    e.wait_ge(s_mm.h, mmQ + 3 * j + q + 1)
                    if q == 0 and j >= 2:
                        e.wait_ge(s_sb[p].h, sb3[p] + 16 * (j // 2))
                    if j < NDC:
                        e.tensor_scalar_mul(stq3[p][:, q * 512:(q + 1) * 512], src[:], SCALE).then_inc(s_v.h, 1)
                    else:
                        e.tensor_copy(stq3[p][:, q * 512:(q + 1) * 512], src[:]).then_inc(s_v.h, 1)
                    s_v.n += 1
            base = mmQ + 6 * NDC
            for t in range(NTH):
                for vh in range(2):
                    it_ = 2 * NDC * 3 + t * 2 + vh
                    p = it_ % 2
                    pa, pb = (pp3[0], pp3[1]) if p == 0 else (pp3[2], pp3[3])
                    e.wait_ge(s_mm.h, base + 2 * (t * 2 + vh + 1))
                    e.wait_ge(s_sb[p].h, sb3[p] + 16 * ((2 * NDC + t * 2 + vh) // 2))
                    e.tensor_copy(stq3[p][:, 0:512], pa[:])
                    e.tensor_copy(stq3[p][:, 512:1024], pb[:]).then_inc(s_v.h, 1)
                    s_v.n += 1

        @b.scalar
        def _(e):
            for j in range(2 * NDC):
                p = j % 2
                e.wait_ge(s_v.h, vQ + 3 * (j + 1))
                if j < NDC:
                    e.dma_start(q_scr[:, j * TC:(j + 1) * TC], stq3[p][:, HOFF:TH]).then_inc(s_sb[p].h, 16)
                else:
                    jk = j - NDC
                    e.dma_start(k_scr[:, jk * TH:(jk + 1) * TH], stq3[p][:, 0:TH]).then_inc(s_sb[p].h, 16)
                s_sb[p].n += 16
                sb_marks[("qk", j)] = s_sb[p].n
            for t in range(NTH):
                for vh in range(2):
                    it_ = t * 2 + vh
                    p = (2 * NDC * 3 + it_) % 2
                    e.wait_ge(s_v.h, vQ + 6 * NDC + it_ + 1)
                    e.dma_start(v_scr[:, t * D + vh * 1024:t * D + (vh + 1) * 1024],
                                stq3[p][:, 0:1024]).then_inc(s_sb[p].h, 16)
                    s_sb[p].n += 16
                    sb_marks[("v", it_)] = s_sb[p].n

    sbQ = [s_sb[0].n, s_sb[1].n]
    es3c.close()
    es3.close()
    vP3 = s_v.n
    mmP3 = s_mm.n
    scP3 = s_sc.n

    es4 = ExitStack()
    Rb = es4.enter_context(nc.sbuf_tensor("Rb", [1, TC], BF16))
    R2c = es4.enter_context(nc.sbuf_tensor("R2c", [128, 8], F32))
    mR2c = es4.enter_context(nc.sbuf_tensor("mR2c", [128, 8], F32))
    gscol1 = es4.enter_context(nc.sbuf_tensor("gscol1", [128, NJ1], F32))
    uscol1 = es4.enter_context(nc.sbuf_tensor("uscol1", [128, NJ1], F32))
    ds1row_sb = es4.enter_context(nc.sbuf_tensor("ds1row_sb", [1, D], BF16))
    sg4 = [es4.enter_context(nc.sbuf_tensor(f"sg4_{k}", [128, TC], BF16)) for k in range(2)]
    st4 = [es4.enter_context(nc.sbuf_tensor(f"st4_{k}", [128, TC], BF16)) for k in range(2)]
    h_all = es4.enter_context(nc.sbuf_tensor("h_allt", [128, NJ1 * TC], BF16))
    hav = [h_all[:, j * TC:(j + 1) * TC] for j in range(NJ1)]
    ppf = [es4.enter_context(nc.psum_tensor(f"ppf{k}", [128, 512], F32)) for k in range(6)]
    pRc = es4.enter_context(nc.psum_tensor("pRc", [128, 1], F32))

    esB4 = ExitStack()
    xn1To = esB4.enter_context(nc.sbuf_tensor("xn1To", [128, NDC * TC], BF16))
    wbg = [esB4.enter_context(nc.sbuf_tensor(f"wbg{k}", [128, NDC * 128], BF16)) for k in range(2)]
    wbu = [esB4.enter_context(nc.sbuf_tensor(f"wbu{k}", [128, NDC * 128], BF16)) for k in range(2)]

    vF = s_v.n
    mmF = s_mm.n
    scF = s_sc.n
    msF = s_ms.n

    with nc.Block() as b:
        @b.sync
        def _(e):
            e.wait_ge(s_v.h, vF)
            e.wait_ge(s_sb[0].h, sbQ[0])
            e.wait_ge(s_sb[1].h, sbQ[1])
            e.dma_start(xn1To[:].rearrange("p (i o) -> p i o", i=NDC),
                        xn1_scr.rearrange("p (i o) -> p i o", i=NDC)[:, :, HOFF:TH]).then_inc(s_c.h, 16)
            e.dma_start(gscol1[:], gs1_pc).then_inc(s_c.h, 16)
            e.dma_start(uscol1[:], us1_pc).then_inc(s_c.h, 16)
            s_c.n += 48
            c4 = s_c.n

        @b.gpsimd
        def _(e):
            e.wait_ge(s_v.h, vF)
            e.wait_ge(s_sb[0].h, sbQ[0])
            e.wait_ge(s_sb[1].h, sbQ[1])
            e.dma_start(ds1row_sb[:], ds1_row).then_inc(s_cg.h, 16)
            s_cg.n += 16
            for j in range(NJ1):
                pj = j % 2
                if j >= 2:
                    e.wait_ge(s_mm.h, mmF + 1 + (j - 1))
                e.dma_start(wbg[pj][:], g1_t[j]).then_inc(s_wq[pj].h, 16)
                s_wq[pj].n += 16
                wq_marks[("g1", j)] = s_wq[pj].n
                e.dma_start(wbu[pj][:], u1_t[j]).then_inc(s_uq[pj].h, 16)
                s_uq[pj].n += 16
                uq_marks[("u1", j)] = s_uq[pj].n

        @b.tensor
        def _(e):
            e.wait_ge(s_c.h, s_c.n)
            e.wait_ge(s_v.h, vF)
            for i in range(NDC):
                st_, sp_ = (i == 0), (i == NDC - 1)
                e.matmul(ppf[4][0:1, :], onescol[:], xn1To[:, i * TC:i * TC + 512], start=st_, stop=sp_)
                mi = e.matmul(ppf[5][0:1, :], onescol[:], xn1To[:, i * TC + 512:(i + 1) * TC], start=st_, stop=sp_)
                if sp_:
                    mi.then_inc(s_mm.h, 1)
            s_mm.n += 1
            for j in range(NJ1):
                pj = j % 2
                e.wait_ge(s_wq[pj].h, wq_marks[("g1", j)])
                e.wait_ge(s_uq[pj].h, uq_marks[("u1", j)])
                e.wait_ge(s_ms.h, msF + 1)
                ga, gb = ppf[2 * pj], ppf[2 * pj + 1]
                if j >= 2:
                    e.wait_ge(s_sc.h, scF + 2 * (j - 1))
                if j >= 1:
                    e.wait_ge(s_v.h, vF + 1 + 2 * j)
                for i in range(NDC):
                    st_ = (i == 0)
                    e.matmul(ga[:], wbg[pj][:, i * 128:(i + 1) * 128], xn1To[:, i * TC:i * TC + 512],
                             start=st_, stop=False)
                    e.matmul(gb[:], wbg[pj][:, i * 128:(i + 1) * 128], xn1To[:, i * TC + 512:(i + 1) * TC],
                             start=st_, stop=False)
                    e.matmul(ppf[4][:], wbu[pj][:, i * 128:(i + 1) * 128], xn1To[:, i * TC:i * TC + 512],
                             start=st_, stop=False)
                    e.matmul(ppf[5][:], wbu[pj][:, i * 128:(i + 1) * 128], xn1To[:, i * TC + 512:(i + 1) * TC],
                             start=st_, stop=False)
                e.matmul(ga[:], negones[:], Rb[:, 0:512], start=False, stop=True)
                e.matmul(gb[:], negones[:], Rb[:, 512:1024], start=False, stop=True)
                e.matmul(ppf[4][:], negones[:], Rb[:, 0:512], start=False, stop=True)
                mi = e.matmul(ppf[5][:], negones[:], Rb[:, 512:1024], start=False, stop=True)
                mi.then_inc(s_mm.h, 1)
                s_mm.n += 1

        @b.vector
        def _(e):
            e.wait_ge(s_mm.h, mmF + 1)
            e.tensor_copy(Rb[:, 0:512], ppf[4][0:1, :])
            e.tensor_copy(Rb[:, 512:1024], ppf[5][0:1, :]).then_inc(s_ms.h, 1)
            s_ms.n += 1
            e.nop().then_inc(s_v.h, 1)
            s_v.n += 1
            for j in range(NJ1):
                e.wait_ge(s_sc.h, scF + 2 * (j + 1))
                e.scalar_tensor_tensor(hav[j][:, 0:512], ppf[4][:], uscol1[:, j:j + 1],
                                       sg4[j % 2][:, 0:512], ALU.mult, ALU.mult).then_inc(s_v.h, 1)
                e.scalar_tensor_tensor(hav[j][:, 512:1024], ppf[5][:], uscol1[:, j:j + 1],
                                       sg4[j % 2][:, 512:1024], ALU.mult, ALU.mult).then_inc(s_v.h, 1)
                s_v.n += 2

        @b.scalar
        def _(e):
            e.wait_ge(s_sb[0].h, sbQ[0])
            e.wait_ge(s_sb[1].h, sbQ[1])
            for j in range(NJ1):
                pj = j % 2
                e.wait_ge(s_mm.h, mmF + 2 + j)
                if j >= 2:
                    e.wait_ge(s_v.h, vF + 1 + 2 * (j - 1))
                e.activation(sg4[pj][:, 0:512], ppf[2 * pj][:], AF.Silu, scale=gscol1[:, j:j + 1]).then_inc(s_sc.h, 1)
                e.activation(sg4[pj][:, 512:1024], ppf[2 * pj + 1][:], AF.Silu, scale=gscol1[:, j:j + 1]).then_inc(s_sc.h, 1)
                s_sc.n += 2

    esB4.close()
    mmG = s_mm.n
    vG = s_v.n
    scG = s_sc.n
    msG = s_ms.n

    esC4 = ExitStack()
    dwq = esC4.enter_context(nc.sbuf_tensor("dwq", [128, NJ1 * 256], BF16))
    ds1b = esC4.enter_context(nc.sbuf_tensor("ds1b", [128, D], F32))
    with nc.Block() as b:
        @b.tensor
        def _(e):
            e.wait_ge(s_v.h, vG)
            e.wait_ge(s_sc.h, scG)
            for t in range(NT):
                if t >= 1:
                    e.wait_ge(s_v.h, vG + t)
                for j in range(NJ1):
                    mi = e.matmul(pRc[:], hav[j][:, t * 128:(t + 1) * 128], onescol[:],
                                  start=(j == 0), stop=(j == NJ1 - 1))
                mi.then_inc(s_mm.h, 1)
                s_mm.n += 1
            e.wait_ge(s_cg.h, s_cg.n)
            for q in range(4):
                mi = e.matmul(ppf[q][:], ones1r[:], ds1row_sb[:, q * 512:(q + 1) * 512], start=True, stop=True)
            mi.then_inc(s_mm.h, 1)
            s_mm.n += 1

        @b.vector
        def _(e):
            for t in range(NT):
                e.wait_ge(s_mm.h, mmG + t + 1)
                e.tensor_copy(R2c[:, t:t + 1], pRc[:]).then_inc(s_v.h, 1)
                s_v.n += 1
            e.wait_ge(s_v.h, vG + NT)
            e.tensor_scalar_mul(mR2c[:], R2c[:], -128.0).then_inc(s_ms.h, 1)
            s_ms.n += 1
            e.wait_ge(s_mm.h, mmG + NT + 1)
            for q in range(4):
                ci = e.tensor_copy(ds1b[:, q * 512:(q + 1) * 512], ppf[q][:])
            ci.then_inc(s_ms.h, 1)
            s_ms.n += 1

    mmR = s_mm.n
    vR = s_v.n
    fs_marks = {}
    with nc.Block() as b:
        @b.gpsimd
        def _(e):
            e.wait_ge(s_mm.h, mmG)
            for dq in range(8):
                if dq >= 1:
                    e.wait_ge(s_mm.h, mmR + NT * dq)
                e.dma_start(dwq[:], d1_t[dq]).then_inc(s_wb.h, 16)
                s_wb.n += 16
                wb_marks[("d1", dq)] = s_wb.n
                for t in range(NT):
                    it_ = dq * NT + t
                    p = it_ % 2
                    e.wait_ge(s_v.h, vR + it_ + 1)
                    e.dma_start(ffn1_scr[t * 128:(t + 1) * 128, dq * 256:(dq + 1) * 256],
                                st4[p][:, 0:256]).then_inc(s_go[p].h, 16)
                    s_go[p].n += 16
                    fs_marks[it_] = s_go[p].n

        @b.tensor
        def _(e):
            e.wait_ge(s_ms.h, msG + 2)
            for dq in range(8):
                e.wait_ge(s_wb.h, wb_marks[("d1", dq)])
                for t in range(NT):
                    it_ = dq * NT + t
                    pf = ppf[it_ % 2]
                    if it_ >= 2:
                        e.wait_ge(s_v.h, vR + (it_ - 1))
                    for j in range(NJ1):
                        mi = e.matmul(pf[:, 0:256], hav[j][:, t * 128:(t + 1) * 128],
                                      dwq[:, j * 256:(j + 1) * 256], start=(j == 0), stop=(j == NJ1 - 1))
                    mi.then_inc(s_mm.h, 1)
                    s_mm.n += 1

        @b.vector
        def _(e):
            e.wait_ge(s_ms.h, msG + 2)
            for dq in range(8):
                for t in range(NT):
                    it_ = dq * NT + t
                    p = it_ % 2
                    pf = ppf[it_ % 2]
                    e.wait_ge(s_mm.h, mmR + it_ + 1)
                    if it_ >= 2:
                        e.wait_ge(s_go[p].h, fs_marks[it_ - 2])
                    e.scalar_tensor_tensor(st4[p][:, 0:256], pf[:, 0:256], mR2c[:, t:t + 1],
                                           ds1b[:, dq * 256:(dq + 1) * 256], ALU.add, ALU.mult).then_inc(s_v.h, 1)
                    s_v.n += 1

    esC4.close()
    es4.close()
    mmD4 = s_mm.n
    vD4 = s_v.n
    scD4 = s_sc.n

    esD = ExitStack()
    o_sb = es.enter_context(nc.sbuf_tensor("o_sb", [128, NH * TC], BF16))
    qts = esD.enter_context(nc.sbuf_tensor("qts", [128, NH * TC], BF16))
    kxs = esD.enter_context(nc.sbuf_tensor("kxs", [128, NH * TH], BF16))
    vxs = esD.enter_context(nc.sbuf_tensor("vxs", [128, NKV * NH * 128], BF16))
    msks = esD.enter_context(nc.sbuf_tensor("msks", [128, 8 * 640], BF16))
    ssb = [esD.enter_context(nc.sbuf_tensor(f"ssb{k}", [128, 640], F32)) for k in range(2)]
    pbb = [esD.enter_context(nc.sbuf_tensor(f"pbb{k}", [128, 640], BF16)) for k in range(2)]
    pnb = [esD.enter_context(nc.sbuf_tensor(f"pnb{k}", [128, 640], BF16)) for k in range(2)]
    ptsb = [esD.enter_context(nc.sbuf_tensor(f"ptsb{k}", [128, 640], BF16)) for k in range(2)]
    mxc = esD.enter_context(nc.sbuf_tensor("mxc", [128, 2], F32))
    smc = esD.enter_context(nc.sbuf_tensor("smc", [128, 2], F32))
    rcc = esD.enter_context(nc.sbuf_tensor("rcc", [128, 2], F32))
    psc = [esD.enter_context(nc.psum_tensor(f"psc{k}", [128, 1024], F32)) for k in range(2)]
    ptt = [esD.enter_context(nc.psum_tensor(f"ptt{k}", [128, 1024], BF16)) for k in range(2)]
    pov = [esD.enter_context(nc.psum_tensor(f"pov{k}", [128, 512], F32)) for k in range(2)]

    NIT = NH * 8
    pma = {}
    _v, _sc, _mm = s_v.n, s_sc.n, s_mm.n
    for it in range(NIT + 1):
        if it < NIT:
            _mm += 1; pma[("sc", it)] = _mm
        if it >= 1:
            _mm += 1; pma[("tp", it - 1)] = _mm
            _mm += 1; pma[("av", it - 1)] = _mm
    for it in range(NIT + 1):
        if it < NIT:
            _v += 1; pma[("mask", it)] = _v
            _v += 1; pma[("max", it)] = _v
        if it >= 1:
            _v += 1; pma[("pnb", it - 1)] = _v
        if it < NIT:
            _v += 1; pma[("recip", it)] = _v
        if it >= 1:
            _v += 1; pma[("ptsb", it - 1)] = _v
    for it in range(NIT + 1):
        if it < NIT:
            _sc += 1; pma[("exp", it)] = _sc
        if it >= 1:
            _sc += 1; pma[("oc", it - 1)] = _sc

    with nc.Block() as b:
        @b.sync
        def _(e):
            e.wait_ge(s_v.h, vD4)
            e.wait_ge(s_sb[0].h, sbQ[0])
            e.wait_ge(s_sb[1].h, sbQ[1])
            e.wait_ge(s_go[0].h, s_go[0].n)
            e.wait_ge(s_go[1].h, s_go[1].n)
            e.dma_start(qts[:], q_scr[:, :]).then_inc(s_c.h, 16)
            e.dma_start(kxs[:], k_scr[:, :]).then_inc(s_c.h, 16)
            e.dma_start(vxs[:], v_scr[:, :]).then_inc(s_c.h, 16)
            e.dma_start(msks[:], msk_in).then_inc(s_c.h, 16)
            s_c.n += 64

        @b.tensor
        def _(e):
            e.wait_ge(s_c.h, s_c.n)
            e.wait_ge(s_v.h, vD4)
            for it in range(NIT + 1):
                if it < NIT:
                    h_, qb = divmod(it, 8)
                    p = it % 2
                    if it >= 2:
                        e.wait_ge(s_v.h, pma[("mask", it - 2)])
                    e.matmul(psc[p][:, 0:512], qts[:, h_ * TC + qb * 128:h_ * TC + (qb + 1) * 128],
                             kxs[:, h_ * TH + qb * 128:h_ * TH + qb * 128 + 512], start=True, stop=True)
                    mi = e.matmul(psc[p][:, 512:640], qts[:, h_ * TC + qb * 128:h_ * TC + (qb + 1) * 128],
                                  kxs[:, h_ * TH + qb * 128 + 512:h_ * TH + qb * 128 + 640],
                                  start=True, stop=True)
                    mi.then_inc(s_mm.h, 1)
                    s_mm.n += 1
                if it >= 1:
                    jt = it - 1
                    h_, qb = divmod(jt, 8)
                    p = jt % 2
                    e.wait_ge(s_v.h, pma[("pnb", jt)])
                    if jt >= 2:
                        e.wait_ge(s_v.h, pma[("ptsb", jt - 2)])
                    for kb in range(5):
                        mi = e.transpose(ptt[p][:, kb * 128:(kb + 1) * 128],
                                         pnb[p][:, kb * 128:(kb + 1) * 128], ident[:])
                    mi.then_inc(s_mm.h, 1)
                    s_mm.n += 1
                    e.wait_ge(s_v.h, pma[("ptsb", jt)])
                    if jt >= 2:
                        e.wait_ge(s_sc.h, pma[("oc", jt - 2)])
                    for kb in range(5):
                        mi = e.matmul(pov[p][:, 0:128],
                                      vxs[:, ((qb + kb) * NH + h_) * 128:((qb + kb) * NH + h_ + 1) * 128],
                                      ptsb[p][:, kb * 128:(kb + 1) * 128], start=(kb == 0), stop=(kb == 4))
                    mi.then_inc(s_mm.h, 1)
                    s_mm.n += 1

        @b.vector
        def _(e):
            for it in range(NIT + 1):
                if it < NIT:
                    h_, qb = divmod(it, 8)
                    p = it % 2
                    e.wait_ge(s_mm.h, pma[("sc", it)])
                    if it >= 2:
                        e.wait_ge(s_sc.h, pma[("exp", it - 2)])
                    e.tensor_tensor(ssb[p][:, 0:640], psc[p][:, 0:640], msks[:, qb * 640:(qb + 1) * 640],
                                    ALU.add).then_inc(s_v.h, 1)
                    e.wait_ge(s_v.h, pma[("mask", it)])
                    e.reduce_max(mxc[:, p:p + 1], ssb[p][:, 0:640], axis=AX.X, negate=True).then_inc(s_v.h, 1)
                    s_v.n += 2
                if it >= 1:
                    jt = it - 1
                    p = jt % 2
                    e.wait_ge(s_sc.h, pma[("exp", jt)])
                    e.wait_ge(s_v.h, pma[("recip", jt)])
                    if jt >= 2:
                        e.wait_ge(s_mm.h, pma[("tp", jt - 2)])
                    e.tensor_scalar_mul(pnb[p][:, 0:640], pbb[p][:, 0:640], rcc[:, p:p + 1]).then_inc(s_v.h, 1)
                    s_v.n += 1
                if it < NIT:
                    p = it % 2
                    e.wait_ge(s_sc.h, pma[("exp", it)])
                    e.reciprocal(rcc[:, p:p + 1], smc[:, p:p + 1]).then_inc(s_v.h, 1)
                    s_v.n += 1
                if it >= 1:
                    jt = it - 1
                    p = jt % 2
                    e.wait_ge(s_mm.h, pma[("tp", jt)])
                    if jt >= 2:
                        e.wait_ge(s_mm.h, pma[("av", jt - 2)])
                    e.tensor_copy(ptsb[p][:, 0:640], ptt[p][:, 0:640]).then_inc(s_v.h, 1)
                    s_v.n += 1

        @b.scalar
        def _(e):
            for it in range(NIT + 1):
                if it < NIT:
                    p = it % 2
                    e.wait_ge(s_v.h, pma[("max", it)])
                    if it >= 2:
                        e.wait_ge(s_v.h, pma[("pnb", it - 2)])
                        e.wait_ge(s_v.h, pma[("recip", it - 2)])
                    e.activation(pbb[p][:, 0:640], ssb[p][:, 0:640], AF.Exp, bias=mxc[:, p:p + 1],
                                 accum_out=smc[:, p:p + 1]).then_inc(s_sc.h, 1)
                    s_sc.n += 1
                if it >= 1:
                    jt = it - 1
                    h_, qb = divmod(jt, 8)
                    p = jt % 2
                    e.wait_ge(s_mm.h, pma[("av", jt)])
                    e.activation(o_sb[:, h_ * TC + qb * 128:h_ * TC + (qb + 1) * 128],
                                 pov[p][:, 0:128], AF.Copy).then_inc(s_sc.h, 1)
                    s_sc.n += 1

    assert s_v.n == _v and s_sc.n == _sc and s_mm.n == _mm, (s_v.n, _v, s_sc.n, _sc, s_mm.n, _mm)
    esD.close()
    mmA4 = s_mm.n
    vA4 = s_v.n
    scA4 = s_sc.n

    go_base4c = [s_go[0].n, s_go[1].n]
    esE = ExitStack()
    stgo = [esE.enter_context(nc.sbuf_tensor(f"stgo{k}", [128, TC], F32)) for k in range(2)]
    woh1 = esE.enter_context(nc.sbuf_tensor("woh1", [128, NDC * 1024], BF16))
    xto = [esE.enter_context(nc.sbuf_tensor(f"xto{k}", [128, 1024], F32)) for k in range(2)]
    fto = [esE.enter_context(nc.sbuf_tensor(f"fto{k}", [128, 1024], F32)) for k in range(2)]
    ppo = [esE.enter_context(nc.psum_tensor(f"ppo{k}", [128, 512], F32)) for k in range(4)]
    with nc.Block() as b:
        @b.sync
        def _(e):
            e.wait_ge(s_v.h, vA4)
            e.wait_ge(s_sc.h, scA4)
            e.wait_ge(s_go[0].h, s_go[0].n)
            e.wait_ge(s_go[1].h, s_go[1].n)
            for dh_ in range(2):
                for t in range(NT):
                    it_ = dh_ * NT + t
                    p = it_ % 2
                    if it_ >= 2:
                        e.wait_ge(s_v.h, vA4 + 2 * (it_ - 1))
                    e.dma_start(xto[p][:], x1_scr[HOFF + t * 128:HOFF + (t + 1) * 128,
                                                  dh_ * 1024:(dh_ + 1) * 1024]).then_inc(s_x[p].h, 16)
                    e.dma_start(fto[p][:], ffn1_scr[t * 128:(t + 1) * 128,
                                                    dh_ * 1024:(dh_ + 1) * 1024]).then_inc(s_x[p].h, 16)
                    s_x[p].n += 32
                    x_marks[("fin", it_)] = s_x[p].n

        @b.gpsimd
        def _(e):
            e.wait_ge(s_v.h, vA4)
            e.wait_ge(s_sc.h, scA4)
            for dh_ in range(2):
                if dh_ == 1:
                    e.wait_ge(s_mm.h, mmA4 + NT)
                e.dma_start(woh1[:], wo1_t[dh_]).then_inc(s_wb.h, 16)
                s_wb.n += 16
                wb_marks[("wo1", dh_)] = s_wb.n
                for t in range(NT):
                    it_ = dh_ * NT + t
                    p = it_ % 2
                    e.wait_ge(s_v.h, vA4 + 2 * (it_ + 1))
                    e.dma_start(out_f[t * 128:(t + 1) * 128, dh_ * 1024:(dh_ + 1) * 1024],
                                stgo[p][:, 0:1024]).then_inc(s_go[p].h, 16)
                    s_go[p].n += 16
            e.wait_ge(s_go[0].h, s_go[0].n)
            e.wait_ge(s_go[1].h, s_go[1].n)

        @b.tensor
        def _(e):
            e.wait_ge(s_v.h, vA4)
            e.wait_ge(s_sc.h, scA4)
            for dh_ in range(2):
                e.wait_ge(s_wb.h, wb_marks[("wo1", dh_)])
                for t in range(NT):
                    it_ = dh_ * NT + t
                    pa, pb = (ppo[0], ppo[1]) if it_ % 2 == 0 else (ppo[2], ppo[3])
                    if it_ >= 2:
                        e.wait_ge(s_v.h, vA4 + 2 * (it_ - 1))
                    for i in range(NDC):
                        st_, sp_ = (i == 0), (i == NDC - 1)
                        e.matmul(pa[:], o_sb[:, i * TC + t * 128:i * TC + (t + 1) * 128],
                                 woh1[:, i * 1024:i * 1024 + 512], start=st_, stop=sp_)
                        mi = e.matmul(pb[:], o_sb[:, i * TC + t * 128:i * TC + (t + 1) * 128],
                                      woh1[:, i * 1024 + 512:(i + 1) * 1024], start=st_, stop=sp_)
                        if sp_:
                            mi.then_inc(s_mm.h, 1)
                    s_mm.n += 1

        @b.vector
        def _(e):
            for dh_ in range(2):
                for t in range(NT):
                    it_ = dh_ * NT + t
                    p = it_ % 2
                    pa, pb = (ppo[0], ppo[1]) if p == 0 else (ppo[2], ppo[3])
                    e.wait_ge(s_x[p].h, x_marks[("fin", it_)])
                    e.wait_ge(s_mm.h, mmA4 + it_ + 1)
                    if it_ >= 2:
                        e.wait_ge(s_go[p].h, go_base4c[p] + 16 * (it_ // 2))
                    e.tensor_tensor(stgo[p][:, 0:512], xto[p][:, 0:512], pa[:], ALU.add)
                    e.tensor_tensor(stgo[p][:, 512:1024], xto[p][:, 512:1024], pb[:], ALU.add).then_inc(s_v.h, 1)
                    e.wait_ge(s_v.h, vA4 + 2 * it_ + 1)
                    e.tensor_tensor(stgo[p][:, 0:1024], stgo[p][:, 0:1024], fto[p][:], ALU.add).then_inc(s_v.h, 1)
                    s_v.n += 2

    esE.close()
    es.close()
    return nc


def host_inputs_fused(inputs):
    shared = dict(
        w_in_t=tile_wT(np.asarray(inputs["L0_attn_W_in"]).T, NJ_IN).astype(BF),
        w_dtT=f32(np.asarray(inputs["L0_attn_W_dt"]).T).astype(BF),
        bdt_pc=col_major_pc(inputs["L0_attn_b_dt"], NJ_DT),
        gw_t=tile_wT(np.asarray(inputs["L0_ffn_gate_w"]).T, NJ_F).astype(BF),
        uw_t=tile_wT(np.asarray(inputs["L0_ffn_up_w"]).T, NJ_F).astype(BF),
        dw_t=np.ascontiguousarray(
            f32(np.asarray(inputs["L0_ffn_down_w"]).T).reshape(NJ_F, 128, 2, 1024)
            .transpose(2, 1, 0, 3).reshape(2, 128, NJ_F * 1024)).astype(BF),
        gs_pc=col_major_pc(inputs["L0_ffn_gate_s"], NJ_F),
        us_pc=col_major_pc(inputs["L0_ffn_up_s"], NJ_F),
        ds_row=f32(np.asarray(inputs["L0_ffn_down_s"]).reshape(1, D)),
        ident=np.eye(128, dtype=BF),
        alog_pc=np.ascontiguousarray(
            np.asarray(inputs["L0_attn_A_log"], np.float32).reshape(NIY, 128, DST)
            .transpose(1, 0, 2).reshape(128, NIY * DST)),
        cw_pc=np.ascontiguousarray(
            np.asarray(inputs["L0_attn_conv_w"], np.float32).reshape(DI, DCONV)
            .reshape(NIY, 128, DCONV).transpose(1, 0, 2).reshape(128, NIY * DCONV)),
        cb_pc=col_major_pc(inputs["L0_attn_conv_b"], NIY),
        dp_pc=col_major_pc(inputs["L0_attn_D_param"], NIY),
        wo_t=np.ascontiguousarray(
            f32(np.asarray(inputs["L0_attn_W_out"]).T).reshape(NIY, 128, 2, 1024)
            .transpose(2, 1, 0, 3).reshape(2, 128, NIY * 1024)).astype(BF),
        wqk_t=tile_wT(f32(np.asarray(inputs["L1_attn_w_qkv"])[0:2 * D].T), 2 * NDC).astype(BF),
        wv_t=np.ascontiguousarray(
            f32(np.asarray(inputs["L1_attn_w_qkv"])[2 * D:3 * D].T)
            .reshape(NDC, 128, D).transpose(1, 0, 2).reshape(128, NDC * D)).astype(BF),
        g1_t=tile_wT_i32(np.asarray(inputs["L1_ffn_gate_w"]).T, NJ1).astype(BF),
        u1_t=tile_wT_i32(np.asarray(inputs["L1_ffn_up_w"]).T, NJ1).astype(BF),
        d1_t=np.ascontiguousarray(
            np.asarray(inputs["L1_ffn_down_w"], np.int32).T
            .reshape(NJ1, 128, 8, 256).transpose(2, 1, 0, 3).reshape(8, 128, NJ1 * 256)).astype(BF),
        gs1_pc=col_major_pc(inputs["L1_ffn_gate_scale"], NJ1),
        us1_pc=col_major_pc(inputs["L1_ffn_up_scale"], NJ1),
        ds1_row=f32(np.asarray(inputs["L1_ffn_down_scale"]).reshape(1, D)),
        wo1_t=np.ascontiguousarray(
            f32(np.asarray(inputs["L1_attn_w_out"]).T).reshape(NDC, 128, 2, 1024)
            .transpose(2, 1, 0, 3).reshape(2, 128, NDC * 1024)).astype(BF),
    )
    eexp = np.zeros((16, 128), dtype=BF)
    for p_ in range(128):
        eexp[p_ % 16, p_] = 1
    ebc = np.zeros((DST, 16, 128), dtype=BF)
    for n in range(DST):
        ebc[n, n, :] = 1
    scs = np.zeros((DST, 128, 128), dtype=BF)
    for n in range(DST):
        for p_ in range(128):
            scs[n, 16 * (p_ // 16) + n, p_] = 1
    shared["eexp"] = eexp
    shared["ebc"] = ebc
    shared["scs"] = scs

    x = np.asarray(inputs["L0_attn_hidden_states"], np.float32)
    jj = np.arange(640)
    pp_ = np.arange(128)
    per_core = []
    for c in range(NCORE):
        bq, hh = c // 2, c % 2
        m = dict(shared)
        if hh == 0:
            xw = np.concatenate([np.zeros((1024, D), np.float32), x[bq, 0:1024]], axis=0)
        else:
            xw = np.ascontiguousarray(x[bq])
        m["x"] = xw
        msk = np.zeros((8, 128, 640), np.float32)
        for qb in range(8):
            kpos = hh * 1024 + qb * 128 - 512 + jj
            band = (jj[None, :] >= pp_[:, None]) & (jj[None, :] <= pp_[:, None] + 512)
            ok = band & (kpos >= 0)[None, :]
            msk[qb] = np.where(ok, 0.0, -1e30).astype(np.float32)
        m["msk"] = np.ascontiguousarray(msk.transpose(1, 0, 2).reshape(128, 8 * 640)).astype(BF)
        per_core.append(m)
    return per_core


def kernel_fused(**inputs):
    nc = build_fused()
    res = run_bass_kernel_spmd(nc, host_inputs_fused(inputs), core_ids=list(range(NCORE))).results
    out = np.concatenate([res[c]["out"] for c in range(NCORE)], axis=0)
    return np.ascontiguousarray(out.reshape(NB, L, D), dtype=np.float32)
